# version 24
# baseline (speedup 1.0000x reference)
import numpy as np
import ml_dtypes
from contextlib import ExitStack

import concourse.bass as bass
import concourse.mybir as mybir
from concourse.bass_utils import run_bass_kernel_spmd

F32 = mybir.dt.float32
BF16 = mybir.dt.bfloat16
AF = mybir.ActivationFunctionType
ALU = mybir.AluOpType
AX = mybir.AxisListType
bfloat16 = ml_dtypes.bfloat16

DEBUG = False
PHASES = "Q123"
N_CORES = 8
BC = 2
T = 2048
D = 2048
NT = 16
NQB = 4
LN_EPS = 1e-5
RMS_EPS = 1e-6
FFN_EPS = float(np.finfo(np.float32).eps)
LOG_INV_SQRT_H = float(np.log(1.0 / np.sqrt(128.0)))
SOFTMAX_SHIFT = -12.0


class Sem:
    def __init__(self, st, nc, name):
        self.h = st.enter_context(nc.semaphore(name))
        self.n = 0

    def inc(self, k=1):
        self.n += k
        return self.n


class Plan:
    def __init__(self):
        self.ops = {k: [] for k in "tvsgy"}

    def add(self, eng, builder, wait=(), inc=None, inc_n=1):
        if inc is not None:
            inc.inc(inc_n)
        wl = [(s.h, v) for (s, v) in wait if v > 0]
        ih = inc.h if inc is not None else None

        def run(e, wl=wl, ih=ih, inc_n=inc_n, builder=builder):
            for h, v in wl:
                e.wait_ge(h, v)
            ins = builder(e)
            if ih is not None:
                ins.then_inc(ih, inc_n)
        self.ops[eng].append(run)

    def raw(self, eng, fn):
        self.ops[eng].append(fn)


def build_nc():
    nc = bass.Bass("TRN2", target_bir_lowering=False)
    kind_scratch = "ExternalOutput" if DEBUG else "Internal"

    x_d = nc.dram_tensor("x", [BC * T, D], F32, kind="ExternalInput").ap()
    wq_d = nc.dram_tensor("wq", [D, 2048], F32, kind="ExternalInput").ap()
    wk_d = nc.dram_tensor("wk", [D, 512], F32, kind="ExternalInput").ap()
    wv_d = nc.dram_tensor("wv", [D, 512], F32, kind="ExternalInput").ap()
    wo_d = nc.dram_tensor("wo", [2048, D], F32, kind="ExternalInput").ap()
    wgu_d = nc.dram_tensor("wgu", [16384, D], F32, kind="ExternalInput").ap()
    wp_d = nc.dram_tensor("wp", [D, 8192], F32, kind="ExternalInput").ap()
    sc_d = nc.dram_tensor("sincos", [T, 128], F32, kind="ExternalInput").ap()
    eye_d = nc.dram_tensor("eye", [128, 128], BF16, kind="ExternalInput").ap()
    y_d = nc.dram_tensor("y", [BC * T, D], F32, kind="ExternalOutput").ap()

    wguT_d = nc.dram_tensor("wguT_s", [128 * 128, 2048], BF16, kind=kind_scratch).ap()
    wpT_d = nc.dram_tensor("wpT_s", [32 * 128, 4096], BF16, kind=kind_scratch).ap()
    xnT_d = nc.dram_tensor("xnT_s", [BC * NQB * 128, 8192], BF16, kind=kind_scratch).ap()
    qT_d = nc.dram_tensor("qT_s", [BC * NQB * 128, 8192], BF16, kind=kind_scratch).ap()
    kT_d = nc.dram_tensor("kT_s", [BC * 128, 8192], BF16, kind=kind_scratch).ap()
    v_d = nc.dram_tensor("v_s", [BC * 128, 8192], BF16, kind=kind_scratch).ap()
    attn_d = nc.dram_tensor("attn_s", [BC * NQB * 128, 8192], F32, kind=kind_scratch).ap()

    P = Plan()
    st_top = ExitStack()
    with st_top:
        sT = Sem(st_top, nc, "sT")
        sV = Sem(st_top, nc, "sV")
        sS = Sem(st_top, nc, "sS")
        sG = Sem(st_top, nc, "sG")
        sB = Sem(st_top, nc, "sB")
        dEye = Sem(st_top, nc, "dEye")
        dGW = [Sem(st_top, nc, f"dGW{i}") for i in range(6)]
        dGWT = Sem(st_top, nc, "dGWT")
        dQE = [Sem(st_top, nc, f"dQE{i}") for i in range(6)]
        dASM = Sem(st_top, nc, "dASM")
        dSC = Sem(st_top, nc, "dSC")
        dX = [Sem(st_top, nc, f"dX{i}") for i in range(2)]
        dXNT_S = Sem(st_top, nc, "dXNT_S")
        dQT_S = Sem(st_top, nc, "dQT_S")
        dKST = [Sem(st_top, nc, f"dKST{i}") for i in range(2)]
        dVST = [Sem(st_top, nc, f"dVST{i}") for i in range(2)]
        dKV = Sem(st_top, nc, "dKV")
        dQTL = [Sem(st_top, nc, f"dQTL{i}") for i in range(2)]
        dX2 = Sem(st_top, nc, "dX2")
        dAT_S = Sem(st_top, nc, "dAT_S")
        dATS2 = [Sem(st_top, nc, f"dATS2{i}") for i in range(2)]
        dXNT = [Sem(st_top, nc, f"dXNT{i}") for i in range(2)]
        dWGU = [Sem(st_top, nc, f"dWGU{i}") for i in range(3)]
        dWP = [Sem(st_top, nc, f"dWP{i}") for i in range(2)]
        dATL = [Sem(st_top, nc, f"dATL{i}") for i in range(2)]
        dOUT = Sem(st_top, nc, "dOUT")

        PS = st_top.enter_context(nc.psum_tensor("PS", [128, 4096], F32))
        eye = st_top.enter_context(nc.sbuf_tensor("eye_sb", [128, 128], BF16))
        ones_col = st_top.enter_context(nc.sbuf_tensor("ones_col", [128, 1], BF16))
        ones_row = st_top.enter_context(nc.sbuf_tensor("ones_row", [1, 128], BF16))
        ones128 = st_top.enter_context(nc.sbuf_tensor("ones128", [128, 128], BF16))

        cst = st_top.enter_context(nc.sbuf_tensor("cst", [128, 8], F32))
        P.add('y', lambda e: e.dma_start(eye[:, :], eye_d), inc=dEye, inc_n=16)
        eye_done = dEye.n
        P.add('v', lambda e: e.memset(ones_col[:, :], 1.0), inc=sV)
        P.add('v', lambda e: e.memset(ones_row[:, :], 1.0), inc=sV)
        P.add('v', lambda e: e.memset(cst[:, 0:1], LN_EPS), inc=sV)
        P.add('v', lambda e: e.memset(cst[:, 1:2], RMS_EPS), inc=sV)
        P.add('v', lambda e: e.memset(cst[:, 2:3], LOG_INV_SQRT_H), inc=sV)
        P.add('v', lambda e: e.memset(cst[:, 3:4], SOFTMAX_SHIFT), inc=sV)
        P.add('v', lambda e: e.memset(cst[:, 4:5], FFN_EPS), inc=sV)
        P.add('v', lambda e: e.memset(ones128[:, :], 1.0), inc=sV)
        ones_done = sV.n
        P.raw('s', lambda e, sh=sV.h, v=ones_done: e.wait_ge(sh, v))

        ALL_DMA_SEMS = ([dEye, dGWT, dASM, dSC, dXNT_S, dQT_S, dKV, dX2, dAT_S, dOUT]
                        + dATS2
                        + dGW + dQE + dX + dKST + dVST + dQTL + dXNT + dWGU + dWP + dATL)

        def barrier():
            for d in ALL_DMA_SEMS:
                if d.n > 0:
                    P.raw('y', lambda e, h=d.h, v=d.n: e.wait_ge(h, v))
            tgt = sB.n + 5
            for eng in "tvsgy":
                P.raw(eng, lambda e, s=sB.h: e.sem_inc(s, 1))
            sB.n = tgt
            for eng in "tvsgy":
                P.raw(eng, lambda e, s=sB.h, t=tgt: e.wait_ge(s, t))

        with ExitStack() as st:
          if '1' in PHASES:
            wq_sb = st.enter_context(nc.sbuf_tensor("wq_sb", [128, 16 * 2048], BF16))
            wk_sb = st.enter_context(nc.sbuf_tensor("wk_sb", [128, 16 * 512], BF16))
            wv_sb = st.enter_context(nc.sbuf_tensor("wv_sb", [128, 16 * 512], BF16))
            sc_sb = st.enter_context(nc.sbuf_tensor("sc_sb", [128, 2048], F32))
            x_b = [st.enter_context(nc.sbuf_tensor(f"x{p}", [128, 2048], F32)) for p in range(2)]
            xn_b = [st.enter_context(nc.sbuf_tensor(f"xn{p}", [128, 2048], BF16)) for p in range(2)]
            qf_b = [st.enter_context(nc.sbuf_tensor(f"qf{p}", [128, 2048], F32)) for p in range(2)]
            qr_b = [st.enter_context(nc.sbuf_tensor(f"p1qr{p}", [128, 2048], BF16)) for p in range(2)]
            kf_b = [st.enter_context(nc.sbuf_tensor(f"kf{p}", [128, 512], F32)) for p in range(2)]
            kr_b = [st.enter_context(nc.sbuf_tensor(f"kr{p}", [128, 512], BF16)) for p in range(2)]
            ra = st.enter_context(nc.sbuf_tensor("ra", [128, 1024], F32))
            rb = st.enter_context(nc.sbuf_tensor("rb", [128, 1024], F32))
            junk = st.enter_context(nc.sbuf_tensor("junk", [128, 2048], BF16))
            sm_b = [st.enter_context(nc.sbuf_tensor(f"sm{p}", [128, 8], F32)) for p in range(2)]
            xnT_qb = st.enter_context(nc.sbuf_tensor("xnT_qb", [128, 8192], BF16))
            qT_qb = st.enter_context(nc.sbuf_tensor("qT_qb", [128, 8192], BF16))
            kst_b = [st.enter_context(nc.sbuf_tensor(f"kst{p}", [128, 512], BF16)) for p in range(2)]
            vst_b = [st.enter_context(nc.sbuf_tensor(f"vst{p}", [128, 512], BF16)) for p in range(2)]

            P.add('g', lambda e: e.dma_start(
                wq_sb[:, :].rearrange("p (c n) -> p c n", n=2048),
                wq_d.rearrange("(c p) n -> p c n", p=128)), inc=dGWT, inc_n=16)
            P.add('g', lambda e: e.dma_start(
                wk_sb[:, :].rearrange("p (c n) -> p c n", n=512),
                wk_d.rearrange("(c p) n -> p c n", p=128)), inc=dGWT, inc_n=16)
            P.add('g', lambda e: e.dma_start(
                wv_sb[:, :].rearrange("p (c n) -> p c n", n=512),
                wv_d.rearrange("(c p) n -> p c n", p=128)), inc=dGWT, inc_n=16)
            w_ld = dGWT.n
            for t in range(NT):
                P.add('y', lambda e, t=t: e.dma_start(
                    sc_sb[:, t * 128:(t + 1) * 128], sc_d[t * 128:(t + 1) * 128, :]),
                    inc=dSC, inc_n=16)
            sc_ld = dSC.n

            haz_x = [0, 0]
            haz_xsq = [0, 0]
            haz_xn = [0, 0]
            haz_qf = [0, 0]
            haz_kf = [0, 0]
            haz_sm = [0, 0]
            haz_qr = [0, 0]
            haz_kr = [0, 0]
            haz_xnT_store = 0
            haz_qT_store = 0
            haz_kst = [0, 0]
            haz_vst = [0, 0]
            ps03_free = 0
            ps_q_free = [0, 0]
            ps_k_free = 0
            ps_v_free = 0
            CC = {}

            NTILES = BC * NT

            def stage_a_load(i):
                p = i % 2
                b, t = i // NT, i % NT
                row0 = b * T + t * 128
                P.add('y', lambda e, x_=x_b[p], row0=row0: e.dma_start(
                    x_[:, :], x_d[row0:row0 + 128, :]),
                    wait=[(sV, haz_x[p]), (sS, haz_xsq[p])], inc=dX[p], inc_n=16)
                CC[('xld', i)] = dX[p].n

            def stage_a_ln(i):
                p = i % 2
                x_, xn, sm = x_b[p], xn_b[p], sm_b[p]
                x_ld = CC[('xld', i)]
                P.add('v', lambda e, sm=sm, x_=x_: e.reduce_sum(
                    sm[:, 0:1], x_[:, :], axis=AX.X),
                    wait=[(dX[p], x_ld)], inc=sV)
                v_sum = sV.n
                P.add('v', lambda e, sm=sm: e.tensor_single_scalar(
                    sm[:, 1:2], sm[:, 0:1], -1.0 / 2048.0, op=ALU.mult),
                    wait=[(sV, v_sum)], inc=sV)
                v_mean = sV.n
                P.add('s', lambda e, x_=x_, sm=sm: e.activation(
                    junk[:, :], x_[:, :], AF.Square, bias=sm[:, 1:2],
                    accum_out=sm[:, 2:3]),
                    wait=[(sV, v_mean)], inc=sS)
                s_ssq = sS.n
                haz_xsq[p] = s_ssq
                P.add('s', lambda e, sm=sm: e.activation(
                    sm[:, 3:4], sm[:, 2:3], AF.Ln, scale=1.0 / 2048.0, bias=cst[:, 0:1]),
                    wait=[(sS, s_ssq)], inc=sS)
                s_lnv = sS.n
                P.add('s', lambda e, sm=sm: e.activation(
                    sm[:, 4:5], sm[:, 3:4], AF.Exp, scale=-0.5),
                    wait=[(sS, s_lnv)], inc=sS)
                CC[('rstd', i)] = sS.n

            def stage_a_xn(i):
                p = i % 2
                x_, xn, sm = x_b[p], xn_b[p], sm_b[p]
                P.add('v', lambda e, xn=xn, x_=x_, sm=sm: e.tensor_scalar(
                    xn[:, :], x_[:, :], sm[:, 1:2], sm[:, 4:5],
                    op0=ALU.add, op1=ALU.mult),
                    wait=[(sS, CC[('rstd', i)]), (sT, haz_xn[p])], inc=sV)
                v_xn = sV.n
                haz_x[p] = v_xn
                CC[('vxn', i)] = v_xn

            def stage_a_tr(i):
                nonlocal ps03_free, haz_xnT_store
                p = i % 2
                b, t = i // NT, i % NT
                qb, t4 = t // 4, t % 4
                xn = xn_b[p]
                for c in range(16):
                    P.add('t', lambda e, xn=xn, c=c: e.matmul(
                        PS[:, c * 128:(c + 1) * 128], xn[:, c * 128:(c + 1) * 128],
                        eye[:, :], start=True, stop=True),
                        wait=([(sV, CC[('vxn', i)]), (sS, ps03_free), (dEye, eye_done)]
                              if c == 0 else []),
                        inc=(sT if c == 15 else None))
                t_xntr = sT.n
                haz_xn[p] = t_xntr
                P.add('s', lambda e, t4=t4: e.copy(
                    xnT_qb[:, :].rearrange("p (c n) -> p c n", n=512)[:, :, t4 * 128:(t4 + 1) * 128],
                    PS[:, 0:2048].rearrange("p (c n) -> p c n", n=128)),
                    wait=[(sT, t_xntr), (dXNT_S, haz_xnT_store)], inc=sS)
                s_xnT = sS.n
                ps03_free = s_xnT
                CC[('sxnT', i)] = s_xnT
                if t4 == 3:
                    rows = (b * NQB + qb) * 128
                    P.add('y', lambda e, rows=rows: e.dma_start(
                        xnT_d[rows:rows + 128, :], xnT_qb[:, :]),
                        wait=[(sS, s_xnT)], inc=dXNT_S, inc_n=16)
                    haz_xnT_store = dXNT_S.n

            def stage_b_q(i):
                p = i % 2
                b, t = i // NT, i % NT
                t4 = t % 4
                qf = qf_b[p]
                s_qev = 0
                for n in range(4):
                    bank = 3072 + (n % 2) * 512
                    for c in range(16):
                        P.add('t', lambda e, n=n, c=c, t4=t4, bank=bank: e.matmul(
                            PS[:, bank:bank + 512],
                            xnT_qb[:, c * 512 + t4 * 128: c * 512 + (t4 + 1) * 128],
                            wq_sb[:, c * 2048 + n * 512: c * 2048 + (n + 1) * 512],
                            start=(c == 0), stop=(c == 15)),
                            wait=([(sS, CC[('sxnT', i)]), (sS, ps_q_free[n % 2]), (dGWT, w_ld)]
                                  if c == 0 else []),
                            inc=(sT if c == 15 else None))
                    t_q = sT.n
                    P.add('s', lambda e, qf=qf, n=n, bank=bank: e.copy(
                        qf[:, n * 512:(n + 1) * 512], PS[:, bank:bank + 512]),
                        wait=[(sT, t_q)] + ([(sV, haz_qf[p])] if n == 0 else []), inc=sS)
                    s_qev = sS.n
                    ps_q_free[n % 2] = s_qev
                CC[('sqev', i)] = s_qev

            def stage_b_kv(i):
                nonlocal ps_k_free, ps_v_free
                p = i % 2
                b, t = i // NT, i % NT
                t4 = t % 4
                qf, qr, kf, kr, sm = qf_b[p], qr_b[p], kf_b[p], kr_b[p], sm_b[p]
                for c in range(16):
                    P.add('t', lambda e, c=c, t4=t4: e.matmul(
                        PS[:, 2048:2560],
                        xnT_qb[:, c * 512 + t4 * 128: c * 512 + (t4 + 1) * 128],
                        wk_sb[:, c * 512: (c + 1) * 512],
                        start=(c == 0), stop=(c == 15)),
                        wait=([(sS, ps_k_free)] if c == 0 else []),
                        inc=(sT if c == 15 else None))
                t_k = sT.n
                for c in range(16):
                    P.add('t', lambda e, c=c, t4=t4: e.matmul(
                        PS[:, 2560:3072],
                        xnT_qb[:, c * 512 + t4 * 128: c * 512 + (t4 + 1) * 128],
                        wv_sb[:, c * 512: (c + 1) * 512],
                        start=(c == 0), stop=(c == 15)),
                        wait=([(sS, ps_v_free)] if c == 0 else []),
                        inc=(sT if c == 15 else None))
                t_v = sT.n
                P.add('s', lambda e, qf=qf, sm=sm: e.activation(
                    junk[:, :], qf[:, :], AF.Square, accum_out=sm[:, 5:6]),
                    wait=[(sS, CC[('sqev', i)]), (sV, haz_sm[p])], inc=sS)
                s1 = sS.n
                P.add('s', lambda e, sm=sm: e.activation(
                    sm[:, 6:7], sm[:, 5:6], AF.Ln, scale=1.0 / 2048.0, bias=cst[:, 1:2]),
                    wait=[(sS, s1)], inc=sS)
                s2 = sS.n
                P.add('s', lambda e, sm=sm: e.activation(
                    sm[:, 5:6], sm[:, 6:7], AF.Exp, scale=-0.5, bias=cst[:, 2:3]),
                    wait=[(sS, s2)], inc=sS)
                s_qrs = sS.n
                P.add('s', lambda e, kf=kf: e.copy(kf[:, :], PS[:, 2048:2560]),
                      wait=[(sT, t_k), (sV, haz_kf[p])], inc=sS)
                s_kev = sS.n
                ps_k_free = s_kev
                P.add('s', lambda e, kf=kf, sm=sm: e.activation(
                    junk[:, 0:512], kf[:, :], AF.Square, accum_out=sm[:, 6:7]),
                    wait=[(sS, s_kev)], inc=sS)
                s1 = sS.n
                P.add('s', lambda e, sm=sm: e.activation(
                    sm[:, 7:8], sm[:, 6:7], AF.Ln, scale=1.0 / 512.0, bias=cst[:, 1:2]),
                    wait=[(sS, s1)], inc=sS)
                s2 = sS.n
                P.add('s', lambda e, sm=sm: e.activation(
                    sm[:, 6:7], sm[:, 7:8], AF.Exp, scale=-0.5),
                    wait=[(sS, s2)], inc=sS)
                s_krs = sS.n
                P.add('s', lambda e, vst=vst_b[p]: e.copy(vst[:, :], PS[:, 2560:3072]),
                      wait=[(sT, t_v), (dVST[p], haz_vst[p])], inc=sS)
                s_vev = sS.n
                ps_v_free = s_vev
                P.add('y', lambda e, vst=vst_b[p], b=b, t=t: e.dma_start(
                    v_d[b * 128:(b + 1) * 128, t * 512:(t + 1) * 512], vst[:, :]),
                    wait=[(sS, s_vev)], inc=dVST[p], inc_n=16)
                haz_vst[p] = dVST[p].n
                sin16 = sc_sb[:, t * 128: t * 128 + 64].rearrange(
                    "p (o k) -> p o k", o=1).broadcast_to([128, 16, 64])
                cos16 = sc_sb[:, t * 128 + 64: t * 128 + 128].rearrange(
                    "p (o k) -> p o k", o=1).broadcast_to([128, 16, 64])
                qv = qf[:, :].rearrange("p (h x) -> p h x", x=128)
                qF, qS = qv[:, :, 0:64], qv[:, :, 64:128]
                rav = ra[:, :].rearrange("p (h k) -> p h k", k=64)
                rbv = rb[:, :].rearrange("p (h k) -> p h k", k=64)
                qrv = qr[:, :].rearrange("p (h x) -> p h x", x=128)
                P.add('v', lambda e, rav=rav, qF=qF, sm=sm, cos16=cos16: e.scalar_tensor_tensor(
                    rav, qF, sm[:, 5:6], cos16, op0=ALU.mult, op1=ALU.mult),
                    wait=[(sS, s_qrs), (dSC, sc_ld)], inc=sV)
                v1 = sV.n
                P.add('v', lambda e, rbv=rbv, qS=qS, sm=sm, sin16=sin16: e.scalar_tensor_tensor(
                    rbv, qS, sm[:, 5:6], sin16, op0=ALU.mult, op1=ALU.mult),
                    wait=[(sS, s_qrs)], inc=sV)
                v2 = sV.n
                P.add('v', lambda e, qrv=qrv, rav=rav, rbv=rbv: e.tensor_tensor(
                    qrv[:, :, 0:64], rav, rbv, op=ALU.subtract),
                    wait=[(sV, v1), (sV, v2), (sT, haz_qr[p])], inc=sV)
                v3 = sV.n
                P.add('v', lambda e, rav=rav, qS=qS, sm=sm, cos16=cos16: e.scalar_tensor_tensor(
                    rav, qS, sm[:, 5:6], cos16, op0=ALU.mult, op1=ALU.mult),
                    wait=[(sV, v3)], inc=sV)
                v4 = sV.n
                P.add('v', lambda e, rbv=rbv, qF=qF, sm=sm, sin16=sin16: e.scalar_tensor_tensor(
                    rbv, qF, sm[:, 5:6], sin16, op0=ALU.mult, op1=ALU.mult),
                    wait=[(sV, v3)], inc=sV)
                v5 = sV.n
                P.add('v', lambda e, qrv=qrv, rav=rav, rbv=rbv: e.tensor_tensor(
                    qrv[:, :, 64:128], rav, rbv, op=ALU.add),
                    wait=[(sV, v4), (sV, v5)], inc=sV)
                v_qrope = sV.n
                haz_qf[p] = v_qrope
                sin4 = sc_sb[:, t * 128: t * 128 + 64].rearrange(
                    "p (o k) -> p o k", o=1).broadcast_to([128, 4, 64])
                cos4 = sc_sb[:, t * 128 + 64: t * 128 + 128].rearrange(
                    "p (o k) -> p o k", o=1).broadcast_to([128, 4, 64])
                kvv = kf[:, :].rearrange("p (h x) -> p h x", x=128)
                kF, kS = kvv[:, :, 0:64], kvv[:, :, 64:128]
                rav4 = ra[:, 0:256].rearrange("p (h k) -> p h k", k=64)
                rbv4 = rb[:, 0:256].rearrange("p (h k) -> p h k", k=64)
                krv = kr[:, :].rearrange("p (h x) -> p h x", x=128)
                P.add('v', lambda e, rav4=rav4, kF=kF, sm=sm, cos4=cos4: e.scalar_tensor_tensor(
                    rav4, kF, sm[:, 6:7], cos4, op0=ALU.mult, op1=ALU.mult),
                    wait=[(sS, s_krs), (sV, v_qrope)], inc=sV)
                v1 = sV.n
                P.add('v', lambda e, rbv4=rbv4, kS=kS, sm=sm, sin4=sin4: e.scalar_tensor_tensor(
                    rbv4, kS, sm[:, 6:7], sin4, op0=ALU.mult, op1=ALU.mult),
                    wait=[(sS, s_krs)], inc=sV)
                v2 = sV.n
                P.add('v', lambda e, krv=krv, rav4=rav4, rbv4=rbv4: e.tensor_tensor(
                    krv[:, :, 0:64], rav4, rbv4, op=ALU.subtract),
                    wait=[(sV, v1), (sV, v2), (sT, haz_kr[p])], inc=sV)
                v3 = sV.n
                P.add('v', lambda e, rav4=rav4, kS=kS, sm=sm, cos4=cos4: e.scalar_tensor_tensor(
                    rav4, kS, sm[:, 6:7], cos4, op0=ALU.mult, op1=ALU.mult),
                    wait=[(sV, v3)], inc=sV)
                v4 = sV.n
                P.add('v', lambda e, rbv4=rbv4, kF=kF, sm=sm, sin4=sin4: e.scalar_tensor_tensor(
                    rbv4, kF, sm[:, 6:7], sin4, op0=ALU.mult, op1=ALU.mult),
                    wait=[(sV, v3)], inc=sV)
                v5 = sV.n
                P.add('v', lambda e, krv=krv, rav4=rav4, rbv4=rbv4: e.tensor_tensor(
                    krv[:, :, 64:128], rav4, rbv4, op=ALU.add),
                    wait=[(sV, v4), (sV, v5)], inc=sV)
                v_krope = sV.n
                haz_kf[p] = v_krope
                haz_sm[p] = v_krope
                CC[('vqrope', i)] = v_qrope
                CC[('vkrope', i)] = v_krope

            def stage_c(i):
                nonlocal ps03_free, ps_k_free, haz_qT_store
                p = i % 2
                b, t = i // NT, i % NT
                qb, t4 = t // 4, t % 4
                for n in range(16):
                    P.add('t', lambda e, qr=qr_b[p], n=n: e.matmul(
                        PS[:, n * 128:(n + 1) * 128], qr[:, n * 128:(n + 1) * 128],
                        eye[:, :], start=True, stop=True),
                        wait=([(sV, CC[('vqrope', i)]), (sS, ps03_free)] if n == 0 else []),
                        inc=(sT if n == 15 else None))
                t_qtr = sT.n
                haz_qr[p] = t_qtr
                P.add('s', lambda e, t4=t4: e.copy(
                    qT_qb[:, :].rearrange("p (h n) -> p h n", n=512)[:, :, t4 * 128:(t4 + 1) * 128],
                    PS[:, 0:2048].rearrange("p (h n) -> p h n", n=128)),
                    wait=[(sT, t_qtr), (dQT_S, haz_qT_store)], inc=sS)
                s_qT = sS.n
                ps03_free = s_qT
                for n in range(4):
                    P.add('t', lambda e, kr=kr_b[p], n=n: e.matmul(
                        PS[:, 2048 + n * 128: 2048 + (n + 1) * 128],
                        kr[:, n * 128:(n + 1) * 128], eye[:, :], start=True, stop=True),
                        wait=([(sV, CC[('vkrope', i)]), (sS, ps_k_free)] if n == 0 else []),
                        inc=(sT if n == 3 else None))
                t_ktr = sT.n
                haz_kr[p] = t_ktr
                P.add('s', lambda e, kst=kst_b[p]: e.copy(kst[:, :], PS[:, 2048:2560]),
                      wait=[(sT, t_ktr), (dKST[p], haz_kst[p])], inc=sS)
                s_kT = sS.n
                ps_k_free = s_kT
                P.add('y', lambda e, kst=kst_b[p], b=b, t=t: e.dma_start(
                    kT_d[b * 128:(b + 1) * 128, :].rearrange(
                        "p (h n) -> p h n", n=2048)[:, :, t * 128:(t + 1) * 128],
                    kst[:, :].rearrange("p (h n) -> p h n", n=128)),
                    wait=[(sS, s_kT)], inc=dKST[p], inc_n=16)
                haz_kst[p] = dKST[p].n
                if t4 == 3:
                    rows = (b * NQB + qb) * 128
                    P.add('y', lambda e, rows=rows: e.dma_start(
                        qT_d[rows:rows + 128, :], qT_qb[:, :]),
                        wait=[(sS, s_qT)], inc=dQT_S, inc_n=16)
                    haz_qT_store = dQT_S.n

            stage_a_load(0)
            for step in range(NTILES + 2):
                if step + 1 < NTILES:
                    stage_a_load(step + 1)
                if 2 <= step:
                    stage_c(step - 2)
                if step < NTILES:
                    stage_a_ln(step)
                    stage_a_xn(step)
                if 1 <= step <= NTILES:
                    stage_b_q(step - 1)
                    stage_b_kv(step - 1)
                if step < NTILES:
                    stage_a_tr(step)

            barrier()

        with ExitStack() as st:
            wo_sb = st.enter_context(nc.sbuf_tensor("wo_sb", [128, 16 * 2048], BF16))
            kT_sb = st.enter_context(nc.sbuf_tensor("kT_sb", [128, 8192], BF16))
            v_sb = st.enter_context(nc.sbuf_tensor("v_sb", [128, 8192], BF16))
            qT_l = [st.enter_context(nc.sbuf_tensor("qTl0", [128, 8192], BF16))]
            P_sb = st.enter_context(nc.sbuf_tensor("P_sb", [128, 4 * 512], BF16))
            encT = st.enter_context(nc.sbuf_tensor("encT", [128, 8192], BF16))
            rs_f = st.enter_context(nc.sbuf_tensor("rs_f", [1, 512], F32))
            rs_bf = st.enter_context(nc.sbuf_tensor("rs_bf", [1, 512], BF16))
            rsb_sb = st.enter_context(nc.sbuf_tensor("rsb_sb", [128, 512], F32))
            rstmp = st.enter_context(nc.sbuf_tensor("rstmp", [128, 512], F32))
            at_b = [st.enter_context(nc.sbuf_tensor(f"atp{p}", [128, 512], F32)) for p in range(2)]
            x2_b = [st.enter_context(nc.sbuf_tensor("x20", [128, 2048], F32))]
            NPQ = 6
            qw_b = [st.enter_context(nc.sbuf_tensor(f"qw{p}", [128, 512], BF16)) for p in range(NPQ)]
            qaw_b = [st.enter_context(nc.sbuf_tensor(f"qaw{p}", [128, 512], BF16)) for p in range(4)]
            qsg_b = [st.enter_context(nc.sbuf_tensor(f"qsg{p}", [128, 512], BF16)) for p in range(4)]
            qsc_b = [st.enter_context(nc.sbuf_tensor(f"qsc{p}", [2, 512], BF16)) for p in range(2)]
            qsr_b = [st.enter_context(nc.sbuf_tensor(f"qsr{p}", [128, 512], BF16)) for p in range(3)]
            qm_b = [st.enter_context(nc.sbuf_tensor(f"qm{p}", [128, 512], BF16)) for p in range(2)]
            qq_b = [st.enter_context(nc.sbuf_tensor(f"qq{p}", [128, 512], BF16)) for p in range(3)]
            blko = st.enter_context(nc.sbuf_tensor("blko", [128, 2], BF16))
            blkT = st.enter_context(nc.sbuf_tensor("blkT", [2, 128], BF16))

            P.add('g', lambda e: e.dma_start(
                wo_sb[:, :].rearrange("p (c n) -> p c n", n=2048),
                wo_d.rearrange("(c p) n -> p c n", p=128)), inc=dGWT, inc_n=16)
            wo_ld = dGWT.n

            NQH = 768
            QC = {}

            def q_src(i):
                if i < 512:
                    f, hf = i // 2, i % 2
                    j, h = f // 2, f % 2
                    c0 = h * 1024 + hf * 512
                    return wgu_d[j * 128:(j + 1) * 128, c0:c0 + 512]
                f, hf = (i - 512) // 2, i % 2
                c, hs = f // 8, f % 8
                c0 = hs * 1024 + hf * 512
                return wp_d[c * 128:(c + 1) * 128, c0:c0 + 512]

            P.add('v', lambda e: e.memset(blko[:, :], 0.0), inc=sV)
            v_bz = sV.n
            P.add('v', lambda e: e.memset(blko[0:64, 0:1], 1.0),
                  wait=[(sV, v_bz)], inc=sV)
            v_b1 = sV.n
            P.add('v', lambda e: e.memset(blko[64:128, 1:2], 1.0),
                  wait=[(sV, v_b1)], inc=sV)
            blko_done = sV.n
            P.add('t', lambda e: e.matmul(
                PS[0:2, 3584:3712], blko[:, :], eye[:, :], start=True, stop=True),
                wait=[(sV, blko_done), (dEye, eye_done)], inc=sT)
            t_blkT = sT.n
            P.add('s', lambda e: e.copy(blkT[:, :], PS[0:2, 3584:3712]),
                  wait=[(sT, t_blkT)], inc=sS)
            s_blkT = sS.n

            def qst_load(i):
                p = i % NPQ
                P.add('g', lambda e, w=qw_b[p], src=q_src(i): e.dma_start(w[:, :], src),
                      wait=[(sT, QC.get(('tr', i - NPQ), 0))], inc=dGW[p], inc_n=16)
                QC[('ld', i)] = dGW[p].n

            def qst_tr(i):
                p = i % NPQ
                tc = 2048 + (i % 2) * 512
                for c in range(4):
                    P.add('t', lambda e, w=qw_b[p], c=c, tc=tc: e.matmul(
                        PS[:, tc + c * 128: tc + (c + 1) * 128],
                        w[:, c * 128:(c + 1) * 128], eye[:, :], start=True, stop=True),
                        wait=([(dGW[p], QC[('ld', i)]), (sS, QC.get(('sgn', i - 2), 0)),
                               (dEye, eye_done)] if c == 0 else []),
                        inc=(sT if c == 3 else None))
                QC[('tr', i)] = sT.n

            def qst_as(i):
                tc = 2048 + (i % 2) * 512
                pa = i % 4
                P.add('s', lambda e, aw=qaw_b[pa], tc=tc: e.activation(
                    aw[:, :], PS[:, tc:tc + 512], AF.Abs, scale=2.0),
                    wait=[(sT, QC[('tr', i)]), (sV, QC.get(('m', i - 4), 0))], inc=sS)
                QC[('aw', i)] = sS.n
                P.add('s', lambda e, sg=qsg_b[pa], tc=tc: e.activation(
                    sg[:, :], PS[:, tc:tc + 512], AF.Sign),
                    wait=[(sG, QC.get(('q', i - 4), 0))], inc=sS)
                QC[('sgn', i)] = sS.n

            def qst_gs(i):
                rc = 3072 + (i % 2) * 512
                w0 = [(sS, QC[('aw', i)]), (sV, QC.get(('evr', i - 2), 0))]
                if i == 1:
                    w0.append((sS, s_blkT))
                P.add('t', lambda e, aw=qaw_b[i % 4], rc=rc: e.matmul(
                    PS[0:2, rc:rc + 512], blko[:, :], aw[:, :], start=True, stop=True),
                    wait=w0, inc=sT)
                QC[('gs', i)] = sT.n

            def qst_screp(i):
                rc = 3072 + (i % 2) * 512
                psc, psr = i % 2, i % 3
                P.add('v', lambda e, sc=qsc_b[psc], rc=rc: e.tensor_scalar(
                    sc[:, :], PS[0:2, rc:rc + 512], 1.0 / 128.0, 1e-8,
                    op0=ALU.mult, op1=ALU.max),
                    wait=[(sT, QC[('gs', i)]), (sT, QC.get(('rep', i - 2), 0))], inc=sV)
                QC[('scv', i)] = sV.n
                w1 = [(sV, QC[('scv', i)])]
                if i == 0:
                    w1.append((sS, s_blkT))
                P.add('t', lambda e, sc=qsc_b[psc], rc=rc: e.matmul(
                    PS[:, rc:rc + 512], blkT[:, :], sc[:, :], start=True, stop=True),
                    wait=w1, inc=sT)
                QC[('rep', i)] = sT.n
                P.add('v', lambda e, sr=qsr_b[psr], rc=rc: e.tensor_copy(
                    sr[:, :], PS[:, rc:rc + 512]),
                    wait=[(sT, QC[('rep', i)]), (sV, QC.get(('wh', i - 3), 0))], inc=sV)
                QC[('evr', i)] = sV.n

            def qst_mqw(i):
                pm, pa, psr, pq = i % 2, i % 4, i % 3, i % 3
                P.add('v', lambda e, m=qm_b[pm], aw=qaw_b[pa], sr=qsr_b[psr]: e.tensor_tensor(
                    m[:, :], aw[:, :], sr[:, :], op=ALU.is_gt),
                    wait=[(sV, QC[('evr', i)]), (sV, QC.get(('wh', i - 2), 0))], inc=sV)
                QC[('m', i)] = sV.n
                P.add('g', lambda e, m=qm_b[pm], sg=qsg_b[pa]: e.tensor_tensor(
                    m[:, :], m[:, :], sg[:, :], op=ALU.mult),
                    wait=[(sV, QC[('m', i)]), (sS, QC[('sgn', i)])], inc=sG)
                QC[('q', i)] = sG.n
                P.add('v', lambda e, q=qq_b[pq], m=qm_b[pm], sr=qsr_b[psr]: e.tensor_tensor(
                    q[:, :], m[:, :], sr[:, :], op=ALU.mult),
                    wait=[(sG, QC[('q', i)]), (dQE[pq], QC.get(('st', i - 3), 0))], inc=sV)
                QC[('wh', i)] = sV.n

            def qst_fin(i):
                pq = i % 3
                if i < 512:
                    f, hf = i // 2, i % 2
                    j, h = f // 2, f % 2
                    jj, aa = (j, 0) if j < 64 else (j - 64, 1)
                    r = (jj * 2 + aa) * 128
                    c0 = h * 1024 + hf * 512
                    P.add('y', lambda e, q=qq_b[pq], r=r, c0=c0: e.dma_start(
                        wguT_d[r:r + 128, c0:c0 + 512], q[:, :]),
                        wait=[(sV, QC[('wh', i)])], inc=dQE[pq], inc_n=16)
                else:
                    f, hf = (i - 512) // 2, i % 2
                    c, hs = f // 8, f % 8
                    for hl in range(4):
                        hh = hf * 4 + hl
                        row = (hs * 4 + (c // 4)) * 128
                        col = hh * 512 + (c % 4) * 128
                        P.add('y', lambda e, q=qq_b[pq], hl=hl, row=row, col=col: e.dma_start(
                            wpT_d[row:row + 128, col:col + 128],
                            q[:, hl * 128:(hl + 1) * 128]),
                            wait=([(sV, QC[('wh', i)])] if hl == 0 else []),
                            inc=dQE[pq], inc_n=16)
                QC[('st', i)] = dQE[pq].n

            q_step = [0]

            def quant_step(n=1):
                for _ in range(n):
                    k = q_step[0]
                    if k >= NQH + 6:
                        return
                    q_step[0] += 1
                    if 4 <= k < NQH + 4:
                        qst_screp(k - 4)
                    if 2 <= k < NQH + 2:
                        qst_as(k - 2)
                    if 5 <= k < NQH + 5:
                        qst_mqw(k - 5)
                    if 6 <= k < NQH + 6:
                        qst_fin(k - 6)
                    if 3 <= k < NQH + 3:
                        qst_gs(k - 3)
                    if 1 <= k < NQH + 1:
                        qst_tr(k - 1)
                    if k < NQH:
                        qst_load(k)

            haz_kv_read = 0
            haz_qT_read = 0
            haz_P = {}
            haz_at = [0, 0]
            haz_x2 = [0]
            ps_L_free = [0, 0]
            ps_PV_free = 0
            ps_sums_free = 0
            ps_op_free = [0, 0]
            haz_encT = 0

            for b in range(BC):
                P.add('y', lambda e, b=b: e.dma_start(kT_sb[:, :], kT_d[b * 128:(b + 1) * 128, :]),
                      wait=[(sT, haz_kv_read)] + [(dd, dd.n) for dd in dKST + dVST],
                      inc=dKV, inc_n=16)
                P.add('y', lambda e, b=b: e.dma_start(v_sb[:, :], v_d[b * 128:(b + 1) * 128, :]),
                      inc=dKV, inc_n=16)
                kv_ld = dKV.n
                for qb in range(NQB):
                    rows = (b * NQB + qb) * 128
                    P.add('y', lambda e, rows=rows: e.dma_start(
                        qT_l[0][:, :], qT_d[rows:rows + 128, :]),
                        wait=[(sT, haz_qT_read), (dQT_S, dQT_S.n)], inc=dQTL[0], inc_n=16)
                    q_ld = dQTL[0].n
                    enc_last = 0
                    for u in range(16):
                        n_h = u
                        kvh = u // 4
                        exp_done = {}
                        for sc in range(18):
                            half = sc % 2
                            col = half * 512
                            if sc < 16:
                                if sc < 2:
                                    w0 = [(dKV, kv_ld), (dQTL[0], q_ld), (sS, ps_L_free[half]),
                                          (sT, haz_P.get(sc % 4, 0))]
                                else:
                                    w0 = [(sS, exp_done[sc - 2])]
                                P.add('t', lambda e, kvh=kvh, sc=sc, col=col, n_h=n_h: e.matmul(
                                    PS[:, col:col + 512],
                                    kT_sb[:, kvh * 2048 + sc * 128: kvh * 2048 + (sc + 1) * 128],
                                    qT_l[0][:, n_h * 512:(n_h + 1) * 512], start=True, stop=True),
                                    wait=w0, inc=sT)
                                t_L = sT.n
                                P.add('s', lambda e, sc=sc, col=col: e.activation(
                                    P_sb[:, (sc % 4) * 512:(sc % 4 + 1) * 512], PS[:, col:col + 512],
                                    AF.Exp, bias=cst[:, 3:4]),
                                    wait=[(sT, t_L), (sT, haz_P.get(sc % 4, 0))], inc=sS)
                                exp_done[sc] = sS.n
                                ps_L_free[half] = sS.n
                            if sc >= 2:
                                pv = sc - 2
                                P.add('t', lambda e, pv=pv, kvh=kvh: e.matmul(
                                    PS[:, 1024:1536],
                                    v_sb[:, pv * 512 + kvh * 128: pv * 512 + (kvh + 1) * 128],
                                    P_sb[:, (pv % 4) * 512:(pv % 4 + 1) * 512],
                                    start=(pv == 0), stop=(pv == 15)),
                                    wait=([(sS, exp_done[pv])] +
                                          ([(sV, ps_PV_free)] if pv == 0 else [])))
                                P.add('t', lambda e, pv=pv: e.matmul(
                                    PS[:, 1536:2048], ones128[:, :],
                                    P_sb[:, (pv % 4) * 512:(pv % 4 + 1) * 512],
                                    start=(pv == 0), stop=(pv == 15)),
                                    wait=([(sV, ones_done), (sS, ps_sums_free)] if pv == 0 else []),
                                    inc=sT)
                                haz_P[pv % 4] = sT.n
                            if sc % 3 == 2:
                                quant_step(1)
                        t_unit = sT.n
                        P.add('s', lambda e: e.activation(
                            rstmp[:, :], PS[:, 1536:2048], AF.Ln),
                            wait=[(sT, t_unit)], inc=sS)
                        s_ln = sS.n
                        ps_sums_free = s_ln
                        P.add('s', lambda e: e.activation(
                            rsb_sb[:, :], rstmp[:, :], AF.Exp, scale=-1.0),
                            wait=[(sS, s_ln), (sV, ps_PV_free)], inc=sS)
                        s_rs = sS.n
                        P.add('v', lambda e, n_h=n_h: e.tensor_tensor(
                            encT[:, n_h * 512:(n_h + 1) * 512], PS[:, 1024:1536],
                            rsb_sb[:, :], op=ALU.mult),
                            wait=[(sS, s_rs), (sT, haz_encT)], inc=sV)
                        ps_PV_free = sV.n
                        enc_last = sV.n
                    for st_i in range(4):
                        xrow = b * T + (qb * 4 + st_i) * 128
                        P.add('y', lambda e, xrow=xrow: e.dma_start(
                            x2_b[0][:, :], x_d[xrow:xrow + 128, :]),
                            wait=[(sV, haz_x2[0])], inc=dX2, inc_n=16)
                        x2_ld = dX2.n
                        for dc in range(4):
                            half = dc % 2
                            for n in range(16):
                                P.add('t', lambda e, n=n, st_i=st_i, dc=dc, half=half: e.matmul(
                                    PS[:, half * 512:(half + 1) * 512],
                                    encT[:, n * 512 + st_i * 128: n * 512 + (st_i + 1) * 128],
                                    wo_sb[:, n * 2048 + dc * 512: n * 2048 + (dc + 1) * 512],
                                    start=(n == 0), stop=(n == 15)),
                                    wait=([(sV, enc_last), (dGWT, wo_ld), (sV, ps_op_free[half]),
                                           (sS, ps_L_free[0]), (sS, ps_L_free[1])]
                                          if n == 0 else []),
                                    inc=(sT if n == 15 else None))
                            t_op = sT.n
                            pa = dc % 2
                            P.add('v', lambda e, pa=pa, dc=dc, half=half: e.tensor_tensor(
                                at_b[pa][:, :],
                                PS[:, half * 512:(half + 1) * 512],
                                x2_b[0][:, dc * 512:(dc + 1) * 512], op=ALU.add),
                                wait=[(sT, t_op), (dATS2[pa], haz_at[pa]), (dX2, x2_ld)], inc=sV)
                            ps_op_free[half] = sV.n
                            s_at = sV.n
                            arow = (b * NQB + qb) * 128
                            P.add('y', lambda e, arow=arow, st_i=st_i, dc=dc, pa=pa: e.dma_start(
                                attn_d[arow:arow + 128, st_i * 2048 + dc * 512: st_i * 2048 + (dc + 1) * 512],
                                at_b[pa][:, :]),
                                wait=[(sV, s_at)], inc=dATS2[pa], inc_n=16)
                            haz_at[pa] = dATS2[pa].n
                        haz_encT = t_op
                        haz_x2[0] = sV.n
                        quant_step(1)
                    haz_qT_read = sT.n
                haz_kv_read = sT.n
            quant_step(NQH + 6)

            barrier()

        with ExitStack() as st:
          if '3' in PHASES:
            xnT_l = [st.enter_context(nc.sbuf_tensor(f"xnTl{p}", [128, 8192], BF16)) for p in range(2)]
            wgu_l = [st.enter_context(nc.sbuf_tensor(f"wgul{p}", [128, 4096], BF16)) for p in range(3)]
            wp_l = [st.enter_context(nc.sbuf_tensor(f"wpl{p}", [128, 4096], BF16)) for p in range(2)]
            hT_sb = st.enter_context(nc.sbuf_tensor("hT_sb", [128, 32768], BF16))
            hsq_b = [st.enter_context(nc.sbuf_tensor(f"hsq{p}", [128, 512], BF16)) for p in range(2)]
            sg_b = [st.enter_context(nc.sbuf_tensor(f"sg{p}", [128, 512], F32)) for p in range(2)]
            at_l = [st.enter_context(nc.sbuf_tensor(f"atl{p}", [128, 512], F32)) for p in range(2)]
            out_sb = st.enter_context(nc.sbuf_tensor("out_sb", [128, 8192], F32))
            rsc_sb = st.enter_context(nc.sbuf_tensor("rsc_sb", [128, 8], F32))

            haz_xnT_rd = [0, 0]
            haz_wgu = [0, 0, 0]
            haz_wp = [0, 0]
            haz_hT = 0
            haz_gate_ps = [0, 0]
            haz_up_ps = [0, 0]
            haz_hsq = [0, 0]
            haz_sg = [0, 0]
            haz_proj_ps = [0, 0]
            haz_out_store = 0
            haz_at_rd = [0, 0]
            haz_rms_ps = 0

            for ci in range(BC * NQB):
                b, qb = ci // NQB, ci % NQB
                pc = ci % 2
                rows = (b * NQB + qb) * 128
                P.add('y', lambda e, rows=rows, pc=pc: e.dma_start(
                    xnT_l[pc][:, :], xnT_d[rows:rows + 128, :]),
                    wait=[(sT, haz_xnT_rd[pc]), (dXNT_S, dXNT_S.n)], inc=dXNT[pc], inc_n=16)
                xnT_ld = dXNT[pc].n
                hT_writes = 0
                v_hsq_by_j = {}
                for j in range(64):
                    pj = j % 2
                    pw = j % 3
                    P.add('y', lambda e, j=j, pw=pw: e.dma_start(
                        wgu_l[pw][:, :].rearrange("p (a n) -> p a n", n=2048),
                        wguT_d[j * 256:(j + 1) * 256, :].rearrange("(a p) n -> p a n", p=128)),
                        wait=[(sT, haz_wgu[pw])] + [(dd, dd.n) for dd in dQE],
                        inc=dWGU[pw], inc_n=16)
                    w_ld2 = dWGU[pw].n
                    for c in range(16):
                        P.add('t', lambda e, c=c, pj=pj, pw=pw, pc=pc: e.matmul(
                            PS[:, pj * 512:(pj + 1) * 512],
                            wgu_l[pw][:, c * 128:(c + 1) * 128],
                            xnT_l[pc][:, c * 512:(c + 1) * 512],
                            start=(c == 0), stop=(c == 15)),
                            wait=(([(dWGU[pw], w_ld2), (dXNT[pc], xnT_ld), (sS, haz_gate_ps[pj])] +
                                   ([(sV, haz_proj_ps[0])] if j == 0 else []))
                                  if c == 0 else []))
                    for c in range(16):
                        P.add('t', lambda e, c=c, pj=pj, pw=pw, pc=pc: e.matmul(
                            PS[:, 1024 + pj * 512: 1024 + (pj + 1) * 512],
                            wgu_l[pw][:, 2048 + c * 128: 2048 + (c + 1) * 128],
                            xnT_l[pc][:, c * 512:(c + 1) * 512],
                            start=(c == 0), stop=(c == 15)),
                            wait=([(sV, haz_up_ps[pj])] if c == 0 else []),
                            inc=(sT if c == 15 else None))
                    t_gu = sT.n
                    haz_wgu[pw] = t_gu
                    haz_xnT_rd[pc] = t_gu
                    P.add('s', lambda e, pj=pj: e.activation(
                        sg_b[pj][:, :], PS[:, pj * 512:(pj + 1) * 512], AF.Silu),
                        wait=[(sT, t_gu), (sV, haz_sg[pj])], inc=sS)
                    s_silu = sS.n
                    haz_gate_ps[pj] = s_silu
                    P.add('v', lambda e, j=j, pj=pj: e.tensor_tensor(
                        hT_sb[:, j * 512:(j + 1) * 512], sg_b[pj][:, :],
                        PS[:, 1024 + pj * 512: 1024 + (pj + 1) * 512], op=ALU.mult),
                        wait=[(sS, s_silu)] + ([(sT, haz_hT)] if j == 0 else []), inc=sV)
                    v_hT = sV.n
                    haz_up_ps[pj] = v_hT
                    haz_sg[pj] = v_hT
                    hT_writes = v_hT
                    P.add('v', lambda e, j=j, pj=pj: e.tensor_tensor(
                        hsq_b[pj][:, :], hT_sb[:, j * 512:(j + 1) * 512],
                        hT_sb[:, j * 512:(j + 1) * 512], op=ALU.mult),
                        wait=[(sV, v_hT), (sT, haz_hsq[pj])], inc=sV)
                    v_hsq_by_j[j] = sV.n
                    for jr in ([j - 1] if j >= 1 else []) + ([j] if j == 63 else []):
                        pjr = jr % 2
                        for st_i in range(4):
                            w0 = []
                            if st_i == 0:
                                w0 = [(sV, v_hsq_by_j[jr])]
                                if jr == 0:
                                    w0 += [(sS, haz_rms_ps), (sV, haz_proj_ps[1])]
                            P.add('t', lambda e, st_i=st_i, pjr=pjr, jr=jr: e.matmul(
                                PS[:, (4 + st_i) * 512: (4 + st_i) * 512 + 1],
                                hsq_b[pjr][:, st_i * 128:(st_i + 1) * 128], ones_col[:, :],
                                start=(jr == 0), stop=(jr == 63)),
                                wait=w0, inc=(sT if st_i == 3 else None))
                        haz_hsq[pjr] = sT.n
                t_rms = sT.n
                P.add('s', lambda e: e.activation(
                    rsc_sb[:, 4:8].rearrange("p (a o) -> p a o", o=1),
                    PS[:, 2048:4096].rearrange("p (a n) -> p a n", n=512)[:, :, 0:1],
                    AF.Ln, scale=1.0 / 8192.0, bias=cst[:, 4:5]),
                    wait=[(sT, t_rms)], inc=sS)
                s1 = sS.n
                P.add('s', lambda e: e.activation(
                    rsc_sb[:, 0:4], rsc_sb[:, 4:8], AF.Exp, scale=-0.5),
                    wait=[(sS, s1)], inc=sS)
                s_rsc = sS.n
                haz_rms_ps = s_rsc
                adds_by_st = [0, 0, 0, 0]
                for q in range(4):
                    ph = q % 2
                    base = ph * 2048
                    for jg in range(8):
                        pw = jg % 2
                        prow = (jg * 4 + q) * 128
                        P.add('y', lambda e, prow=prow, pw=pw: e.dma_start(
                            wp_l[pw][:, :], wpT_d[prow:prow + 128, :]),
                            wait=[(sT, haz_wp[pw]), (dASM, dASM.n)], inc=dWP[pw], inc_n=16)
                        wp_ld = dWP[pw].n
                        for jin in range(8):
                            j = jg * 8 + jin
                            for st_i in range(4):
                                w0 = []
                                if jin == 0 and st_i == 0 and jg == 0:
                                    w0 = [(dWP[pw], wp_ld), (sV, hT_writes), (sS, s_rsc),
                                          (sV, haz_proj_ps[ph])]
                                elif jin == 0 and st_i == 0:
                                    w0 = [(dWP[pw], wp_ld)]
                                P.add('t', lambda e, j=j, jin=jin, st_i=st_i, pw=pw, base=base: e.matmul(
                                    PS[:, base + st_i * 512: base + (st_i + 1) * 512],
                                    hT_sb[:, j * 512 + st_i * 128: j * 512 + (st_i + 1) * 128],
                                    wp_l[pw][:, jin * 512:(jin + 1) * 512],
                                    start=(j == 0), stop=(j == 63)),
                                    wait=w0,
                                    inc=(sT if (jin == 7 and st_i == 3) else None))
                        haz_wp[pw] = sT.n
                    t_q = sT.n
                    if q >= 2:
                        haz_hT = t_q
                    for st_i in range(4):
                        pa = (q * 4 + st_i) % 2
                        arow = (b * NQB + qb) * 128
                        P.add('y', lambda e, arow=arow, st_i=st_i, q=q, pa=pa: e.dma_start(
                            at_l[pa][:, :],
                            attn_d[arow:arow + 128, st_i * 2048 + q * 512: st_i * 2048 + (q + 1) * 512]),
                            wait=[(sV, haz_at_rd[pa]), (dAT_S, dAT_S.n)], inc=dATL[pa], inc_n=16)
                        at_ld = dATL[pa].n
                        P.add('v', lambda e, st_i=st_i, q=q, base=base, pa=pa: e.scalar_tensor_tensor(
                            out_sb[:, st_i * 2048 + q * 512: st_i * 2048 + (q + 1) * 512],
                            PS[:, base + st_i * 512: base + (st_i + 1) * 512],
                            rsc_sb[:, st_i:st_i + 1], at_l[pa][:, :],
                            op0=ALU.mult, op1=ALU.add),
                            wait=[(sT, t_q), (dATL[pa], at_ld), (sS, s_rsc), (dOUT, haz_out_store)],
                            inc=sV)
                        adds_by_st[st_i] = sV.n
                        haz_at_rd[pa] = sV.n
                        haz_proj_ps[ph] = sV.n
                for st_i in range(4):
                    yrow = b * T + (qb * 4 + st_i) * 128
                    P.add('y', lambda e, yrow=yrow, st_i=st_i: e.dma_start(
                        y_d[yrow:yrow + 128, :], out_sb[:, st_i * 2048:(st_i + 1) * 2048]),
                        wait=[(sV, adds_by_st[st_i])], inc=dOUT, inc_n=16)
                haz_out_store = dOUT.n

        P.raw('y', lambda e, v=dOUT.n, h=dOUT.h: e.wait_ge(h, v))

        with nc.Block() as block:
            @block.tensor
            def _(e):
                for f in P.ops['t']:
                    f(e)

            @block.vector
            def _(e):
                for f in P.ops['v']:
                    f(e)

            @block.scalar
            def _(e):
                for f in P.ops['s']:
                    f(e)

            @block.gpsimd
            def _(e):
                for f in P.ops['g']:
                    f(e)

            @block.sync
            def _(e):
                for f in P.ops['y']:
                    f(e)

    return nc


def _sincos_table():
    fraction = np.arange(0, 128, 2, dtype=np.float64) / 128.0
    inv_freq = 1.0 / (10000.0 ** fraction)
    t = np.arange(T, dtype=np.float64)[:, None]
    ang = t * inv_freq[None, :]
    return np.concatenate([np.sin(ang), np.cos(ang)], axis=1).astype(np.float32)


def kernel(L0_attn_x, L0_attn_q_kernel, L0_attn_k_kernel, L0_attn_v_kernel,
           L0_attn_out_kernel, L0_ffn_w_gate_up, L0_ffn_w_proj):
    x = np.ascontiguousarray(np.asarray(L0_attn_x, dtype=np.float32))
    wq = np.ascontiguousarray(np.asarray(L0_attn_q_kernel, dtype=np.float32))
    wk = np.ascontiguousarray(np.asarray(L0_attn_k_kernel, dtype=np.float32))
    wv = np.ascontiguousarray(np.asarray(L0_attn_v_kernel, dtype=np.float32))
    wo = np.ascontiguousarray(np.asarray(L0_attn_out_kernel, dtype=np.float32))
    wgu = np.ascontiguousarray(np.asarray(L0_ffn_w_gate_up, dtype=np.float32))
    wp = np.ascontiguousarray(np.asarray(L0_ffn_w_proj, dtype=np.float32))
    sincos = _sincos_table()
    eye = np.eye(128, dtype=np.float32).astype(bfloat16)

    nc = build_nc()
    in_maps = []
    for c in range(N_CORES):
        in_maps.append(dict(
            x=np.ascontiguousarray(x[c * BC:(c + 1) * BC].reshape(BC * T, D)),
            wq=wq, wk=wk, wv=wv, wo=wo, wgu=wgu, wp=wp,
            sincos=sincos, eye=eye))
    res = run_bass_kernel_spmd(nc, in_maps, core_ids=list(range(N_CORES)))
    out = np.concatenate([np.asarray(r["y"]).reshape(BC, T, D) for r in res.results], axis=0)
    return out


if __name__ == "__main__":
    import time
    t0 = time.time()
    nc = build_nc()
    n = sum(1 for _ in nc.all_instructions())
    print(f"build ok in {time.time()-t0:.1f}s, {n} instructions")


# version 26
# speedup vs baseline: 1.0973x; 1.0973x over previous
import numpy as np
import ml_dtypes
from contextlib import ExitStack

import concourse.bass as bass
import concourse.mybir as mybir
from concourse.bass_utils import run_bass_kernel_spmd

F32 = mybir.dt.float32
BF16 = mybir.dt.bfloat16
AF = mybir.ActivationFunctionType
ALU = mybir.AluOpType
AX = mybir.AxisListType
bfloat16 = ml_dtypes.bfloat16

DEBUG = False
PHASES = "Q123"
N_CORES = 8
BC = 2
T = 2048
D = 2048
NT = 16
NQB = 4
LN_EPS = 1e-5
RMS_EPS = 1e-6
FFN_EPS = float(np.finfo(np.float32).eps)
LOG_INV_SQRT_H = float(np.log(1.0 / np.sqrt(128.0)))
SOFTMAX_SHIFT = -12.0


class Sem:
    def __init__(self, st, nc, name):
        self.h = st.enter_context(nc.semaphore(name))
        self.n = 0

    def inc(self, k=1):
        self.n += k
        return self.n


class Plan:
    def __init__(self):
        self.ops = {k: [] for k in "tvsgy"}

    def add(self, eng, builder, wait=(), inc=None, inc_n=1):
        if inc is not None:
            inc.inc(inc_n)
        wl = [(s.h, v) for (s, v) in wait if v > 0]
        ih = inc.h if inc is not None else None

        def run(e, wl=wl, ih=ih, inc_n=inc_n, builder=builder):
            for h, v in wl:
                e.wait_ge(h, v)
            ins = builder(e)
            if ih is not None:
                ins.then_inc(ih, inc_n)
        self.ops[eng].append(run)

    def raw(self, eng, fn):
        self.ops[eng].append(fn)


def build_nc():
    nc = bass.Bass("TRN2", target_bir_lowering=False)
    kind_scratch = "ExternalOutput" if DEBUG else "Internal"

    x_d = nc.dram_tensor("x", [BC * T, D], F32, kind="ExternalInput").ap()
    wq_d = nc.dram_tensor("wq", [D, 2048], F32, kind="ExternalInput").ap()
    wk_d = nc.dram_tensor("wk", [D, 512], F32, kind="ExternalInput").ap()
    wv_d = nc.dram_tensor("wv", [D, 512], F32, kind="ExternalInput").ap()
    wo_d = nc.dram_tensor("wo", [2048, D], F32, kind="ExternalInput").ap()
    wgu_d = nc.dram_tensor("wgu", [16384, D], F32, kind="ExternalInput").ap()
    wp_d = nc.dram_tensor("wp", [D, 8192], F32, kind="ExternalInput").ap()
    sc_d = nc.dram_tensor("sincos", [T, 128], F32, kind="ExternalInput").ap()
    eye_d = nc.dram_tensor("eye", [128, 128], BF16, kind="ExternalInput").ap()
    y_d = nc.dram_tensor("y", [BC * T, D], F32, kind="ExternalOutput").ap()

    wguT_d = nc.dram_tensor("wguT_s", [128 * 128, 2048], BF16, kind=kind_scratch).ap()
    wpT_d = nc.dram_tensor("wpT_s", [32 * 128, 4096], BF16, kind=kind_scratch).ap()
    xnT_d = nc.dram_tensor("xnT_s", [BC * NQB * 128, 8192], BF16, kind=kind_scratch).ap()
    qT_d = nc.dram_tensor("qT_s", [BC * NQB * 128, 8192], BF16, kind=kind_scratch).ap()
    kT_d = nc.dram_tensor("kT_s", [BC * 128, 8192], BF16, kind=kind_scratch).ap()
    v_d = nc.dram_tensor("v_s", [BC * 128, 8192], BF16, kind=kind_scratch).ap()
    attn_d = nc.dram_tensor("attn_s", [BC * NQB * 128, 8192], F32, kind=kind_scratch).ap()

    P = Plan()
    st_top = ExitStack()
    with st_top:
        sT = Sem(st_top, nc, "sT")
        sV = Sem(st_top, nc, "sV")
        sS = Sem(st_top, nc, "sS")
        sG = Sem(st_top, nc, "sG")
        sB = Sem(st_top, nc, "sB")
        dEye = Sem(st_top, nc, "dEye")
        dGW = [Sem(st_top, nc, f"dGW{i}") for i in range(6)]
        dGWT = Sem(st_top, nc, "dGWT")
        dQE = [Sem(st_top, nc, f"dQE{i}") for i in range(6)]
        dASM = Sem(st_top, nc, "dASM")
        dSC = Sem(st_top, nc, "dSC")
        dX = [Sem(st_top, nc, f"dX{i}") for i in range(2)]
        dXNT_S = Sem(st_top, nc, "dXNT_S")
        dQT_S = Sem(st_top, nc, "dQT_S")
        dKST = [Sem(st_top, nc, f"dKST{i}") for i in range(2)]
        dVST = [Sem(st_top, nc, f"dVST{i}") for i in range(2)]
        dKV = Sem(st_top, nc, "dKV")
        dQTL = [Sem(st_top, nc, f"dQTL{i}") for i in range(2)]
        dX2 = Sem(st_top, nc, "dX2")
        dAT_S = Sem(st_top, nc, "dAT_S")
        dATS2 = [Sem(st_top, nc, f"dATS2{i}") for i in range(2)]
        dXNT = [Sem(st_top, nc, f"dXNT{i}") for i in range(2)]
        dWGU = [Sem(st_top, nc, f"dWGU{i}") for i in range(3)]
        dWP = [Sem(st_top, nc, f"dWP{i}") for i in range(2)]
        dATL = [Sem(st_top, nc, f"dATL{i}") for i in range(2)]
        dOUT = Sem(st_top, nc, "dOUT")

        PS = st_top.enter_context(nc.psum_tensor("PS", [128, 4096], F32))
        eye = st_top.enter_context(nc.sbuf_tensor("eye_sb", [128, 128], BF16))
        ones_col = st_top.enter_context(nc.sbuf_tensor("ones_col", [128, 1], BF16))
        ones_row = st_top.enter_context(nc.sbuf_tensor("ones_row", [1, 128], BF16))
        ones128 = st_top.enter_context(nc.sbuf_tensor("ones128", [128, 128], BF16))

        cst = st_top.enter_context(nc.sbuf_tensor("cst", [128, 8], F32))
        P.add('y', lambda e: e.dma_start(eye[:, :], eye_d), inc=dEye, inc_n=16)
        eye_done = dEye.n
        P.add('v', lambda e: e.memset(ones_col[:, :], 1.0), inc=sV)
        P.add('v', lambda e: e.memset(ones_row[:, :], 1.0), inc=sV)
        P.add('v', lambda e: e.memset(cst[:, 0:1], LN_EPS), inc=sV)
        P.add('v', lambda e: e.memset(cst[:, 1:2], RMS_EPS), inc=sV)
        P.add('v', lambda e: e.memset(cst[:, 2:3], LOG_INV_SQRT_H), inc=sV)
        P.add('v', lambda e: e.memset(cst[:, 3:4], SOFTMAX_SHIFT), inc=sV)
        P.add('v', lambda e: e.memset(cst[:, 4:5], FFN_EPS), inc=sV)
        P.add('v', lambda e: e.memset(ones128[:, :], 1.0), inc=sV)
        ones_done = sV.n
        P.raw('s', lambda e, sh=sV.h, v=ones_done: e.wait_ge(sh, v))

        ALL_DMA_SEMS = ([dEye, dGWT, dASM, dSC, dXNT_S, dQT_S, dKV, dX2, dAT_S, dOUT]
                        + dATS2
                        + dGW + dQE + dX + dKST + dVST + dQTL + dXNT + dWGU + dWP + dATL)

        def barrier():
            for d in ALL_DMA_SEMS:
                if d.n > 0:
                    P.raw('y', lambda e, h=d.h, v=d.n: e.wait_ge(h, v))
            tgt = sB.n + 5
            for eng in "tvsgy":
                P.raw(eng, lambda e, s=sB.h: e.sem_inc(s, 1))
            sB.n = tgt
            for eng in "tvsgy":
                P.raw(eng, lambda e, s=sB.h, t=tgt: e.wait_ge(s, t))

        with ExitStack() as st:
          if '1' in PHASES:
            wq_sb = st.enter_context(nc.sbuf_tensor("wq_sb", [128, 16 * 2048], BF16))
            wk_sb = st.enter_context(nc.sbuf_tensor("wk_sb", [128, 16 * 512], BF16))
            wv_sb = st.enter_context(nc.sbuf_tensor("wv_sb", [128, 16 * 512], BF16))
            sc_sb = st.enter_context(nc.sbuf_tensor("sc_sb", [128, 2048], F32))
            x_b = [st.enter_context(nc.sbuf_tensor(f"x{p}", [128, 2048], F32)) for p in range(2)]
            xn_b = [st.enter_context(nc.sbuf_tensor(f"xn{p}", [128, 2048], BF16)) for p in range(2)]
            qf_b = [st.enter_context(nc.sbuf_tensor(f"qf{p}", [128, 2048], F32)) for p in range(2)]
            qr_b = [st.enter_context(nc.sbuf_tensor(f"p1qr{p}", [128, 2048], BF16)) for p in range(2)]
            kf_b = [st.enter_context(nc.sbuf_tensor(f"kf{p}", [128, 512], F32)) for p in range(2)]
            kr_b = [st.enter_context(nc.sbuf_tensor(f"kr{p}", [128, 512], BF16)) for p in range(2)]
            ra = st.enter_context(nc.sbuf_tensor("ra", [128, 1024], F32))
            rb = st.enter_context(nc.sbuf_tensor("rb", [128, 1024], F32))
            junk = st.enter_context(nc.sbuf_tensor("junk", [128, 2048], BF16))
            sm_b = [st.enter_context(nc.sbuf_tensor(f"sm{p}", [128, 8], F32)) for p in range(2)]
            xnT_qb = st.enter_context(nc.sbuf_tensor("xnT_qb", [128, 8192], BF16))
            qT_qb = st.enter_context(nc.sbuf_tensor("qT_qb", [128, 8192], BF16))
            kst_b = [st.enter_context(nc.sbuf_tensor(f"kst{p}", [128, 512], BF16)) for p in range(2)]
            vst_b = [st.enter_context(nc.sbuf_tensor(f"vst{p}", [128, 512], BF16)) for p in range(2)]

            P.add('g', lambda e: e.dma_start(
                wq_sb[:, :].rearrange("p (c n) -> p c n", n=2048),
                wq_d.rearrange("(c p) n -> p c n", p=128)), inc=dGWT, inc_n=16)
            P.add('g', lambda e: e.dma_start(
                wk_sb[:, :].rearrange("p (c n) -> p c n", n=512),
                wk_d.rearrange("(c p) n -> p c n", p=128)), inc=dGWT, inc_n=16)
            P.add('g', lambda e: e.dma_start(
                wv_sb[:, :].rearrange("p (c n) -> p c n", n=512),
                wv_d.rearrange("(c p) n -> p c n", p=128)), inc=dGWT, inc_n=16)
            w_ld = dGWT.n
            for t in range(NT):
                P.add('y', lambda e, t=t: e.dma_start(
                    sc_sb[:, t * 128:(t + 1) * 128], sc_d[t * 128:(t + 1) * 128, :]),
                    inc=dSC, inc_n=16)
            sc_ld = dSC.n

            haz_x = [0, 0]
            haz_xsq = [0, 0]
            haz_xn = [0, 0]
            haz_qf = [0, 0]
            haz_kf = [0, 0]
            haz_sm = [0, 0]
            haz_qr = [0, 0]
            haz_kr = [0, 0]
            haz_xnT_store = 0
            haz_qT_store = 0
            haz_kst = [0, 0]
            haz_vst = [0, 0]
            ps03_free = 0
            ps_q_free = [0, 0]
            ps_k_free = 0
            ps_v_free = 0
            CC = {}

            NTILES = BC * NT

            def stage_a_load(i):
                p = i % 2
                b, t = i // NT, i % NT
                row0 = b * T + t * 128
                P.add('y', lambda e, x_=x_b[p], row0=row0: e.dma_start(
                    x_[:, :], x_d[row0:row0 + 128, :]),
                    wait=[(sV, haz_x[p]), (sS, haz_xsq[p])], inc=dX[p], inc_n=16)
                CC[('xld', i)] = dX[p].n

            def stage_a_ln(i):
                p = i % 2
                x_, xn, sm = x_b[p], xn_b[p], sm_b[p]
                x_ld = CC[('xld', i)]
                P.add('v', lambda e, sm=sm, x_=x_: e.reduce_sum(
                    sm[:, 0:1], x_[:, :], axis=AX.X),
                    wait=[(dX[p], x_ld)], inc=sV)
                v_sum = sV.n
                P.add('v', lambda e, sm=sm: e.tensor_single_scalar(
                    sm[:, 1:2], sm[:, 0:1], -1.0 / 2048.0, op=ALU.mult),
                    wait=[(sV, v_sum)], inc=sV)
                v_mean = sV.n
                P.add('s', lambda e, x_=x_, sm=sm: e.activation(
                    junk[:, :], x_[:, :], AF.Square, bias=sm[:, 1:2],
                    accum_out=sm[:, 2:3]),
                    wait=[(sV, v_mean)], inc=sS)
                s_ssq = sS.n
                haz_xsq[p] = s_ssq
                P.add('s', lambda e, sm=sm: e.activation(
                    sm[:, 3:4], sm[:, 2:3], AF.Ln, scale=1.0 / 2048.0, bias=cst[:, 0:1]),
                    wait=[(sS, s_ssq)], inc=sS)
                s_lnv = sS.n
                P.add('s', lambda e, sm=sm: e.activation(
                    sm[:, 4:5], sm[:, 3:4], AF.Exp, scale=-0.5),
                    wait=[(sS, s_lnv)], inc=sS)
                CC[('rstd', i)] = sS.n

            def stage_a_xn(i):
                p = i % 2
                x_, xn, sm = x_b[p], xn_b[p], sm_b[p]
                P.add('v', lambda e, xn=xn, x_=x_, sm=sm: e.tensor_scalar(
                    xn[:, :], x_[:, :], sm[:, 1:2], sm[:, 4:5],
                    op0=ALU.add, op1=ALU.mult),
                    wait=[(sS, CC[('rstd', i)]), (sT, haz_xn[p])], inc=sV)
                v_xn = sV.n
                haz_x[p] = v_xn
                CC[('vxn', i)] = v_xn

            def stage_a_tr(i):
                nonlocal ps03_free, haz_xnT_store
                p = i % 2
                b, t = i // NT, i % NT
                qb, t4 = t // 4, t % 4
                xn = xn_b[p]
                for c in range(16):
                    P.add('t', lambda e, xn=xn, c=c: e.matmul(
                        PS[:, c * 128:(c + 1) * 128], xn[:, c * 128:(c + 1) * 128],
                        eye[:, :], start=True, stop=True),
                        wait=([(sV, CC[('vxn', i)]), (sS, ps03_free), (dEye, eye_done)]
                              if c == 0 else []),
                        inc=(sT if c == 15 else None))
                t_xntr = sT.n
                haz_xn[p] = t_xntr
                P.add('s', lambda e, t4=t4: e.copy(
                    xnT_qb[:, :].rearrange("p (c n) -> p c n", n=512)[:, :, t4 * 128:(t4 + 1) * 128],
                    PS[:, 0:2048].rearrange("p (c n) -> p c n", n=128)),
                    wait=[(sT, t_xntr), (dXNT_S, haz_xnT_store)], inc=sS)
                s_xnT = sS.n
                ps03_free = s_xnT
                CC[('sxnT', i)] = s_xnT
                if t4 == 3:
                    rows = (b * NQB + qb) * 128
                    P.add('y', lambda e, rows=rows: e.dma_start(
                        xnT_d[rows:rows + 128, :], xnT_qb[:, :]),
                        wait=[(sS, s_xnT)], inc=dXNT_S, inc_n=16)
                    haz_xnT_store = dXNT_S.n

            def stage_b_q(i):
                p = i % 2
                b, t = i // NT, i % NT
                t4 = t % 4
                qf = qf_b[p]
                s_qev = 0
                for n in range(4):
                    bank = 3072 + (n % 2) * 512
                    for c in range(16):
                        P.add('t', lambda e, n=n, c=c, t4=t4, bank=bank: e.matmul(
                            PS[:, bank:bank + 512],
                            xnT_qb[:, c * 512 + t4 * 128: c * 512 + (t4 + 1) * 128],
                            wq_sb[:, c * 2048 + n * 512: c * 2048 + (n + 1) * 512],
                            start=(c == 0), stop=(c == 15)),
                            wait=([(sS, CC[('sxnT', i)]), (sS, ps_q_free[n % 2]), (dGWT, w_ld)]
                                  if c == 0 else []),
                            inc=(sT if c == 15 else None))
                    t_q = sT.n
                    P.add('s', lambda e, qf=qf, n=n, bank=bank: e.copy(
                        qf[:, n * 512:(n + 1) * 512], PS[:, bank:bank + 512]),
                        wait=[(sT, t_q)] + ([(sV, haz_qf[p])] if n == 0 else []), inc=sS)
                    s_qev = sS.n
                    ps_q_free[n % 2] = s_qev
                CC[('sqev', i)] = s_qev

            def stage_b_kv(i):
                nonlocal ps_k_free, ps_v_free
                p = i % 2
                b, t = i // NT, i % NT
                t4 = t % 4
                qf, qr, kf, kr, sm = qf_b[p], qr_b[p], kf_b[p], kr_b[p], sm_b[p]
                for c in range(16):
                    P.add('t', lambda e, c=c, t4=t4: e.matmul(
                        PS[:, 2048:2560],
                        xnT_qb[:, c * 512 + t4 * 128: c * 512 + (t4 + 1) * 128],
                        wk_sb[:, c * 512: (c + 1) * 512],
                        start=(c == 0), stop=(c == 15)),
                        wait=([(sS, ps_k_free)] if c == 0 else []),
                        inc=(sT if c == 15 else None))
                t_k = sT.n
                for c in range(16):
                    P.add('t', lambda e, c=c, t4=t4: e.matmul(
                        PS[:, 2560:3072],
                        xnT_qb[:, c * 512 + t4 * 128: c * 512 + (t4 + 1) * 128],
                        wv_sb[:, c * 512: (c + 1) * 512],
                        start=(c == 0), stop=(c == 15)),
                        wait=([(sS, ps_v_free)] if c == 0 else []),
                        inc=(sT if c == 15 else None))
                t_v = sT.n
                P.add('s', lambda e, qf=qf, sm=sm: e.activation(
                    junk[:, :], qf[:, :], AF.Square, accum_out=sm[:, 5:6]),
                    wait=[(sS, CC[('sqev', i)]), (sV, haz_sm[p])], inc=sS)
                s1 = sS.n
                P.add('s', lambda e, sm=sm: e.activation(
                    sm[:, 6:7], sm[:, 5:6], AF.Ln, scale=1.0 / 2048.0, bias=cst[:, 1:2]),
                    wait=[(sS, s1)], inc=sS)
                s2 = sS.n
                P.add('s', lambda e, sm=sm: e.activation(
                    sm[:, 5:6], sm[:, 6:7], AF.Exp, scale=-0.5, bias=cst[:, 2:3]),
                    wait=[(sS, s2)], inc=sS)
                s_qrs = sS.n
                P.add('s', lambda e, kf=kf: e.copy(kf[:, :], PS[:, 2048:2560]),
                      wait=[(sT, t_k), (sV, haz_kf[p])], inc=sS)
                s_kev = sS.n
                ps_k_free = s_kev
                P.add('s', lambda e, kf=kf, sm=sm: e.activation(
                    junk[:, 0:512], kf[:, :], AF.Square, accum_out=sm[:, 6:7]),
                    wait=[(sS, s_kev)], inc=sS)
                s1 = sS.n
                P.add('s', lambda e, sm=sm: e.activation(
                    sm[:, 7:8], sm[:, 6:7], AF.Ln, scale=1.0 / 512.0, bias=cst[:, 1:2]),
                    wait=[(sS, s1)], inc=sS)
                s2 = sS.n
                P.add('s', lambda e, sm=sm: e.activation(
                    sm[:, 6:7], sm[:, 7:8], AF.Exp, scale=-0.5),
                    wait=[(sS, s2)], inc=sS)
                s_krs = sS.n
                P.add('s', lambda e, vst=vst_b[p]: e.copy(vst[:, :], PS[:, 2560:3072]),
                      wait=[(sT, t_v), (dVST[p], haz_vst[p])], inc=sS)
                s_vev = sS.n
                ps_v_free = s_vev
                P.add('y', lambda e, vst=vst_b[p], b=b, t=t: e.dma_start(
                    v_d[b * 128:(b + 1) * 128, t * 512:(t + 1) * 512], vst[:, :]),
                    wait=[(sS, s_vev)], inc=dVST[p], inc_n=16)
                haz_vst[p] = dVST[p].n
                sin16 = sc_sb[:, t * 128: t * 128 + 64].rearrange(
                    "p (o k) -> p o k", o=1).broadcast_to([128, 16, 64])
                cos16 = sc_sb[:, t * 128 + 64: t * 128 + 128].rearrange(
                    "p (o k) -> p o k", o=1).broadcast_to([128, 16, 64])
                qv = qf[:, :].rearrange("p (h x) -> p h x", x=128)
                qF, qS = qv[:, :, 0:64], qv[:, :, 64:128]
                rav = ra[:, :].rearrange("p (h k) -> p h k", k=64)
                rbv = rb[:, :].rearrange("p (h k) -> p h k", k=64)
                qrv = qr[:, :].rearrange("p (h x) -> p h x", x=128)
                P.add('v', lambda e, rav=rav, qF=qF, sm=sm, cos16=cos16: e.scalar_tensor_tensor(
                    rav, qF, sm[:, 5:6], cos16, op0=ALU.mult, op1=ALU.mult),
                    wait=[(sS, s_qrs), (dSC, sc_ld)], inc=sV)
                v1 = sV.n
                P.add('v', lambda e, rbv=rbv, qS=qS, sm=sm, sin16=sin16: e.scalar_tensor_tensor(
                    rbv, qS, sm[:, 5:6], sin16, op0=ALU.mult, op1=ALU.mult),
                    wait=[(sS, s_qrs)], inc=sV)
                v2 = sV.n
                P.add('v', lambda e, qrv=qrv, rav=rav, rbv=rbv: e.tensor_tensor(
                    qrv[:, :, 0:64], rav, rbv, op=ALU.subtract),
                    wait=[(sV, v1), (sV, v2), (sT, haz_qr[p])], inc=sV)
                v3 = sV.n
                P.add('v', lambda e, rav=rav, qS=qS, sm=sm, cos16=cos16: e.scalar_tensor_tensor(
                    rav, qS, sm[:, 5:6], cos16, op0=ALU.mult, op1=ALU.mult),
                    wait=[(sV, v3)], inc=sV)
                v4 = sV.n
                P.add('v', lambda e, rbv=rbv, qF=qF, sm=sm, sin16=sin16: e.scalar_tensor_tensor(
                    rbv, qF, sm[:, 5:6], sin16, op0=ALU.mult, op1=ALU.mult),
                    wait=[(sV, v3)], inc=sV)
                v5 = sV.n
                P.add('v', lambda e, qrv=qrv, rav=rav, rbv=rbv: e.tensor_tensor(
                    qrv[:, :, 64:128], rav, rbv, op=ALU.add),
                    wait=[(sV, v4), (sV, v5)], inc=sV)
                v_qrope = sV.n
                haz_qf[p] = v_qrope
                sin4 = sc_sb[:, t * 128: t * 128 + 64].rearrange(
                    "p (o k) -> p o k", o=1).broadcast_to([128, 4, 64])
                cos4 = sc_sb[:, t * 128 + 64: t * 128 + 128].rearrange(
                    "p (o k) -> p o k", o=1).broadcast_to([128, 4, 64])
                kvv = kf[:, :].rearrange("p (h x) -> p h x", x=128)
                kF, kS = kvv[:, :, 0:64], kvv[:, :, 64:128]
                rav4 = ra[:, 0:256].rearrange("p (h k) -> p h k", k=64)
                rbv4 = rb[:, 0:256].rearrange("p (h k) -> p h k", k=64)
                krv = kr[:, :].rearrange("p (h x) -> p h x", x=128)
                P.add('v', lambda e, rav4=rav4, kF=kF, sm=sm, cos4=cos4: e.scalar_tensor_tensor(
                    rav4, kF, sm[:, 6:7], cos4, op0=ALU.mult, op1=ALU.mult),
                    wait=[(sS, s_krs), (sV, v_qrope)], inc=sV)
                v1 = sV.n
                P.add('v', lambda e, rbv4=rbv4, kS=kS, sm=sm, sin4=sin4: e.scalar_tensor_tensor(
                    rbv4, kS, sm[:, 6:7], sin4, op0=ALU.mult, op1=ALU.mult),
                    wait=[(sS, s_krs)], inc=sV)
                v2 = sV.n
                P.add('v', lambda e, krv=krv, rav4=rav4, rbv4=rbv4: e.tensor_tensor(
                    krv[:, :, 0:64], rav4, rbv4, op=ALU.subtract),
                    wait=[(sV, v1), (sV, v2), (sT, haz_kr[p])], inc=sV)
                v3 = sV.n
                P.add('v', lambda e, rav4=rav4, kS=kS, sm=sm, cos4=cos4: e.scalar_tensor_tensor(
                    rav4, kS, sm[:, 6:7], cos4, op0=ALU.mult, op1=ALU.mult),
                    wait=[(sV, v3)], inc=sV)
                v4 = sV.n
                P.add('v', lambda e, rbv4=rbv4, kF=kF, sm=sm, sin4=sin4: e.scalar_tensor_tensor(
                    rbv4, kF, sm[:, 6:7], sin4, op0=ALU.mult, op1=ALU.mult),
                    wait=[(sV, v3)], inc=sV)
                v5 = sV.n
                P.add('v', lambda e, krv=krv, rav4=rav4, rbv4=rbv4: e.tensor_tensor(
                    krv[:, :, 64:128], rav4, rbv4, op=ALU.add),
                    wait=[(sV, v4), (sV, v5)], inc=sV)
                v_krope = sV.n
                haz_kf[p] = v_krope
                haz_sm[p] = v_krope
                CC[('vqrope', i)] = v_qrope
                CC[('vkrope', i)] = v_krope

            def stage_c(i):
                nonlocal ps03_free, ps_k_free, haz_qT_store
                p = i % 2
                b, t = i // NT, i % NT
                qb, t4 = t // 4, t % 4
                for n in range(16):
                    P.add('t', lambda e, qr=qr_b[p], n=n: e.matmul(
                        PS[:, n * 128:(n + 1) * 128], qr[:, n * 128:(n + 1) * 128],
                        eye[:, :], start=True, stop=True),
                        wait=([(sV, CC[('vqrope', i)]), (sS, ps03_free)] if n == 0 else []),
                        inc=(sT if n == 15 else None))
                t_qtr = sT.n
                haz_qr[p] = t_qtr
                P.add('s', lambda e, t4=t4: e.copy(
                    qT_qb[:, :].rearrange("p (h n) -> p h n", n=512)[:, :, t4 * 128:(t4 + 1) * 128],
                    PS[:, 0:2048].rearrange("p (h n) -> p h n", n=128)),
                    wait=[(sT, t_qtr), (dQT_S, haz_qT_store)], inc=sS)
                s_qT = sS.n
                ps03_free = s_qT
                for n in range(4):
                    P.add('t', lambda e, kr=kr_b[p], n=n: e.matmul(
                        PS[:, 2048 + n * 128: 2048 + (n + 1) * 128],
                        kr[:, n * 128:(n + 1) * 128], eye[:, :], start=True, stop=True),
                        wait=([(sV, CC[('vkrope', i)]), (sS, ps_k_free)] if n == 0 else []),
                        inc=(sT if n == 3 else None))
                t_ktr = sT.n
                haz_kr[p] = t_ktr
                P.add('s', lambda e, kst=kst_b[p]: e.copy(kst[:, :], PS[:, 2048:2560]),
                      wait=[(sT, t_ktr), (dKST[p], haz_kst[p])], inc=sS)
                s_kT = sS.n
                ps_k_free = s_kT
                P.add('y', lambda e, kst=kst_b[p], b=b, t=t: e.dma_start(
                    kT_d[b * 128:(b + 1) * 128, :].rearrange(
                        "p (h n) -> p h n", n=2048)[:, :, t * 128:(t + 1) * 128],
                    kst[:, :].rearrange("p (h n) -> p h n", n=128)),
                    wait=[(sS, s_kT)], inc=dKST[p], inc_n=16)
                haz_kst[p] = dKST[p].n
                if t4 == 3:
                    rows = (b * NQB + qb) * 128
                    P.add('y', lambda e, rows=rows: e.dma_start(
                        qT_d[rows:rows + 128, :], qT_qb[:, :]),
                        wait=[(sS, s_qT)], inc=dQT_S, inc_n=16)
                    haz_qT_store = dQT_S.n

            stage_a_load(0)
            for step in range(NTILES + 2):
                if step + 1 < NTILES:
                    stage_a_load(step + 1)
                if 2 <= step:
                    stage_c(step - 2)
                if step < NTILES:
                    stage_a_ln(step)
                    stage_a_xn(step)
                if 1 <= step <= NTILES:
                    stage_b_q(step - 1)
                    stage_b_kv(step - 1)
                if step < NTILES:
                    stage_a_tr(step)

            barrier()

        with ExitStack() as st:
            wo_sb = st.enter_context(nc.sbuf_tensor("wo_sb", [128, 16 * 2048], BF16))
            kT_sb = st.enter_context(nc.sbuf_tensor("kT_sb", [128, 8192], BF16))
            v_sb = st.enter_context(nc.sbuf_tensor("v_sb", [128, 8192], BF16))
            qT_l = [st.enter_context(nc.sbuf_tensor("qTl0", [128, 8192], BF16))]
            P_sb = st.enter_context(nc.sbuf_tensor("P_sb", [128, 4 * 512], BF16))
            encT = st.enter_context(nc.sbuf_tensor("encT", [128, 8192], BF16))
            rs_f = st.enter_context(nc.sbuf_tensor("rs_f", [1, 512], F32))
            rs_bf = st.enter_context(nc.sbuf_tensor("rs_bf", [1, 512], BF16))
            rsb_sb = st.enter_context(nc.sbuf_tensor("rsb_sb", [128, 512], F32))
            rstmp = st.enter_context(nc.sbuf_tensor("rstmp", [128, 512], F32))
            at_b = [st.enter_context(nc.sbuf_tensor(f"atp{p}", [128, 512], F32)) for p in range(2)]
            x2_b = [st.enter_context(nc.sbuf_tensor("x20", [128, 2048], F32))]
            NPQ = 6
            qw_b = [st.enter_context(nc.sbuf_tensor(f"qw{p}", [128, 512], BF16)) for p in range(NPQ)]
            qaw_b = [st.enter_context(nc.sbuf_tensor(f"qaw{p}", [128, 512], BF16)) for p in range(4)]
            qsg_b = [st.enter_context(nc.sbuf_tensor(f"qsg{p}", [128, 512], BF16)) for p in range(4)]
            qsc_b = [st.enter_context(nc.sbuf_tensor(f"qsc{p}", [2, 512], BF16)) for p in range(2)]
            qsr_b = [st.enter_context(nc.sbuf_tensor(f"qsr{p}", [128, 512], BF16)) for p in range(3)]
            qm_b = [st.enter_context(nc.sbuf_tensor(f"qm{p}", [128, 512], BF16)) for p in range(2)]
            qq_b = [st.enter_context(nc.sbuf_tensor(f"qq{p}", [128, 512], BF16)) for p in range(3)]
            blko = st.enter_context(nc.sbuf_tensor("blko", [128, 2], BF16))
            blkT = st.enter_context(nc.sbuf_tensor("blkT", [2, 128], BF16))

            P.add('g', lambda e: e.dma_start(
                wo_sb[:, :].rearrange("p (c n) -> p c n", n=2048),
                wo_d.rearrange("(c p) n -> p c n", p=128)), inc=dGWT, inc_n=16)
            wo_ld = dGWT.n

            NQH = 768
            QC = {}

            def q_src(i):
                if i < 512:
                    f, hf = i // 2, i % 2
                    j, h = f // 2, f % 2
                    c0 = h * 1024 + hf * 512
                    return wgu_d[j * 128:(j + 1) * 128, c0:c0 + 512]
                f, hf = (i - 512) // 2, i % 2
                c, hs = f // 8, f % 8
                c0 = hs * 1024 + hf * 512
                return wp_d[c * 128:(c + 1) * 128, c0:c0 + 512]

            P.add('v', lambda e: e.memset(blko[64:128, 1:2], 1.0), inc=sV)
            P.add('v', lambda e: e.memset(blko[0:64, 1:2], 0.0), inc=sV)
            P.add('v', lambda e: e.memset(blko[64:128, 0:1], 0.0), inc=sV)
            P.add('v', lambda e: e.memset(blko[0:64, 0:1], 1.0), inc=sV)
            blko_done = sV.n
            P.add('t', lambda e: e.matmul(
                PS[0:2, 3584:3712], blko[:, :], eye[:, :], start=True, stop=True),
                wait=[(sV, blko_done), (dEye, eye_done)], inc=sT)
            t_blkT = sT.n
            P.add('s', lambda e: e.copy(blkT[:, :], PS[0:2, 3584:3712]),
                  wait=[(sT, t_blkT)], inc=sS)
            s_blkT = sS.n

            def qst_load(i):
                p = i % NPQ
                P.add('g', lambda e, w=qw_b[p], src=q_src(i): e.dma_start(w[:, :], src),
                      wait=[(sT, QC.get(('tr', i - NPQ), 0))], inc=dGW[p], inc_n=16)
                QC[('ld', i)] = dGW[p].n

            def qst_tr(i):
                p = i % NPQ
                tc = 2048 + (i % 2) * 512
                for c in range(4):
                    P.add('t', lambda e, w=qw_b[p], c=c, tc=tc: e.matmul(
                        PS[:, tc + c * 128: tc + (c + 1) * 128],
                        w[:, c * 128:(c + 1) * 128], eye[:, :], start=True, stop=True),
                        wait=([(dGW[p], QC[('ld', i)]), (sS, QC.get(('sgn', i - 2), 0)),
                               (dEye, eye_done)] if c == 0 else []),
                        inc=(sT if c == 3 else None))
                QC[('tr', i)] = sT.n

            def qst_as(i):
                tc = 2048 + (i % 2) * 512
                pa = i % 4
                P.add('s', lambda e, aw=qaw_b[pa], tc=tc: e.activation(
                    aw[:, :], PS[:, tc:tc + 512], AF.Abs, scale=2.0),
                    wait=[(sT, QC[('tr', i)]), (sV, QC.get(('m', i - 4), 0))], inc=sS)
                QC[('aw', i)] = sS.n
                P.add('s', lambda e, sg=qsg_b[pa], tc=tc: e.activation(
                    sg[:, :], PS[:, tc:tc + 512], AF.Sign),
                    wait=[(sG, QC.get(('q', i - 4), 0))], inc=sS)
                QC[('sgn', i)] = sS.n

            def qst_gs(i):
                rc = 3072 + (i % 2) * 512
                w0 = [(sS, QC[('aw', i)]), (sV, QC.get(('evr', i - 2), 0))]
                if i == 1:
                    w0.append((sS, s_blkT))
                P.add('t', lambda e, aw=qaw_b[i % 4], rc=rc: e.matmul(
                    PS[0:2, rc:rc + 512], blko[:, :], aw[:, :], start=True, stop=True),
                    wait=w0, inc=sT)
                QC[('gs', i)] = sT.n

            def qst_screp(i):
                rc = 3072 + (i % 2) * 512
                psc, psr = i % 2, i % 3
                P.add('v', lambda e, sc=qsc_b[psc], rc=rc: e.tensor_scalar(
                    sc[:, :], PS[0:2, rc:rc + 512], 1.0 / 128.0, 1e-8,
                    op0=ALU.mult, op1=ALU.max),
                    wait=[(sT, QC[('gs', i)]), (sT, QC.get(('rep', i - 2), 0))], inc=sV)
                QC[('scv', i)] = sV.n
                w1 = [(sV, QC[('scv', i)])]
                if i == 0:
                    w1.append((sS, s_blkT))
                P.add('t', lambda e, sc=qsc_b[psc], rc=rc: e.matmul(
                    PS[:, rc:rc + 512], blkT[:, :], sc[:, :], start=True, stop=True),
                    wait=w1, inc=sT)
                QC[('rep', i)] = sT.n
                P.add('v', lambda e, sr=qsr_b[psr], rc=rc: e.tensor_copy(
                    sr[:, :], PS[:, rc:rc + 512]),
                    wait=[(sT, QC[('rep', i)]), (sV, QC.get(('wh', i - 3), 0))], inc=sV)
                QC[('evr', i)] = sV.n

            def qst_mqw(i):
                pm, pa, psr, pq = i % 2, i % 4, i % 3, i % 3
                P.add('v', lambda e, m=qm_b[pm], aw=qaw_b[pa], sr=qsr_b[psr]: e.tensor_tensor(
                    m[:, :], aw[:, :], sr[:, :], op=ALU.is_gt),
                    wait=[(sV, QC[('evr', i)]), (sV, QC.get(('wh', i - 2), 0))], inc=sV)
                QC[('m', i)] = sV.n
                P.add('g', lambda e, m=qm_b[pm], sg=qsg_b[pa]: e.tensor_tensor(
                    m[:, :], m[:, :], sg[:, :], op=ALU.mult),
                    wait=[(sV, QC[('m', i)]), (sS, QC[('sgn', i)])], inc=sG)
                QC[('q', i)] = sG.n
                P.add('v', lambda e, q=qq_b[pq], m=qm_b[pm], sr=qsr_b[psr]: e.tensor_tensor(
                    q[:, :], m[:, :], sr[:, :], op=ALU.mult),
                    wait=[(sG, QC[('q', i)]), (dQE[pq], QC.get(('st', i - 3), 0))], inc=sV)
                QC[('wh', i)] = sV.n

            def qst_fin(i):
                pq = i % 3
                if i < 512:
                    f, hf = i // 2, i % 2
                    j, h = f // 2, f % 2
                    jj, aa = (j, 0) if j < 64 else (j - 64, 1)
                    r = (jj * 2 + aa) * 128
                    c0 = h * 1024 + hf * 512
                    P.add('y', lambda e, q=qq_b[pq], r=r, c0=c0: e.dma_start(
                        wguT_d[r:r + 128, c0:c0 + 512], q[:, :]),
                        wait=[(sV, QC[('wh', i)])], inc=dQE[pq], inc_n=16)
                else:
                    f, hf = (i - 512) // 2, i % 2
                    c, hs = f // 8, f % 8
                    for hl in range(4):
                        hh = hf * 4 + hl
                        row = (hs * 4 + (c // 4)) * 128
                        col = hh * 512 + (c % 4) * 128
                        P.add('y', lambda e, q=qq_b[pq], hl=hl, row=row, col=col: e.dma_start(
                            wpT_d[row:row + 128, col:col + 128],
                            q[:, hl * 128:(hl + 1) * 128]),
                            wait=([(sV, QC[('wh', i)])] if hl == 0 else []),
                            inc=dQE[pq], inc_n=16)
                QC[('st', i)] = dQE[pq].n

            q_step = [0]

            def quant_step(n=1):
                for _ in range(n):
                    k = q_step[0]
                    if k >= NQH + 6:
                        return
                    q_step[0] += 1
                    if 4 <= k < NQH + 4:
                        qst_screp(k - 4)
                    if 2 <= k < NQH + 2:
                        qst_as(k - 2)
                    if 5 <= k < NQH + 5:
                        qst_mqw(k - 5)
                    if 6 <= k < NQH + 6:
                        qst_fin(k - 6)
                    if 3 <= k < NQH + 3:
                        qst_gs(k - 3)
                    if 1 <= k < NQH + 1:
                        qst_tr(k - 1)
                    if k < NQH:
                        qst_load(k)

            haz_kv_read = 0
            haz_qT_read = 0
            haz_P = {}
            haz_at = [0, 0]
            haz_x2 = [0]
            ps_L_free = [0, 0]
            ps_PV_free = 0
            ps_sums_free = 0
            ps_op_free = [0, 0]
            haz_encT = 0

            for b in range(BC):
                P.add('y', lambda e, b=b: e.dma_start(kT_sb[:, :], kT_d[b * 128:(b + 1) * 128, :]),
                      wait=[(sT, haz_kv_read)] + [(dd, dd.n) for dd in dKST + dVST],
                      inc=dKV, inc_n=16)
                P.add('y', lambda e, b=b: e.dma_start(v_sb[:, :], v_d[b * 128:(b + 1) * 128, :]),
                      inc=dKV, inc_n=16)
                kv_ld = dKV.n
                for qb in range(NQB):
                    rows = (b * NQB + qb) * 128
                    P.add('y', lambda e, rows=rows: e.dma_start(
                        qT_l[0][:, :], qT_d[rows:rows + 128, :]),
                        wait=[(sT, haz_qT_read), (dQT_S, dQT_S.n)], inc=dQTL[0], inc_n=16)
                    q_ld = dQTL[0].n
                    enc_last = 0
                    for u in range(16):
                        n_h = u
                        kvh = u // 4
                        exp_done = {}
                        for sc in range(18):
                            half = sc % 2
                            col = half * 512
                            if sc < 16:
                                if sc < 2:
                                    w0 = [(dKV, kv_ld), (dQTL[0], q_ld), (sS, ps_L_free[half]),
                                          (sT, haz_P.get(sc % 4, 0))]
                                else:
                                    w0 = [(sS, exp_done[sc - 2])]
                                P.add('t', lambda e, kvh=kvh, sc=sc, col=col, n_h=n_h: e.matmul(
                                    PS[:, col:col + 512],
                                    kT_sb[:, kvh * 2048 + sc * 128: kvh * 2048 + (sc + 1) * 128],
                                    qT_l[0][:, n_h * 512:(n_h + 1) * 512], start=True, stop=True),
                                    wait=w0, inc=sT)
                                t_L = sT.n
                                P.add('s', lambda e, sc=sc, col=col: e.activation(
                                    P_sb[:, (sc % 4) * 512:(sc % 4 + 1) * 512], PS[:, col:col + 512],
                                    AF.Exp, bias=cst[:, 3:4]),
                                    wait=[(sT, t_L), (sT, haz_P.get(sc % 4, 0))], inc=sS)
                                exp_done[sc] = sS.n
                                ps_L_free[half] = sS.n
                            if sc >= 2:
                                pv = sc - 2
                                P.add('t', lambda e, pv=pv, kvh=kvh: e.matmul(
                                    PS[:, 1024:1536],
                                    v_sb[:, pv * 512 + kvh * 128: pv * 512 + (kvh + 1) * 128],
                                    P_sb[:, (pv % 4) * 512:(pv % 4 + 1) * 512],
                                    start=(pv == 0), stop=(pv == 15)),
                                    wait=([(sS, exp_done[pv])] +
                                          ([(sV, ps_PV_free)] if pv == 0 else [])))
                                P.add('t', lambda e, pv=pv: e.matmul(
                                    PS[:, 1536:2048], ones128[:, :],
                                    P_sb[:, (pv % 4) * 512:(pv % 4 + 1) * 512],
                                    start=(pv == 0), stop=(pv == 15)),
                                    wait=([(sV, ones_done), (sS, ps_sums_free)] if pv == 0 else []),
                                    inc=sT)
                                haz_P[pv % 4] = sT.n
                            if sc % 3 == 2:
                                quant_step(1)
                        t_unit = sT.n
                        P.add('s', lambda e: e.activation(
                            rstmp[:, :], PS[:, 1536:2048], AF.Ln),
                            wait=[(sT, t_unit)], inc=sS)
                        s_ln = sS.n
                        ps_sums_free = s_ln
                        P.add('s', lambda e: e.activation(
                            rsb_sb[:, :], rstmp[:, :], AF.Exp, scale=-1.0),
                            wait=[(sS, s_ln), (sV, ps_PV_free)], inc=sS)
                        s_rs = sS.n
                        P.add('v', lambda e, n_h=n_h: e.tensor_tensor(
                            encT[:, n_h * 512:(n_h + 1) * 512], PS[:, 1024:1536],
                            rsb_sb[:, :], op=ALU.mult),
                            wait=[(sS, s_rs), (sT, haz_encT)], inc=sV)
                        ps_PV_free = sV.n
                        enc_last = sV.n
                    for st_i in range(4):
                        xrow = b * T + (qb * 4 + st_i) * 128
                        P.add('y', lambda e, xrow=xrow: e.dma_start(
                            x2_b[0][:, :], x_d[xrow:xrow + 128, :]),
                            wait=[(sV, haz_x2[0])], inc=dX2, inc_n=16)
                        x2_ld = dX2.n
                        for dc in range(4):
                            half = dc % 2
                            for n in range(16):
                                P.add('t', lambda e, n=n, st_i=st_i, dc=dc, half=half: e.matmul(
                                    PS[:, half * 512:(half + 1) * 512],
                                    encT[:, n * 512 + st_i * 128: n * 512 + (st_i + 1) * 128],
                                    wo_sb[:, n * 2048 + dc * 512: n * 2048 + (dc + 1) * 512],
                                    start=(n == 0), stop=(n == 15)),
                                    wait=([(sV, enc_last), (dGWT, wo_ld), (sV, ps_op_free[half]),
                                           (sS, ps_L_free[0]), (sS, ps_L_free[1])]
                                          if n == 0 else []),
                                    inc=(sT if n == 15 else None))
                            t_op = sT.n
                            pa = dc % 2
                            P.add('v', lambda e, pa=pa, dc=dc, half=half: e.tensor_tensor(
                                at_b[pa][:, :],
                                PS[:, half * 512:(half + 1) * 512],
                                x2_b[0][:, dc * 512:(dc + 1) * 512], op=ALU.add),
                                wait=[(sT, t_op), (dATS2[pa], haz_at[pa]), (dX2, x2_ld)], inc=sV)
                            ps_op_free[half] = sV.n
                            s_at = sV.n
                            arow = (b * NQB + qb) * 128
                            P.add('y', lambda e, arow=arow, st_i=st_i, dc=dc, pa=pa: e.dma_start(
                                attn_d[arow:arow + 128, st_i * 2048 + dc * 512: st_i * 2048 + (dc + 1) * 512],
                                at_b[pa][:, :]),
                                wait=[(sV, s_at)], inc=dATS2[pa], inc_n=16)
                            haz_at[pa] = dATS2[pa].n
                        haz_encT = t_op
                        haz_x2[0] = sV.n
                        quant_step(1)
                    haz_qT_read = sT.n
                haz_kv_read = sT.n
            quant_step(NQH + 6)

            barrier()

        with ExitStack() as st:
          if '3' in PHASES:
            xnT_l = [st.enter_context(nc.sbuf_tensor(f"xnTl{p}", [128, 8192], BF16)) for p in range(2)]
            wgu_l = [st.enter_context(nc.sbuf_tensor(f"wgul{p}", [128, 4096], BF16)) for p in range(3)]
            wp_l = [st.enter_context(nc.sbuf_tensor(f"wpl{p}", [128, 4096], BF16)) for p in range(2)]
            hT_sb = st.enter_context(nc.sbuf_tensor("hT_sb", [128, 32768], BF16))
            hsq_b = [st.enter_context(nc.sbuf_tensor(f"hsq{p}", [128, 512], BF16)) for p in range(2)]
            sg_b = [st.enter_context(nc.sbuf_tensor(f"sg{p}", [128, 512], F32)) for p in range(2)]
            at_l = [st.enter_context(nc.sbuf_tensor(f"atl{p}", [128, 512], F32)) for p in range(2)]
            out_sb = st.enter_context(nc.sbuf_tensor("out_sb", [128, 8192], F32))
            rsc_sb = st.enter_context(nc.sbuf_tensor("rsc_sb", [128, 8], F32))

            haz_xnT_rd = [0, 0]
            haz_wgu = [0, 0, 0]
            haz_wp = [0, 0]
            haz_hT = 0
            haz_gate_ps = [0, 0]
            haz_up_ps = [0, 0]
            haz_hsq = [0, 0]
            haz_sg = [0, 0]
            haz_proj_ps = [0, 0]
            haz_out_store = 0
            haz_at_rd = [0, 0]
            haz_rms_ps = 0

            for ci in range(BC * NQB):
                b, qb = ci // NQB, ci % NQB
                pc = ci % 2
                rows = (b * NQB + qb) * 128
                P.add('y', lambda e, rows=rows, pc=pc: e.dma_start(
                    xnT_l[pc][:, :], xnT_d[rows:rows + 128, :]),
                    wait=[(sT, haz_xnT_rd[pc]), (dXNT_S, dXNT_S.n)], inc=dXNT[pc], inc_n=16)
                xnT_ld = dXNT[pc].n
                hT_writes = 0
                v_hsq_by_j = {}
                for j in range(64):
                    pj = j % 2
                    pw = j % 3
                    P.add('y', lambda e, j=j, pw=pw: e.dma_start(
                        wgu_l[pw][:, :].rearrange("p (a n) -> p a n", n=2048),
                        wguT_d[j * 256:(j + 1) * 256, :].rearrange("(a p) n -> p a n", p=128)),
                        wait=[(sT, haz_wgu[pw])] + [(dd, dd.n) for dd in dQE],
                        inc=dWGU[pw], inc_n=16)
                    w_ld2 = dWGU[pw].n
                    for c in range(16):
                        P.add('t', lambda e, c=c, pj=pj, pw=pw, pc=pc: e.matmul(
                            PS[:, pj * 512:(pj + 1) * 512],
                            wgu_l[pw][:, c * 128:(c + 1) * 128],
                            xnT_l[pc][:, c * 512:(c + 1) * 512],
                            start=(c == 0), stop=(c == 15)),
                            wait=(([(dWGU[pw], w_ld2), (dXNT[pc], xnT_ld), (sS, haz_gate_ps[pj])] +
                                   ([(sV, haz_proj_ps[0])] if j == 0 else []))
                                  if c == 0 else []))
                    for c in range(16):
                        P.add('t', lambda e, c=c, pj=pj, pw=pw, pc=pc: e.matmul(
                            PS[:, 1024 + pj * 512: 1024 + (pj + 1) * 512],
                            wgu_l[pw][:, 2048 + c * 128: 2048 + (c + 1) * 128],
                            xnT_l[pc][:, c * 512:(c + 1) * 512],
                            start=(c == 0), stop=(c == 15)),
                            wait=([(sV, haz_up_ps[pj])] if c == 0 else []),
                            inc=(sT if c == 15 else None))
                    t_gu = sT.n
                    haz_wgu[pw] = t_gu
                    haz_xnT_rd[pc] = t_gu
                    P.add('s', lambda e, pj=pj: e.activation(
                        sg_b[pj][:, :], PS[:, pj * 512:(pj + 1) * 512], AF.Silu),
                        wait=[(sT, t_gu), (sV, haz_sg[pj])], inc=sS)
                    s_silu = sS.n
                    haz_gate_ps[pj] = s_silu
                    P.add('v', lambda e, j=j, pj=pj: e.tensor_tensor(
                        hT_sb[:, j * 512:(j + 1) * 512], sg_b[pj][:, :],
                        PS[:, 1024 + pj * 512: 1024 + (pj + 1) * 512], op=ALU.mult),
                        wait=[(sS, s_silu)] + ([(sT, haz_hT)] if j == 0 else []), inc=sV)
                    v_hT = sV.n
                    haz_up_ps[pj] = v_hT
                    haz_sg[pj] = v_hT
                    hT_writes = v_hT
                    P.add('v', lambda e, j=j, pj=pj: e.tensor_tensor(
                        hsq_b[pj][:, :], hT_sb[:, j * 512:(j + 1) * 512],
                        hT_sb[:, j * 512:(j + 1) * 512], op=ALU.mult),
                        wait=[(sV, v_hT), (sT, haz_hsq[pj])], inc=sV)
                    v_hsq_by_j[j] = sV.n
                    for jr in ([j - 1] if j >= 1 else []) + ([j] if j == 63 else []):
                        pjr = jr % 2
                        for st_i in range(4):
                            w0 = []
                            if st_i == 0:
                                w0 = [(sV, v_hsq_by_j[jr])]
                                if jr == 0:
                                    w0 += [(sS, haz_rms_ps), (sV, haz_proj_ps[1])]
                            P.add('t', lambda e, st_i=st_i, pjr=pjr, jr=jr: e.matmul(
                                PS[:, (4 + st_i) * 512: (4 + st_i) * 512 + 1],
                                hsq_b[pjr][:, st_i * 128:(st_i + 1) * 128], ones_col[:, :],
                                start=(jr == 0), stop=(jr == 63)),
                                wait=w0, inc=(sT if st_i == 3 else None))
                        haz_hsq[pjr] = sT.n
                t_rms = sT.n
                P.add('s', lambda e: e.activation(
                    rsc_sb[:, 4:8].rearrange("p (a o) -> p a o", o=1),
                    PS[:, 2048:4096].rearrange("p (a n) -> p a n", n=512)[:, :, 0:1],
                    AF.Ln, scale=1.0 / 8192.0, bias=cst[:, 4:5]),
                    wait=[(sT, t_rms)], inc=sS)
                s1 = sS.n
                P.add('s', lambda e: e.activation(
                    rsc_sb[:, 0:4], rsc_sb[:, 4:8], AF.Exp, scale=-0.5),
                    wait=[(sS, s1)], inc=sS)
                s_rsc = sS.n
                haz_rms_ps = s_rsc
                adds_by_st = [0, 0, 0, 0]
                for q in range(4):
                    ph = q % 2
                    base = ph * 2048
                    for jg in range(8):
                        pw = jg % 2
                        prow = (jg * 4 + q) * 128
                        P.add('y', lambda e, prow=prow, pw=pw: e.dma_start(
                            wp_l[pw][:, :], wpT_d[prow:prow + 128, :]),
                            wait=[(sT, haz_wp[pw]), (dASM, dASM.n)], inc=dWP[pw], inc_n=16)
                        wp_ld = dWP[pw].n
                        for jin in range(8):
                            j = jg * 8 + jin
                            for st_i in range(4):
                                w0 = []
                                if jin == 0 and st_i == 0 and jg == 0:
                                    w0 = [(dWP[pw], wp_ld), (sV, hT_writes), (sS, s_rsc),
                                          (sV, haz_proj_ps[ph])]
                                elif jin == 0 and st_i == 0:
                                    w0 = [(dWP[pw], wp_ld)]
                                P.add('t', lambda e, j=j, jin=jin, st_i=st_i, pw=pw, base=base: e.matmul(
                                    PS[:, base + st_i * 512: base + (st_i + 1) * 512],
                                    hT_sb[:, j * 512 + st_i * 128: j * 512 + (st_i + 1) * 128],
                                    wp_l[pw][:, jin * 512:(jin + 1) * 512],
                                    start=(j == 0), stop=(j == 63)),
                                    wait=w0,
                                    inc=(sT if (jin == 7 and st_i == 3) else None))
                        haz_wp[pw] = sT.n
                    t_q = sT.n
                    if q >= 2:
                        haz_hT = t_q
                    for st_i in range(4):
                        pa = (q * 4 + st_i) % 2
                        arow = (b * NQB + qb) * 128
                        P.add('y', lambda e, arow=arow, st_i=st_i, q=q, pa=pa: e.dma_start(
                            at_l[pa][:, :],
                            attn_d[arow:arow + 128, st_i * 2048 + q * 512: st_i * 2048 + (q + 1) * 512]),
                            wait=[(sV, haz_at_rd[pa]), (dAT_S, dAT_S.n)], inc=dATL[pa], inc_n=16)
                        at_ld = dATL[pa].n
                        P.add('v', lambda e, st_i=st_i, q=q, base=base, pa=pa: e.scalar_tensor_tensor(
                            out_sb[:, st_i * 2048 + q * 512: st_i * 2048 + (q + 1) * 512],
                            PS[:, base + st_i * 512: base + (st_i + 1) * 512],
                            rsc_sb[:, st_i:st_i + 1], at_l[pa][:, :],
                            op0=ALU.mult, op1=ALU.add),
                            wait=[(sT, t_q), (dATL[pa], at_ld), (sS, s_rsc), (dOUT, haz_out_store)],
                            inc=sV)
                        adds_by_st[st_i] = sV.n
                        haz_at_rd[pa] = sV.n
                        haz_proj_ps[ph] = sV.n
                for st_i in range(4):
                    yrow = b * T + (qb * 4 + st_i) * 128
                    P.add('y', lambda e, yrow=yrow, st_i=st_i: e.dma_start(
                        y_d[yrow:yrow + 128, :], out_sb[:, st_i * 2048:(st_i + 1) * 2048]),
                        wait=[(sV, adds_by_st[st_i])], inc=dOUT, inc_n=16)
                haz_out_store = dOUT.n

        P.raw('y', lambda e, v=dOUT.n, h=dOUT.h: e.wait_ge(h, v))

        with nc.Block() as block:
            @block.tensor
            def _(e):
                for f in P.ops['t']:
                    f(e)

            @block.vector
            def _(e):
                for f in P.ops['v']:
                    f(e)

            @block.scalar
            def _(e):
                for f in P.ops['s']:
                    f(e)

            @block.gpsimd
            def _(e):
                for f in P.ops['g']:
                    f(e)

            @block.sync
            def _(e):
                for f in P.ops['y']:
                    f(e)

    return nc


def _sincos_table():
    fraction = np.arange(0, 128, 2, dtype=np.float64) / 128.0
    inv_freq = 1.0 / (10000.0 ** fraction)
    t = np.arange(T, dtype=np.float64)[:, None]
    ang = t * inv_freq[None, :]
    return np.concatenate([np.sin(ang), np.cos(ang)], axis=1).astype(np.float32)


def kernel(L0_attn_x, L0_attn_q_kernel, L0_attn_k_kernel, L0_attn_v_kernel,
           L0_attn_out_kernel, L0_ffn_w_gate_up, L0_ffn_w_proj):
    x = np.ascontiguousarray(np.asarray(L0_attn_x, dtype=np.float32))
    wq = np.ascontiguousarray(np.asarray(L0_attn_q_kernel, dtype=np.float32))
    wk = np.ascontiguousarray(np.asarray(L0_attn_k_kernel, dtype=np.float32))
    wv = np.ascontiguousarray(np.asarray(L0_attn_v_kernel, dtype=np.float32))
    wo = np.ascontiguousarray(np.asarray(L0_attn_out_kernel, dtype=np.float32))
    wgu = np.ascontiguousarray(np.asarray(L0_ffn_w_gate_up, dtype=np.float32))
    wp = np.ascontiguousarray(np.asarray(L0_ffn_w_proj, dtype=np.float32))
    sincos = _sincos_table()
    eye = np.eye(128, dtype=np.float32).astype(bfloat16)

    nc = build_nc()
    in_maps = []
    for c in range(N_CORES):
        in_maps.append(dict(
            x=np.ascontiguousarray(x[c * BC:(c + 1) * BC].reshape(BC * T, D)),
            wq=wq, wk=wk, wv=wv, wo=wo, wgu=wgu, wp=wp,
            sincos=sincos, eye=eye))
    res = run_bass_kernel_spmd(nc, in_maps, core_ids=list(range(N_CORES)))
    out = np.concatenate([np.asarray(r["y"]).reshape(BC, T, D) for r in res.results], axis=0)
    return out


if __name__ == "__main__":
    import time
    t0 = time.time()
    nc = build_nc()
    n = sum(1 for _ in nc.all_instructions())
    print(f"build ok in {time.time()-t0:.1f}s, {n} instructions")


# version 27
# speedup vs baseline: 1.1517x; 1.0496x over previous
import numpy as np
import ml_dtypes
from contextlib import ExitStack

import concourse.bass as bass
import concourse.mybir as mybir
from concourse.bass_utils import run_bass_kernel_spmd

F32 = mybir.dt.float32
BF16 = mybir.dt.bfloat16
AF = mybir.ActivationFunctionType
ALU = mybir.AluOpType
AX = mybir.AxisListType
bfloat16 = ml_dtypes.bfloat16

DEBUG = False
PHASES = "Q123"
N_CORES = 8
BC = 2
T = 2048
D = 2048
NT = 16
NQB = 4
LN_EPS = 1e-5
RMS_EPS = 1e-6
FFN_EPS = float(np.finfo(np.float32).eps)
LOG_INV_SQRT_H = float(np.log(1.0 / np.sqrt(128.0)))
SOFTMAX_SHIFT = -12.0


class Sem:
    def __init__(self, st, nc, name):
        self.h = st.enter_context(nc.semaphore(name))
        self.n = 0

    def inc(self, k=1):
        self.n += k
        return self.n


class Plan:
    def __init__(self):
        self.ops = {k: [] for k in "tvsgy"}

    def add(self, eng, builder, wait=(), inc=None, inc_n=1):
        if inc is not None:
            inc.inc(inc_n)
        wl = [(s.h, v) for (s, v) in wait if v > 0]
        ih = inc.h if inc is not None else None

        def run(e, wl=wl, ih=ih, inc_n=inc_n, builder=builder):
            for h, v in wl:
                e.wait_ge(h, v)
            ins = builder(e)
            if ih is not None:
                ins.then_inc(ih, inc_n)
        self.ops[eng].append(run)

    def raw(self, eng, fn):
        self.ops[eng].append(fn)


def build_nc():
    nc = bass.Bass("TRN2", target_bir_lowering=False)
    kind_scratch = "ExternalOutput" if DEBUG else "Internal"

    x_d = nc.dram_tensor("x", [BC * T, D], F32, kind="ExternalInput").ap()
    wq_d = nc.dram_tensor("wq", [D, 2048], F32, kind="ExternalInput").ap()
    wk_d = nc.dram_tensor("wk", [D, 512], F32, kind="ExternalInput").ap()
    wv_d = nc.dram_tensor("wv", [D, 512], F32, kind="ExternalInput").ap()
    wo_d = nc.dram_tensor("wo", [2048, D], F32, kind="ExternalInput").ap()
    wgu_d = nc.dram_tensor("wgu", [16384, D], F32, kind="ExternalInput").ap()
    wp_d = nc.dram_tensor("wp", [D, 8192], F32, kind="ExternalInput").ap()
    sc_d = nc.dram_tensor("sincos", [T, 128], F32, kind="ExternalInput").ap()
    eye_d = nc.dram_tensor("eye", [128, 128], BF16, kind="ExternalInput").ap()
    y_d = nc.dram_tensor("y", [BC * T, D], F32, kind="ExternalOutput").ap()

    wguT_d = nc.dram_tensor("wguT_s", [128 * 128, 2048], BF16, kind=kind_scratch).ap()
    wpT_d = nc.dram_tensor("wpT_s", [32 * 128, 4096], BF16, kind=kind_scratch).ap()
    xnT_d = nc.dram_tensor("xnT_s", [BC * NQB * 128, 8192], BF16, kind=kind_scratch).ap()
    qT_d = nc.dram_tensor("qT_s", [BC * NQB * 128, 8192], BF16, kind=kind_scratch).ap()
    kT_d = nc.dram_tensor("kT_s", [BC * 128, 8192], BF16, kind=kind_scratch).ap()
    v_d = nc.dram_tensor("v_s", [BC * 128, 8192], BF16, kind=kind_scratch).ap()
    attn_d = nc.dram_tensor("attn_s", [BC * NQB * 128, 8192], F32, kind=kind_scratch).ap()

    P = Plan()
    st_top = ExitStack()
    with st_top:
        sT = Sem(st_top, nc, "sT")
        sV = Sem(st_top, nc, "sV")
        sS = Sem(st_top, nc, "sS")
        sG = Sem(st_top, nc, "sG")
        sB = Sem(st_top, nc, "sB")
        dEye = Sem(st_top, nc, "dEye")
        dGW = [Sem(st_top, nc, f"dGW{i}") for i in range(6)]
        dGWT = Sem(st_top, nc, "dGWT")
        dQE = [Sem(st_top, nc, f"dQE{i}") for i in range(6)]
        dASM = Sem(st_top, nc, "dASM")
        dSC = Sem(st_top, nc, "dSC")
        dX = [Sem(st_top, nc, f"dX{i}") for i in range(2)]
        dXNT_S = Sem(st_top, nc, "dXNT_S")
        dQT_S = Sem(st_top, nc, "dQT_S")
        dKST = [Sem(st_top, nc, f"dKST{i}") for i in range(2)]
        dVST = [Sem(st_top, nc, f"dVST{i}") for i in range(2)]
        dKV = Sem(st_top, nc, "dKV")
        dQTL = [Sem(st_top, nc, f"dQTL{i}") for i in range(2)]
        dX2 = Sem(st_top, nc, "dX2")
        dAT_S = Sem(st_top, nc, "dAT_S")
        dATS2 = [Sem(st_top, nc, f"dATS2{i}") for i in range(2)]
        dXNT = [Sem(st_top, nc, f"dXNT{i}") for i in range(2)]
        dWGU = [Sem(st_top, nc, f"dWGU{i}") for i in range(3)]
        dWP = [Sem(st_top, nc, f"dWP{i}") for i in range(2)]
        dATL = [Sem(st_top, nc, f"dATL{i}") for i in range(2)]
        dOUT = Sem(st_top, nc, "dOUT")

        PS = st_top.enter_context(nc.psum_tensor("PS", [128, 4096], F32))
        eye = st_top.enter_context(nc.sbuf_tensor("eye_sb", [128, 128], BF16))
        ones_col = st_top.enter_context(nc.sbuf_tensor("ones_col", [128, 1], BF16))
        ones_row = st_top.enter_context(nc.sbuf_tensor("ones_row", [1, 128], BF16))
        ones128 = st_top.enter_context(nc.sbuf_tensor("ones128", [128, 128], BF16))

        cst = st_top.enter_context(nc.sbuf_tensor("cst", [128, 8], F32))
        P.add('y', lambda e: e.dma_start(eye[:, :], eye_d), inc=dEye, inc_n=16)
        eye_done = dEye.n
        P.add('v', lambda e: e.memset(ones_col[:, :], 1.0), inc=sV)
        P.add('v', lambda e: e.memset(ones_row[:, :], 1.0), inc=sV)
        P.add('v', lambda e: e.memset(cst[:, 0:1], LN_EPS), inc=sV)
        P.add('v', lambda e: e.memset(cst[:, 1:2], RMS_EPS), inc=sV)
        P.add('v', lambda e: e.memset(cst[:, 2:3], LOG_INV_SQRT_H), inc=sV)
        P.add('v', lambda e: e.memset(cst[:, 3:4], SOFTMAX_SHIFT), inc=sV)
        P.add('v', lambda e: e.memset(cst[:, 4:5], FFN_EPS), inc=sV)
        P.add('v', lambda e: e.memset(ones128[:, :], 1.0), inc=sV)
        ones_done = sV.n
        P.raw('s', lambda e, sh=sV.h, v=ones_done: e.wait_ge(sh, v))

        ALL_DMA_SEMS = ([dEye, dGWT, dASM, dSC, dXNT_S, dQT_S, dKV, dX2, dAT_S, dOUT]
                        + dATS2
                        + dGW + dQE + dX + dKST + dVST + dQTL + dXNT + dWGU + dWP + dATL)

        def barrier():
            for d in ALL_DMA_SEMS:
                if d.n > 0:
                    P.raw('y', lambda e, h=d.h, v=d.n: e.wait_ge(h, v))
            tgt = sB.n + 5
            for eng in "tvsgy":
                P.raw(eng, lambda e, s=sB.h: e.sem_inc(s, 1))
            sB.n = tgt
            for eng in "tvsgy":
                P.raw(eng, lambda e, s=sB.h, t=tgt: e.wait_ge(s, t))

        with ExitStack() as st:
          if '1' in PHASES:
            wq_sb = st.enter_context(nc.sbuf_tensor("wq_sb", [128, 16 * 2048], BF16))
            wk_sb = st.enter_context(nc.sbuf_tensor("wk_sb", [128, 16 * 512], BF16))
            wv_sb = st.enter_context(nc.sbuf_tensor("wv_sb", [128, 16 * 512], BF16))
            sc_sb = st.enter_context(nc.sbuf_tensor("sc_sb", [128, 2048], F32))
            x_b = [st.enter_context(nc.sbuf_tensor(f"x{p}", [128, 2048], F32)) for p in range(2)]
            xn_b = [st.enter_context(nc.sbuf_tensor(f"xn{p}", [128, 2048], BF16)) for p in range(2)]
            qf_b = [st.enter_context(nc.sbuf_tensor(f"qf{p}", [128, 2048], F32)) for p in range(2)]
            qr_b = [st.enter_context(nc.sbuf_tensor(f"p1qr{p}", [128, 2048], BF16)) for p in range(2)]
            kf_b = [st.enter_context(nc.sbuf_tensor(f"kf{p}", [128, 512], F32)) for p in range(2)]
            kr_b = [st.enter_context(nc.sbuf_tensor(f"kr{p}", [128, 512], BF16)) for p in range(2)]
            ra = st.enter_context(nc.sbuf_tensor("ra", [128, 1024], F32))
            rb = st.enter_context(nc.sbuf_tensor("rb", [128, 1024], F32))
            junk = st.enter_context(nc.sbuf_tensor("junk", [128, 2048], BF16))
            sm_b = [st.enter_context(nc.sbuf_tensor(f"sm{p}", [128, 8], F32)) for p in range(2)]
            xnT_qb = st.enter_context(nc.sbuf_tensor("xnT_qb", [128, 8192], BF16))
            qT_qb = st.enter_context(nc.sbuf_tensor("qT_qb", [128, 8192], BF16))
            kst_b = [st.enter_context(nc.sbuf_tensor(f"kst{p}", [128, 512], BF16)) for p in range(2)]
            vst_b = [st.enter_context(nc.sbuf_tensor(f"vst{p}", [128, 512], BF16)) for p in range(2)]

            P.add('g', lambda e: e.dma_start(
                wq_sb[:, :].rearrange("p (c n) -> p c n", n=2048),
                wq_d.rearrange("(c p) n -> p c n", p=128)), inc=dGWT, inc_n=16)
            P.add('g', lambda e: e.dma_start(
                wk_sb[:, :].rearrange("p (c n) -> p c n", n=512),
                wk_d.rearrange("(c p) n -> p c n", p=128)), inc=dGWT, inc_n=16)
            P.add('g', lambda e: e.dma_start(
                wv_sb[:, :].rearrange("p (c n) -> p c n", n=512),
                wv_d.rearrange("(c p) n -> p c n", p=128)), inc=dGWT, inc_n=16)
            w_ld = dGWT.n
            for t in range(NT):
                P.add('y', lambda e, t=t: e.dma_start(
                    sc_sb[:, t * 128:(t + 1) * 128], sc_d[t * 128:(t + 1) * 128, :]),
                    inc=dSC, inc_n=16)
            sc_ld = dSC.n

            haz_x = [0, 0]
            haz_xsq = [0, 0]
            haz_xn = [0, 0]
            haz_qf = [0, 0]
            haz_kf = [0, 0]
            haz_sm = [0, 0]
            haz_qr = [0, 0]
            haz_kr = [0, 0]
            haz_xnT_store = 0
            haz_qT_store = 0
            haz_kst = [0, 0]
            haz_vst = [0, 0]
            ps03_free = 0
            ps_q_free = [0, 0]
            ps_k_free = 0
            ps_v_free = 0
            CC = {}

            NTILES = BC * NT

            def stage_a_load(i):
                p = i % 2
                b, t = i // NT, i % NT
                row0 = b * T + t * 128
                P.add('y', lambda e, x_=x_b[p], row0=row0: e.dma_start(
                    x_[:, :], x_d[row0:row0 + 128, :]),
                    wait=[(sV, haz_x[p]), (sS, haz_xsq[p])], inc=dX[p], inc_n=16)
                CC[('xld', i)] = dX[p].n

            def stage_a_ln(i):
                p = i % 2
                x_, xn, sm = x_b[p], xn_b[p], sm_b[p]
                x_ld = CC[('xld', i)]
                P.add('v', lambda e, sm=sm, x_=x_: e.reduce_sum(
                    sm[:, 0:1], x_[:, :], axis=AX.X),
                    wait=[(dX[p], x_ld)], inc=sV)
                v_sum = sV.n
                P.add('v', lambda e, sm=sm: e.tensor_single_scalar(
                    sm[:, 1:2], sm[:, 0:1], -1.0 / 2048.0, op=ALU.mult),
                    wait=[(sV, v_sum)], inc=sV)
                v_mean = sV.n
                P.add('s', lambda e, x_=x_, sm=sm: e.activation(
                    junk[:, :], x_[:, :], AF.Square, bias=sm[:, 1:2],
                    accum_out=sm[:, 2:3]),
                    wait=[(sV, v_mean)], inc=sS)
                s_ssq = sS.n
                haz_xsq[p] = s_ssq
                P.add('s', lambda e, sm=sm: e.activation(
                    sm[:, 3:4], sm[:, 2:3], AF.Ln, scale=1.0 / 2048.0, bias=cst[:, 0:1]),
                    wait=[(sS, s_ssq)], inc=sS)
                s_lnv = sS.n
                P.add('s', lambda e, sm=sm: e.activation(
                    sm[:, 4:5], sm[:, 3:4], AF.Exp, scale=-0.5),
                    wait=[(sS, s_lnv)], inc=sS)
                CC[('rstd', i)] = sS.n

            def stage_a_xn(i):
                p = i % 2
                x_, xn, sm = x_b[p], xn_b[p], sm_b[p]
                P.add('v', lambda e, xn=xn, x_=x_, sm=sm: e.tensor_scalar(
                    xn[:, :], x_[:, :], sm[:, 1:2], sm[:, 4:5],
                    op0=ALU.add, op1=ALU.mult),
                    wait=[(sS, CC[('rstd', i)]), (sT, haz_xn[p])], inc=sV)
                v_xn = sV.n
                haz_x[p] = v_xn
                CC[('vxn', i)] = v_xn

            def stage_a_tr(i):
                nonlocal ps03_free, haz_xnT_store
                p = i % 2
                b, t = i // NT, i % NT
                qb, t4 = t // 4, t % 4
                xn = xn_b[p]
                for c in range(16):
                    P.add('t', lambda e, xn=xn, c=c: e.matmul(
                        PS[:, c * 128:(c + 1) * 128], xn[:, c * 128:(c + 1) * 128],
                        eye[:, :], start=True, stop=True),
                        wait=([(sV, CC[('vxn', i)]), (sS, ps03_free), (dEye, eye_done)]
                              if c == 0 else []),
                        inc=(sT if c == 15 else None))
                t_xntr = sT.n
                haz_xn[p] = t_xntr
                P.add('s', lambda e, t4=t4: e.copy(
                    xnT_qb[:, :].rearrange("p (c n) -> p c n", n=512)[:, :, t4 * 128:(t4 + 1) * 128],
                    PS[:, 0:2048].rearrange("p (c n) -> p c n", n=128)),
                    wait=[(sT, t_xntr), (dXNT_S, haz_xnT_store)], inc=sS)
                s_xnT = sS.n
                ps03_free = s_xnT
                CC[('sxnT', i)] = s_xnT
                if t4 == 3:
                    rows = (b * NQB + qb) * 128
                    P.add('y', lambda e, rows=rows: e.dma_start(
                        xnT_d[rows:rows + 128, :], xnT_qb[:, :]),
                        wait=[(sS, s_xnT)], inc=dXNT_S, inc_n=16)
                    haz_xnT_store = dXNT_S.n

            def stage_b_q(i):
                p = i % 2
                b, t = i // NT, i % NT
                t4 = t % 4
                qf = qf_b[p]
                s_qev = 0
                for n in range(4):
                    bank = 3072 + (n % 2) * 512
                    for c in range(16):
                        P.add('t', lambda e, n=n, c=c, t4=t4, bank=bank: e.matmul(
                            PS[:, bank:bank + 512],
                            xnT_qb[:, c * 512 + t4 * 128: c * 512 + (t4 + 1) * 128],
                            wq_sb[:, c * 2048 + n * 512: c * 2048 + (n + 1) * 512],
                            start=(c == 0), stop=(c == 15)),
                            wait=([(sS, CC[('sxnT', i)]), (sS, ps_q_free[n % 2]), (dGWT, w_ld)]
                                  if c == 0 else []),
                            inc=(sT if c == 15 else None))
                    t_q = sT.n
                    P.add('s', lambda e, qf=qf, n=n, bank=bank: e.copy(
                        qf[:, n * 512:(n + 1) * 512], PS[:, bank:bank + 512]),
                        wait=[(sT, t_q)] + ([(sV, haz_qf[p])] if n == 0 else []), inc=sS)
                    s_qev = sS.n
                    ps_q_free[n % 2] = s_qev
                CC[('sqev', i)] = s_qev

            def stage_b_kv(i):
                nonlocal ps_k_free, ps_v_free
                p = i % 2
                b, t = i // NT, i % NT
                t4 = t % 4
                qf, qr, kf, kr, sm = qf_b[p], qr_b[p], kf_b[p], kr_b[p], sm_b[p]
                for c in range(16):
                    P.add('t', lambda e, c=c, t4=t4: e.matmul(
                        PS[:, 2048:2560],
                        xnT_qb[:, c * 512 + t4 * 128: c * 512 + (t4 + 1) * 128],
                        wk_sb[:, c * 512: (c + 1) * 512],
                        start=(c == 0), stop=(c == 15)),
                        wait=([(sS, ps_k_free)] if c == 0 else []),
                        inc=(sT if c == 15 else None))
                t_k = sT.n
                for c in range(16):
                    P.add('t', lambda e, c=c, t4=t4: e.matmul(
                        PS[:, 2560:3072],
                        xnT_qb[:, c * 512 + t4 * 128: c * 512 + (t4 + 1) * 128],
                        wv_sb[:, c * 512: (c + 1) * 512],
                        start=(c == 0), stop=(c == 15)),
                        wait=([(sS, ps_v_free)] if c == 0 else []),
                        inc=(sT if c == 15 else None))
                t_v = sT.n
                P.add('s', lambda e, qf=qf, sm=sm: e.activation(
                    junk[:, :], qf[:, :], AF.Square, accum_out=sm[:, 5:6]),
                    wait=[(sS, CC[('sqev', i)]), (sV, haz_sm[p])], inc=sS)
                s1 = sS.n
                P.add('s', lambda e, sm=sm: e.activation(
                    sm[:, 6:7], sm[:, 5:6], AF.Ln, scale=1.0 / 2048.0, bias=cst[:, 1:2]),
                    wait=[(sS, s1)], inc=sS)
                s2 = sS.n
                P.add('s', lambda e, sm=sm: e.activation(
                    sm[:, 5:6], sm[:, 6:7], AF.Exp, scale=-0.5, bias=cst[:, 2:3]),
                    wait=[(sS, s2)], inc=sS)
                s_qrs = sS.n
                P.add('s', lambda e, kf=kf: e.copy(kf[:, :], PS[:, 2048:2560]),
                      wait=[(sT, t_k), (sV, haz_kf[p])], inc=sS)
                s_kev = sS.n
                ps_k_free = s_kev
                P.add('s', lambda e, kf=kf, sm=sm: e.activation(
                    junk[:, 0:512], kf[:, :], AF.Square, accum_out=sm[:, 6:7]),
                    wait=[(sS, s_kev)], inc=sS)
                s1 = sS.n
                P.add('s', lambda e, sm=sm: e.activation(
                    sm[:, 7:8], sm[:, 6:7], AF.Ln, scale=1.0 / 512.0, bias=cst[:, 1:2]),
                    wait=[(sS, s1)], inc=sS)
                s2 = sS.n
                P.add('s', lambda e, sm=sm: e.activation(
                    sm[:, 6:7], sm[:, 7:8], AF.Exp, scale=-0.5),
                    wait=[(sS, s2)], inc=sS)
                s_krs = sS.n
                P.add('s', lambda e, vst=vst_b[p]: e.copy(vst[:, :], PS[:, 2560:3072]),
                      wait=[(sT, t_v), (dVST[p], haz_vst[p])], inc=sS)
                s_vev = sS.n
                ps_v_free = s_vev
                P.add('y', lambda e, vst=vst_b[p], b=b, t=t: e.dma_start(
                    v_d[b * 128:(b + 1) * 128, t * 512:(t + 1) * 512], vst[:, :]),
                    wait=[(sS, s_vev)], inc=dVST[p], inc_n=16)
                haz_vst[p] = dVST[p].n
                sin16 = sc_sb[:, t * 128: t * 128 + 64].rearrange(
                    "p (o k) -> p o k", o=1).broadcast_to([128, 16, 64])
                cos16 = sc_sb[:, t * 128 + 64: t * 128 + 128].rearrange(
                    "p (o k) -> p o k", o=1).broadcast_to([128, 16, 64])
                qv = qf[:, :].rearrange("p (h x) -> p h x", x=128)
                qF, qS = qv[:, :, 0:64], qv[:, :, 64:128]
                rav = ra[:, :].rearrange("p (h k) -> p h k", k=64)
                rbv = rb[:, :].rearrange("p (h k) -> p h k", k=64)
                qrv = qr[:, :].rearrange("p (h x) -> p h x", x=128)
                P.add('v', lambda e, rav=rav, qF=qF, sm=sm, cos16=cos16: e.scalar_tensor_tensor(
                    rav, qF, sm[:, 5:6], cos16, op0=ALU.mult, op1=ALU.mult),
                    wait=[(sS, s_qrs), (dSC, sc_ld)], inc=sV)
                v1 = sV.n
                P.add('v', lambda e, rbv=rbv, qS=qS, sm=sm, sin16=sin16: e.scalar_tensor_tensor(
                    rbv, qS, sm[:, 5:6], sin16, op0=ALU.mult, op1=ALU.mult),
                    wait=[(sS, s_qrs)], inc=sV)
                v2 = sV.n
                P.add('v', lambda e, qrv=qrv, rav=rav, rbv=rbv: e.tensor_tensor(
                    qrv[:, :, 0:64], rav, rbv, op=ALU.subtract),
                    wait=[(sV, v1), (sV, v2), (sT, haz_qr[p])], inc=sV)
                v3 = sV.n
                P.add('v', lambda e, rav=rav, qS=qS, sm=sm, cos16=cos16: e.scalar_tensor_tensor(
                    rav, qS, sm[:, 5:6], cos16, op0=ALU.mult, op1=ALU.mult),
                    wait=[(sV, v3)], inc=sV)
                v4 = sV.n
                P.add('v', lambda e, rbv=rbv, qF=qF, sm=sm, sin16=sin16: e.scalar_tensor_tensor(
                    rbv, qF, sm[:, 5:6], sin16, op0=ALU.mult, op1=ALU.mult),
                    wait=[(sV, v3)], inc=sV)
                v5 = sV.n
                P.add('v', lambda e, qrv=qrv, rav=rav, rbv=rbv: e.tensor_tensor(
                    qrv[:, :, 64:128], rav, rbv, op=ALU.add),
                    wait=[(sV, v4), (sV, v5)], inc=sV)
                v_qrope = sV.n
                haz_qf[p] = v_qrope
                sin4 = sc_sb[:, t * 128: t * 128 + 64].rearrange(
                    "p (o k) -> p o k", o=1).broadcast_to([128, 4, 64])
                cos4 = sc_sb[:, t * 128 + 64: t * 128 + 128].rearrange(
                    "p (o k) -> p o k", o=1).broadcast_to([128, 4, 64])
                kvv = kf[:, :].rearrange("p (h x) -> p h x", x=128)
                kF, kS = kvv[:, :, 0:64], kvv[:, :, 64:128]
                rav4 = ra[:, 0:256].rearrange("p (h k) -> p h k", k=64)
                rbv4 = rb[:, 0:256].rearrange("p (h k) -> p h k", k=64)
                krv = kr[:, :].rearrange("p (h x) -> p h x", x=128)
                P.add('v', lambda e, rav4=rav4, kF=kF, sm=sm, cos4=cos4: e.scalar_tensor_tensor(
                    rav4, kF, sm[:, 6:7], cos4, op0=ALU.mult, op1=ALU.mult),
                    wait=[(sS, s_krs), (sV, v_qrope)], inc=sV)
                v1 = sV.n
                P.add('v', lambda e, rbv4=rbv4, kS=kS, sm=sm, sin4=sin4: e.scalar_tensor_tensor(
                    rbv4, kS, sm[:, 6:7], sin4, op0=ALU.mult, op1=ALU.mult),
                    wait=[(sS, s_krs)], inc=sV)
                v2 = sV.n
                P.add('v', lambda e, krv=krv, rav4=rav4, rbv4=rbv4: e.tensor_tensor(
                    krv[:, :, 0:64], rav4, rbv4, op=ALU.subtract),
                    wait=[(sV, v1), (sV, v2), (sT, haz_kr[p])], inc=sV)
                v3 = sV.n
                P.add('v', lambda e, rav4=rav4, kS=kS, sm=sm, cos4=cos4: e.scalar_tensor_tensor(
                    rav4, kS, sm[:, 6:7], cos4, op0=ALU.mult, op1=ALU.mult),
                    wait=[(sV, v3)], inc=sV)
                v4 = sV.n
                P.add('v', lambda e, rbv4=rbv4, kF=kF, sm=sm, sin4=sin4: e.scalar_tensor_tensor(
                    rbv4, kF, sm[:, 6:7], sin4, op0=ALU.mult, op1=ALU.mult),
                    wait=[(sV, v3)], inc=sV)
                v5 = sV.n
                P.add('v', lambda e, krv=krv, rav4=rav4, rbv4=rbv4: e.tensor_tensor(
                    krv[:, :, 64:128], rav4, rbv4, op=ALU.add),
                    wait=[(sV, v4), (sV, v5)], inc=sV)
                v_krope = sV.n
                haz_kf[p] = v_krope
                haz_sm[p] = v_krope
                CC[('vqrope', i)] = v_qrope
                CC[('vkrope', i)] = v_krope

            def stage_c(i):
                nonlocal ps03_free, ps_k_free, haz_qT_store
                p = i % 2
                b, t = i // NT, i % NT
                qb, t4 = t // 4, t % 4
                for n in range(16):
                    P.add('t', lambda e, qr=qr_b[p], n=n: e.matmul(
                        PS[:, n * 128:(n + 1) * 128], qr[:, n * 128:(n + 1) * 128],
                        eye[:, :], start=True, stop=True),
                        wait=([(sV, CC[('vqrope', i)]), (sS, ps03_free)] if n == 0 else []),
                        inc=(sT if n == 15 else None))
                t_qtr = sT.n
                haz_qr[p] = t_qtr
                P.add('s', lambda e, t4=t4: e.copy(
                    qT_qb[:, :].rearrange("p (h n) -> p h n", n=512)[:, :, t4 * 128:(t4 + 1) * 128],
                    PS[:, 0:2048].rearrange("p (h n) -> p h n", n=128)),
                    wait=[(sT, t_qtr), (dQT_S, haz_qT_store)], inc=sS)
                s_qT = sS.n
                ps03_free = s_qT
                for n in range(4):
                    P.add('t', lambda e, kr=kr_b[p], n=n: e.matmul(
                        PS[:, 2048 + n * 128: 2048 + (n + 1) * 128],
                        kr[:, n * 128:(n + 1) * 128], eye[:, :], start=True, stop=True),
                        wait=([(sV, CC[('vkrope', i)]), (sS, ps_k_free)] if n == 0 else []),
                        inc=(sT if n == 3 else None))
                t_ktr = sT.n
                haz_kr[p] = t_ktr
                P.add('s', lambda e, kst=kst_b[p]: e.copy(kst[:, :], PS[:, 2048:2560]),
                      wait=[(sT, t_ktr), (dKST[p], haz_kst[p])], inc=sS)
                s_kT = sS.n
                ps_k_free = s_kT
                P.add('y', lambda e, kst=kst_b[p], b=b, t=t: e.dma_start(
                    kT_d[b * 128:(b + 1) * 128, :].rearrange(
                        "p (h n) -> p h n", n=2048)[:, :, t * 128:(t + 1) * 128],
                    kst[:, :].rearrange("p (h n) -> p h n", n=128)),
                    wait=[(sS, s_kT)], inc=dKST[p], inc_n=16)
                haz_kst[p] = dKST[p].n
                if t4 == 3:
                    rows = (b * NQB + qb) * 128
                    P.add('y', lambda e, rows=rows: e.dma_start(
                        qT_d[rows:rows + 128, :], qT_qb[:, :]),
                        wait=[(sS, s_qT)], inc=dQT_S, inc_n=16)
                    haz_qT_store = dQT_S.n

            stage_a_load(0)
            for step in range(NTILES + 2):
                if step + 1 < NTILES:
                    stage_a_load(step + 1)
                if 2 <= step:
                    stage_c(step - 2)
                if step < NTILES:
                    stage_a_ln(step)
                    stage_a_xn(step)
                if 1 <= step <= NTILES:
                    stage_b_q(step - 1)
                    stage_b_kv(step - 1)
                if step < NTILES:
                    stage_a_tr(step)

            barrier()

        with ExitStack() as st:
            wo_sb = st.enter_context(nc.sbuf_tensor("wo_sb", [128, 16 * 2048], BF16))
            kT_sb = st.enter_context(nc.sbuf_tensor("kT_sb", [128, 8192], BF16))
            v_sb = st.enter_context(nc.sbuf_tensor("v_sb", [128, 8192], BF16))
            qT_l = [st.enter_context(nc.sbuf_tensor("qTl0", [128, 8192], BF16))]
            P_sb = st.enter_context(nc.sbuf_tensor("P_sb", [128, 4 * 512], BF16))
            encT = st.enter_context(nc.sbuf_tensor("encT", [128, 8192], BF16))
            rs_f = st.enter_context(nc.sbuf_tensor("rs_f", [1, 512], F32))
            rs_bf = st.enter_context(nc.sbuf_tensor("rs_bf", [1, 512], BF16))
            rsb_sb = st.enter_context(nc.sbuf_tensor("rsb_sb", [128, 512], F32))
            rstmp = st.enter_context(nc.sbuf_tensor("rstmp", [128, 512], F32))
            at_b = [st.enter_context(nc.sbuf_tensor(f"atp{p}", [128, 512], F32)) for p in range(2)]
            x2_b = [st.enter_context(nc.sbuf_tensor("x20", [128, 2048], F32))]
            NPQ = 6
            qw_b = [st.enter_context(nc.sbuf_tensor(f"qw{p}", [128, 512], BF16)) for p in range(NPQ)]
            qaw_b = [st.enter_context(nc.sbuf_tensor(f"qaw{p}", [128, 512], BF16)) for p in range(4)]
            qsg_b = [st.enter_context(nc.sbuf_tensor(f"qsg{p}", [128, 512], BF16)) for p in range(4)]
            qsc_b = [st.enter_context(nc.sbuf_tensor(f"qsc{p}", [2, 512], BF16)) for p in range(2)]
            qsr_b = [st.enter_context(nc.sbuf_tensor(f"qsr{p}", [128, 512], BF16)) for p in range(3)]
            qm_b = [st.enter_context(nc.sbuf_tensor(f"qm{p}", [128, 512], BF16)) for p in range(2)]
            qq_b = [st.enter_context(nc.sbuf_tensor(f"qq{p}", [128, 512], BF16)) for p in range(3)]
            blko = st.enter_context(nc.sbuf_tensor("blko", [128, 2], BF16))
            blkT = st.enter_context(nc.sbuf_tensor("blkT", [2, 128], BF16))

            P.add('g', lambda e: e.dma_start(
                wo_sb[:, :].rearrange("p (c n) -> p c n", n=2048),
                wo_d.rearrange("(c p) n -> p c n", p=128)), inc=dGWT, inc_n=16)
            wo_ld = dGWT.n

            NQH = 768
            QC = {}

            def q_src(i):
                if i < 512:
                    f, hf = i // 2, i % 2
                    j, h = f // 2, f % 2
                    c0 = h * 1024 + hf * 512
                    return wgu_d[j * 128:(j + 1) * 128, c0:c0 + 512]
                f, hf = (i - 512) // 2, i % 2
                c, hs = f // 8, f % 8
                c0 = hs * 1024 + hf * 512
                return wp_d[c * 128:(c + 1) * 128, c0:c0 + 512]

            P.add('v', lambda e: e.memset(blko[0:64, 1:2], 0.0), inc=sV)
            P.add('v', lambda e: e.memset(blko[64:128, 1:2], 1.0), inc=sV)
            P.add('v', lambda e: e.memset(blko[0:64, 0:1], 1.0), inc=sV)
            P.add('v', lambda e: e.memset(blko[64:128, 0:1], 0.0), inc=sV)
            blko_done = sV.n
            P.add('t', lambda e: e.matmul(
                PS[0:2, 3584:3712], blko[:, :], eye[:, :], start=True, stop=True),
                wait=[(sV, blko_done), (dEye, eye_done)], inc=sT)
            t_blkT = sT.n
            P.add('s', lambda e: e.copy(blkT[:, :], PS[0:2, 3584:3712]),
                  wait=[(sT, t_blkT)], inc=sS)
            s_blkT = sS.n

            def qst_load(i):
                p = i % NPQ
                P.add('g', lambda e, w=qw_b[p], src=q_src(i): e.dma_start(w[:, :], src),
                      wait=[(sT, QC.get(('tr', i - NPQ), 0))], inc=dGW[p], inc_n=16)
                QC[('ld', i)] = dGW[p].n

            def qst_tr(i):
                p = i % NPQ
                tc = 2048 + (i % 2) * 512
                for c in range(4):
                    P.add('t', lambda e, w=qw_b[p], c=c, tc=tc: e.matmul(
                        PS[:, tc + c * 128: tc + (c + 1) * 128],
                        w[:, c * 128:(c + 1) * 128], eye[:, :], start=True, stop=True),
                        wait=([(dGW[p], QC[('ld', i)]), (sS, QC.get(('sgn', i - 2), 0)),
                               (dEye, eye_done)] if c == 0 else []),
                        inc=(sT if c == 3 else None))
                QC[('tr', i)] = sT.n

            def qst_as(i):
                tc = 2048 + (i % 2) * 512
                pa = i % 4
                P.add('s', lambda e, aw=qaw_b[pa], tc=tc: e.activation(
                    aw[:, :], PS[:, tc:tc + 512], AF.Abs, scale=2.0),
                    wait=[(sT, QC[('tr', i)]), (sV, QC.get(('m', i - 4), 0))], inc=sS)
                QC[('aw', i)] = sS.n
                P.add('s', lambda e, sg=qsg_b[pa], tc=tc: e.activation(
                    sg[:, :], PS[:, tc:tc + 512], AF.Sign),
                    wait=[(sG, QC.get(('q', i - 4), 0))], inc=sS)
                QC[('sgn', i)] = sS.n

            def qst_gs(i):
                rc = 3072 + (i % 2) * 512
                w0 = [(sS, QC[('aw', i)]), (sV, QC.get(('evr', i - 2), 0))]
                if i == 1:
                    w0.append((sS, s_blkT))
                P.add('t', lambda e, aw=qaw_b[i % 4], rc=rc: e.matmul(
                    PS[0:2, rc:rc + 512], blko[:, :], aw[:, :], start=True, stop=True),
                    wait=w0, inc=sT)
                QC[('gs', i)] = sT.n

            def qst_screp(i):
                rc = 3072 + (i % 2) * 512
                psc, psr = i % 2, i % 3
                P.add('v', lambda e, sc=qsc_b[psc], rc=rc: e.tensor_scalar(
                    sc[:, :], PS[0:2, rc:rc + 512], 1.0 / 128.0, 1e-8,
                    op0=ALU.mult, op1=ALU.max),
                    wait=[(sT, QC[('gs', i)]), (sT, QC.get(('rep', i - 2), 0))], inc=sV)
                QC[('scv', i)] = sV.n
                w1 = [(sV, QC[('scv', i)])]
                if i == 0:
                    w1.append((sS, s_blkT))
                P.add('t', lambda e, sc=qsc_b[psc], rc=rc: e.matmul(
                    PS[:, rc:rc + 512], blkT[:, :], sc[:, :], start=True, stop=True),
                    wait=w1, inc=sT)
                QC[('rep', i)] = sT.n
                P.add('v', lambda e, sr=qsr_b[psr], rc=rc: e.tensor_copy(
                    sr[:, :], PS[:, rc:rc + 512]),
                    wait=[(sT, QC[('rep', i)]), (sV, QC.get(('wh', i - 3), 0))], inc=sV)
                QC[('evr', i)] = sV.n

            def qst_mqw(i):
                pm, pa, psr, pq = i % 2, i % 4, i % 3, i % 3
                P.add('v', lambda e, m=qm_b[pm], aw=qaw_b[pa], sr=qsr_b[psr]: e.tensor_tensor(
                    m[:, :], aw[:, :], sr[:, :], op=ALU.is_gt),
                    wait=[(sV, QC[('evr', i)]), (sV, QC.get(('wh', i - 2), 0))], inc=sV)
                QC[('m', i)] = sV.n
                P.add('g', lambda e, m=qm_b[pm], sg=qsg_b[pa]: e.tensor_tensor(
                    m[:, :], m[:, :], sg[:, :], op=ALU.mult),
                    wait=[(sV, QC[('m', i)]), (sS, QC[('sgn', i)])], inc=sG)
                QC[('q', i)] = sG.n
                P.add('v', lambda e, q=qq_b[pq], m=qm_b[pm], sr=qsr_b[psr]: e.tensor_tensor(
                    q[:, :], m[:, :], sr[:, :], op=ALU.mult),
                    wait=[(sG, QC[('q', i)]), (dQE[pq], QC.get(('st', i - 3), 0))], inc=sV)
                QC[('wh', i)] = sV.n

            def qst_fin(i):
                pq = i % 3
                if i < 512:
                    f, hf = i // 2, i % 2
                    j, h = f // 2, f % 2
                    jj, aa = (j, 0) if j < 64 else (j - 64, 1)
                    r = (jj * 2 + aa) * 128
                    c0 = h * 1024 + hf * 512
                    P.add('y', lambda e, q=qq_b[pq], r=r, c0=c0: e.dma_start(
                        wguT_d[r:r + 128, c0:c0 + 512], q[:, :]),
                        wait=[(sV, QC[('wh', i)])], inc=dQE[pq], inc_n=16)
                else:
                    f, hf = (i - 512) // 2, i % 2
                    c, hs = f // 8, f % 8
                    for hl in range(4):
                        hh = hf * 4 + hl
                        row = (hs * 4 + (c // 4)) * 128
                        col = hh * 512 + (c % 4) * 128
                        P.add('y', lambda e, q=qq_b[pq], hl=hl, row=row, col=col: e.dma_start(
                            wpT_d[row:row + 128, col:col + 128],
                            q[:, hl * 128:(hl + 1) * 128]),
                            wait=([(sV, QC[('wh', i)])] if hl == 0 else []),
                            inc=dQE[pq], inc_n=16)
                QC[('st', i)] = dQE[pq].n

            q_step = [0]

            def quant_step(n=1):
                for _ in range(n):
                    k = q_step[0]
                    if k >= NQH + 6:
                        return
                    q_step[0] += 1
                    if 4 <= k < NQH + 4:
                        qst_screp(k - 4)
                    if 2 <= k < NQH + 2:
                        qst_as(k - 2)
                    if 5 <= k < NQH + 5:
                        qst_mqw(k - 5)
                    if 6 <= k < NQH + 6:
                        qst_fin(k - 6)
                    if 3 <= k < NQH + 3:
                        qst_gs(k - 3)
                    if 1 <= k < NQH + 1:
                        qst_tr(k - 1)
                    if k < NQH:
                        qst_load(k)

            haz_kv_read = 0
            haz_qT_read = 0
            haz_P = {}
            haz_at = [0, 0]
            haz_x2 = [0]
            ps_L_free = [0, 0]
            ps_PV_free = 0
            ps_sums_free = 0
            ps_op_free = [0, 0]
            haz_encT = 0

            for b in range(BC):
                P.add('y', lambda e, b=b: e.dma_start(kT_sb[:, :], kT_d[b * 128:(b + 1) * 128, :]),
                      wait=[(sT, haz_kv_read)] + [(dd, dd.n) for dd in dKST + dVST],
                      inc=dKV, inc_n=16)
                P.add('y', lambda e, b=b: e.dma_start(v_sb[:, :], v_d[b * 128:(b + 1) * 128, :]),
                      inc=dKV, inc_n=16)
                kv_ld = dKV.n
                for qb in range(NQB):
                    rows = (b * NQB + qb) * 128
                    P.add('y', lambda e, rows=rows: e.dma_start(
                        qT_l[0][:, :], qT_d[rows:rows + 128, :]),
                        wait=[(sT, haz_qT_read), (dQT_S, dQT_S.n)], inc=dQTL[0], inc_n=16)
                    q_ld = dQTL[0].n
                    enc_last = 0
                    for u in range(16):
                        n_h = u
                        kvh = u // 4
                        exp_done = {}
                        for sc in range(18):
                            half = sc % 2
                            col = half * 512
                            if sc < 16:
                                if sc < 2:
                                    w0 = [(dKV, kv_ld), (dQTL[0], q_ld), (sS, ps_L_free[half]),
                                          (sT, haz_P.get(sc % 4, 0))]
                                else:
                                    w0 = [(sS, exp_done[sc - 2])]
                                P.add('t', lambda e, kvh=kvh, sc=sc, col=col, n_h=n_h: e.matmul(
                                    PS[:, col:col + 512],
                                    kT_sb[:, kvh * 2048 + sc * 128: kvh * 2048 + (sc + 1) * 128],
                                    qT_l[0][:, n_h * 512:(n_h + 1) * 512], start=True, stop=True),
                                    wait=w0, inc=sT)
                                t_L = sT.n
                                P.add('s', lambda e, sc=sc, col=col: e.activation(
                                    P_sb[:, (sc % 4) * 512:(sc % 4 + 1) * 512], PS[:, col:col + 512],
                                    AF.Exp, bias=cst[:, 3:4]),
                                    wait=[(sT, t_L), (sT, haz_P.get(sc % 4, 0))], inc=sS)
                                exp_done[sc] = sS.n
                                ps_L_free[half] = sS.n
                            if sc >= 2:
                                pv = sc - 2
                                P.add('t', lambda e, pv=pv, kvh=kvh: e.matmul(
                                    PS[:, 1024:1536],
                                    v_sb[:, pv * 512 + kvh * 128: pv * 512 + (kvh + 1) * 128],
                                    P_sb[:, (pv % 4) * 512:(pv % 4 + 1) * 512],
                                    start=(pv == 0), stop=(pv == 15)),
                                    wait=([(sS, exp_done[pv])] +
                                          ([(sV, ps_PV_free)] if pv == 0 else [])))
                                P.add('t', lambda e, pv=pv: e.matmul(
                                    PS[:, 1536:2048], ones128[:, :],
                                    P_sb[:, (pv % 4) * 512:(pv % 4 + 1) * 512],
                                    start=(pv == 0), stop=(pv == 15)),
                                    wait=([(sV, ones_done), (sS, ps_sums_free)] if pv == 0 else []),
                                    inc=sT)
                                haz_P[pv % 4] = sT.n
                            if sc % 3 == 2:
                                quant_step(1)
                        t_unit = sT.n
                        P.add('s', lambda e: e.activation(
                            rstmp[:, :], PS[:, 1536:2048], AF.Ln),
                            wait=[(sT, t_unit)], inc=sS)
                        s_ln = sS.n
                        ps_sums_free = s_ln
                        P.add('s', lambda e: e.activation(
                            rsb_sb[:, :], rstmp[:, :], AF.Exp, scale=-1.0),
                            wait=[(sS, s_ln), (sV, ps_PV_free)], inc=sS)
                        s_rs = sS.n
                        P.add('v', lambda e, n_h=n_h: e.tensor_tensor(
                            encT[:, n_h * 512:(n_h + 1) * 512], PS[:, 1024:1536],
                            rsb_sb[:, :], op=ALU.mult),
                            wait=[(sS, s_rs), (sT, haz_encT)], inc=sV)
                        ps_PV_free = sV.n
                        enc_last = sV.n
                    for st_i in range(4):
                        xrow = b * T + (qb * 4 + st_i) * 128
                        P.add('y', lambda e, xrow=xrow: e.dma_start(
                            x2_b[0][:, :], x_d[xrow:xrow + 128, :]),
                            wait=[(sV, haz_x2[0])], inc=dX2, inc_n=16)
                        x2_ld = dX2.n
                        for dc in range(4):
                            half = dc % 2
                            for n in range(16):
                                P.add('t', lambda e, n=n, st_i=st_i, dc=dc, half=half: e.matmul(
                                    PS[:, half * 512:(half + 1) * 512],
                                    encT[:, n * 512 + st_i * 128: n * 512 + (st_i + 1) * 128],
                                    wo_sb[:, n * 2048 + dc * 512: n * 2048 + (dc + 1) * 512],
                                    start=(n == 0), stop=(n == 15)),
                                    wait=([(sV, enc_last), (dGWT, wo_ld), (sV, ps_op_free[half]),
                                           (sS, ps_L_free[0]), (sS, ps_L_free[1])]
                                          if n == 0 else []),
                                    inc=(sT if n == 15 else None))
                            t_op = sT.n
                            pa = dc % 2
                            P.add('v', lambda e, pa=pa, dc=dc, half=half: e.tensor_tensor(
                                at_b[pa][:, :],
                                PS[:, half * 512:(half + 1) * 512],
                                x2_b[0][:, dc * 512:(dc + 1) * 512], op=ALU.add),
                                wait=[(sT, t_op), (dATS2[pa], haz_at[pa]), (dX2, x2_ld)], inc=sV)
                            ps_op_free[half] = sV.n
                            s_at = sV.n
                            arow = (b * NQB + qb) * 128
                            P.add('y', lambda e, arow=arow, st_i=st_i, dc=dc, pa=pa: e.dma_start(
                                attn_d[arow:arow + 128, st_i * 2048 + dc * 512: st_i * 2048 + (dc + 1) * 512],
                                at_b[pa][:, :]),
                                wait=[(sV, s_at)], inc=dATS2[pa], inc_n=16)
                            haz_at[pa] = dATS2[pa].n
                        haz_encT = t_op
                        haz_x2[0] = sV.n
                        quant_step(1)
                    haz_qT_read = sT.n
                haz_kv_read = sT.n
            quant_step(NQH + 6)

            barrier()

        with ExitStack() as st:
          if '3' in PHASES:
            xnT_l = [st.enter_context(nc.sbuf_tensor(f"xnTl{p}", [128, 8192], BF16)) for p in range(2)]
            wgu_l = [st.enter_context(nc.sbuf_tensor(f"wgul{p}", [128, 4096], BF16)) for p in range(3)]
            wp_l = [st.enter_context(nc.sbuf_tensor(f"wpl{p}", [128, 4096], BF16)) for p in range(2)]
            hT_sb = st.enter_context(nc.sbuf_tensor("hT_sb", [128, 32768], BF16))
            hsq_b = [st.enter_context(nc.sbuf_tensor(f"hsq{p}", [128, 512], BF16)) for p in range(2)]
            sg_b = [st.enter_context(nc.sbuf_tensor(f"sg{p}", [128, 512], F32)) for p in range(2)]
            at_l = [st.enter_context(nc.sbuf_tensor(f"atl{p}", [128, 512], F32)) for p in range(2)]
            out_sb = st.enter_context(nc.sbuf_tensor("out_sb", [128, 8192], F32))
            rsc_sb = st.enter_context(nc.sbuf_tensor("rsc_sb", [128, 8], F32))

            haz_xnT_rd = [0, 0]
            haz_wgu = [0, 0, 0]
            haz_wp = [0, 0]
            haz_hT = 0
            haz_gate_ps = [0, 0]
            haz_up_ps = [0, 0]
            haz_hsq = [0, 0]
            haz_sg = [0, 0]
            haz_proj_ps = [0, 0]
            haz_out_store = 0
            haz_at_rd = [0, 0]
            haz_rms_ps = 0

            for ci in range(BC * NQB):
                b, qb = ci // NQB, ci % NQB
                pc = ci % 2
                rows = (b * NQB + qb) * 128
                P.add('y', lambda e, rows=rows, pc=pc: e.dma_start(
                    xnT_l[pc][:, :], xnT_d[rows:rows + 128, :]),
                    wait=[(sT, haz_xnT_rd[pc]), (dXNT_S, dXNT_S.n)], inc=dXNT[pc], inc_n=16)
                xnT_ld = dXNT[pc].n
                hT_writes = 0
                v_hsq_by_j = {}
                for j in range(64):
                    pj = j % 2
                    pw = j % 3
                    P.add('y', lambda e, j=j, pw=pw: e.dma_start(
                        wgu_l[pw][:, :].rearrange("p (a n) -> p a n", n=2048),
                        wguT_d[j * 256:(j + 1) * 256, :].rearrange("(a p) n -> p a n", p=128)),
                        wait=[(sT, haz_wgu[pw])] + [(dd, dd.n) for dd in dQE],
                        inc=dWGU[pw], inc_n=16)
                    w_ld2 = dWGU[pw].n
                    for c in range(16):
                        P.add('t', lambda e, c=c, pj=pj, pw=pw, pc=pc: e.matmul(
                            PS[:, pj * 512:(pj + 1) * 512],
                            wgu_l[pw][:, c * 128:(c + 1) * 128],
                            xnT_l[pc][:, c * 512:(c + 1) * 512],
                            start=(c == 0), stop=(c == 15)),
                            wait=(([(dWGU[pw], w_ld2), (dXNT[pc], xnT_ld), (sS, haz_gate_ps[pj])] +
                                   ([(sV, haz_proj_ps[0])] if j == 0 else []))
                                  if c == 0 else []))
                    for c in range(16):
                        P.add('t', lambda e, c=c, pj=pj, pw=pw, pc=pc: e.matmul(
                            PS[:, 1024 + pj * 512: 1024 + (pj + 1) * 512],
                            wgu_l[pw][:, 2048 + c * 128: 2048 + (c + 1) * 128],
                            xnT_l[pc][:, c * 512:(c + 1) * 512],
                            start=(c == 0), stop=(c == 15)),
                            wait=([(sV, haz_up_ps[pj])] if c == 0 else []),
                            inc=(sT if c == 15 else None))
                    t_gu = sT.n
                    haz_wgu[pw] = t_gu
                    haz_xnT_rd[pc] = t_gu
                    P.add('s', lambda e, pj=pj: e.activation(
                        sg_b[pj][:, :], PS[:, pj * 512:(pj + 1) * 512], AF.Silu),
                        wait=[(sT, t_gu), (sV, haz_sg[pj])], inc=sS)
                    s_silu = sS.n
                    haz_gate_ps[pj] = s_silu
                    P.add('v', lambda e, j=j, pj=pj: e.tensor_tensor(
                        hT_sb[:, j * 512:(j + 1) * 512], sg_b[pj][:, :],
                        PS[:, 1024 + pj * 512: 1024 + (pj + 1) * 512], op=ALU.mult),
                        wait=[(sS, s_silu)] + ([(sT, haz_hT)] if j == 0 else []), inc=sV)
                    v_hT = sV.n
                    haz_up_ps[pj] = v_hT
                    haz_sg[pj] = v_hT
                    hT_writes = v_hT
                    P.add('v', lambda e, j=j, pj=pj: e.tensor_tensor(
                        hsq_b[pj][:, :], hT_sb[:, j * 512:(j + 1) * 512],
                        hT_sb[:, j * 512:(j + 1) * 512], op=ALU.mult),
                        wait=[(sV, v_hT), (sT, haz_hsq[pj])], inc=sV)
                    v_hsq_by_j[j] = sV.n
                    for jr in ([j - 1] if j >= 1 else []) + ([j] if j == 63 else []):
                        pjr = jr % 2
                        for st_i in range(4):
                            w0 = []
                            if st_i == 0:
                                w0 = [(sV, v_hsq_by_j[jr])]
                                if jr == 0:
                                    w0 += [(sS, haz_rms_ps), (sV, haz_proj_ps[1])]
                            P.add('t', lambda e, st_i=st_i, pjr=pjr, jr=jr: e.matmul(
                                PS[:, (4 + st_i) * 512: (4 + st_i) * 512 + 1],
                                hsq_b[pjr][:, st_i * 128:(st_i + 1) * 128], ones_col[:, :],
                                start=(jr == 0), stop=(jr == 63)),
                                wait=w0, inc=(sT if st_i == 3 else None))
                        haz_hsq[pjr] = sT.n
                t_rms = sT.n
                P.add('s', lambda e: e.activation(
                    rsc_sb[:, 4:8].rearrange("p (a o) -> p a o", o=1),
                    PS[:, 2048:4096].rearrange("p (a n) -> p a n", n=512)[:, :, 0:1],
                    AF.Ln, scale=1.0 / 8192.0, bias=cst[:, 4:5]),
                    wait=[(sT, t_rms)], inc=sS)
                s1 = sS.n
                P.add('s', lambda e: e.activation(
                    rsc_sb[:, 0:4], rsc_sb[:, 4:8], AF.Exp, scale=-0.5),
                    wait=[(sS, s1)], inc=sS)
                s_rsc = sS.n
                haz_rms_ps = s_rsc
                adds_by_st = [0, 0, 0, 0]
                for q in range(4):
                    ph = q % 2
                    base = ph * 2048
                    for jg in range(8):
                        pw = jg % 2
                        prow = (jg * 4 + q) * 128
                        P.add('y', lambda e, prow=prow, pw=pw: e.dma_start(
                            wp_l[pw][:, :], wpT_d[prow:prow + 128, :]),
                            wait=[(sT, haz_wp[pw]), (dASM, dASM.n)], inc=dWP[pw], inc_n=16)
                        wp_ld = dWP[pw].n
                        for jin in range(8):
                            j = jg * 8 + jin
                            for st_i in range(4):
                                w0 = []
                                if jin == 0 and st_i == 0 and jg == 0:
                                    w0 = [(dWP[pw], wp_ld), (sV, hT_writes), (sS, s_rsc),
                                          (sV, haz_proj_ps[ph])]
                                elif jin == 0 and st_i == 0:
                                    w0 = [(dWP[pw], wp_ld)]
                                P.add('t', lambda e, j=j, jin=jin, st_i=st_i, pw=pw, base=base: e.matmul(
                                    PS[:, base + st_i * 512: base + (st_i + 1) * 512],
                                    hT_sb[:, j * 512 + st_i * 128: j * 512 + (st_i + 1) * 128],
                                    wp_l[pw][:, jin * 512:(jin + 1) * 512],
                                    start=(j == 0), stop=(j == 63)),
                                    wait=w0,
                                    inc=(sT if (jin == 7 and st_i == 3) else None))
                        haz_wp[pw] = sT.n
                    t_q = sT.n
                    if q >= 2:
                        haz_hT = t_q
                    for st_i in range(4):
                        pa = (q * 4 + st_i) % 2
                        arow = (b * NQB + qb) * 128
                        P.add('y', lambda e, arow=arow, st_i=st_i, q=q, pa=pa: e.dma_start(
                            at_l[pa][:, :],
                            attn_d[arow:arow + 128, st_i * 2048 + q * 512: st_i * 2048 + (q + 1) * 512]),
                            wait=[(sV, haz_at_rd[pa]), (dAT_S, dAT_S.n)], inc=dATL[pa], inc_n=16)
                        at_ld = dATL[pa].n
                        P.add('v', lambda e, st_i=st_i, q=q, base=base, pa=pa: e.scalar_tensor_tensor(
                            out_sb[:, st_i * 2048 + q * 512: st_i * 2048 + (q + 1) * 512],
                            PS[:, base + st_i * 512: base + (st_i + 1) * 512],
                            rsc_sb[:, st_i:st_i + 1], at_l[pa][:, :],
                            op0=ALU.mult, op1=ALU.add),
                            wait=[(sT, t_q), (dATL[pa], at_ld), (sS, s_rsc), (dOUT, haz_out_store)],
                            inc=sV)
                        adds_by_st[st_i] = sV.n
                        haz_at_rd[pa] = sV.n
                        haz_proj_ps[ph] = sV.n
                for st_i in range(4):
                    yrow = b * T + (qb * 4 + st_i) * 128
                    P.add('y', lambda e, yrow=yrow, st_i=st_i: e.dma_start(
                        y_d[yrow:yrow + 128, :], out_sb[:, st_i * 2048:(st_i + 1) * 2048]),
                        wait=[(sV, adds_by_st[st_i])], inc=dOUT, inc_n=16)
                haz_out_store = dOUT.n

        P.raw('y', lambda e, v=dOUT.n, h=dOUT.h: e.wait_ge(h, v))

        with nc.Block() as block:
            @block.tensor
            def _(e):
                for f in P.ops['t']:
                    f(e)

            @block.vector
            def _(e):
                for f in P.ops['v']:
                    f(e)

            @block.scalar
            def _(e):
                for f in P.ops['s']:
                    f(e)

            @block.gpsimd
            def _(e):
                for f in P.ops['g']:
                    f(e)

            @block.sync
            def _(e):
                for f in P.ops['y']:
                    f(e)

    return nc


def _sincos_table():
    fraction = np.arange(0, 128, 2, dtype=np.float64) / 128.0
    inv_freq = 1.0 / (10000.0 ** fraction)
    t = np.arange(T, dtype=np.float64)[:, None]
    ang = t * inv_freq[None, :]
    return np.concatenate([np.sin(ang), np.cos(ang)], axis=1).astype(np.float32)


def kernel(L0_attn_x, L0_attn_q_kernel, L0_attn_k_kernel, L0_attn_v_kernel,
           L0_attn_out_kernel, L0_ffn_w_gate_up, L0_ffn_w_proj):
    x = np.ascontiguousarray(np.asarray(L0_attn_x, dtype=np.float32))
    wq = np.ascontiguousarray(np.asarray(L0_attn_q_kernel, dtype=np.float32))
    wk = np.ascontiguousarray(np.asarray(L0_attn_k_kernel, dtype=np.float32))
    wv = np.ascontiguousarray(np.asarray(L0_attn_v_kernel, dtype=np.float32))
    wo = np.ascontiguousarray(np.asarray(L0_attn_out_kernel, dtype=np.float32))
    wgu = np.ascontiguousarray(np.asarray(L0_ffn_w_gate_up, dtype=np.float32))
    wp = np.ascontiguousarray(np.asarray(L0_ffn_w_proj, dtype=np.float32))
    sincos = _sincos_table()
    eye = np.eye(128, dtype=np.float32).astype(bfloat16)

    nc = build_nc()
    in_maps = []
    for c in range(N_CORES):
        in_maps.append(dict(
            x=np.ascontiguousarray(x[c * BC:(c + 1) * BC].reshape(BC * T, D)),
            wq=wq, wk=wk, wv=wv, wo=wo, wgu=wgu, wp=wp,
            sincos=sincos, eye=eye))
    res = run_bass_kernel_spmd(nc, in_maps, core_ids=list(range(N_CORES)))
    out = np.concatenate([np.asarray(r["y"]).reshape(BC, T, D) for r in res.results], axis=0)
    return out


if __name__ == "__main__":
    import time
    t0 = time.time()
    nc = build_nc()
    n = sum(1 for _ in nc.all_instructions())
    print(f"build ok in {time.time()-t0:.1f}s, {n} instructions")


# version 28 (final)
# speedup vs baseline: 1.1662x; 1.0126x over previous
import numpy as np
import ml_dtypes
from contextlib import ExitStack

import concourse.bass as bass
import concourse.mybir as mybir
from concourse.bass_utils import run_bass_kernel_spmd

F32 = mybir.dt.float32
BF16 = mybir.dt.bfloat16
AF = mybir.ActivationFunctionType
ALU = mybir.AluOpType
AX = mybir.AxisListType
bfloat16 = ml_dtypes.bfloat16

DEBUG = False
PHASES = "Q123"
N_CORES = 8
BC = 2
T = 2048
D = 2048
NT = 16
NQB = 4
LN_EPS = 1e-5
RMS_EPS = 1e-6
FFN_EPS = float(np.finfo(np.float32).eps)
LOG_INV_SQRT_H = float(np.log(1.0 / np.sqrt(128.0)))
SOFTMAX_SHIFT = -12.0


class Sem:
    def __init__(self, st, nc, name):
        self.h = st.enter_context(nc.semaphore(name))
        self.n = 0

    def inc(self, k=1):
        self.n += k
        return self.n


class Plan:
    def __init__(self):
        self.ops = {k: [] for k in "tvsgy"}

    def add(self, eng, builder, wait=(), inc=None, inc_n=1):
        if inc is not None:
            inc.inc(inc_n)
        wl = [(s.h, v) for (s, v) in wait if v > 0]
        ih = inc.h if inc is not None else None

        def run(e, wl=wl, ih=ih, inc_n=inc_n, builder=builder):
            for h, v in wl:
                e.wait_ge(h, v)
            ins = builder(e)
            if ih is not None:
                ins.then_inc(ih, inc_n)
        self.ops[eng].append(run)

    def raw(self, eng, fn):
        self.ops[eng].append(fn)


def build_nc():
    nc = bass.Bass("TRN2", target_bir_lowering=False)
    kind_scratch = "ExternalOutput" if DEBUG else "Internal"

    x_d = nc.dram_tensor("x", [BC * T, D], F32, kind="ExternalInput").ap()
    wq_d = nc.dram_tensor("wq", [D, 2048], F32, kind="ExternalInput").ap()
    wk_d = nc.dram_tensor("wk", [D, 512], F32, kind="ExternalInput").ap()
    wv_d = nc.dram_tensor("wv", [D, 512], F32, kind="ExternalInput").ap()
    wo_d = nc.dram_tensor("wo", [2048, D], F32, kind="ExternalInput").ap()
    wgu_d = nc.dram_tensor("wgu", [16384, D], F32, kind="ExternalInput").ap()
    wp_d = nc.dram_tensor("wp", [D, 8192], F32, kind="ExternalInput").ap()
    sc_d = nc.dram_tensor("sincos", [T, 128], F32, kind="ExternalInput").ap()
    eye_d = nc.dram_tensor("eye", [128, 128], BF16, kind="ExternalInput").ap()
    y_d = nc.dram_tensor("y", [BC * T, D], F32, kind="ExternalOutput").ap()

    wguT_d = nc.dram_tensor("wguT_s", [128 * 128, 2048], BF16, kind=kind_scratch).ap()
    wpT_d = nc.dram_tensor("wpT_s", [32 * 128, 4096], BF16, kind=kind_scratch).ap()
    xnT_d = nc.dram_tensor("xnT_s", [BC * NQB * 128, 8192], BF16, kind=kind_scratch).ap()
    qT_d = nc.dram_tensor("qT_s", [BC * NQB * 128, 8192], BF16, kind=kind_scratch).ap()
    kT_d = nc.dram_tensor("kT_s", [BC * 128, 8192], BF16, kind=kind_scratch).ap()
    v_d = nc.dram_tensor("v_s", [BC * 128, 8192], BF16, kind=kind_scratch).ap()
    attn_d = nc.dram_tensor("attn_s", [BC * NQB * 128, 8192], F32, kind=kind_scratch).ap()

    P = Plan()
    st_top = ExitStack()
    with st_top:
        sT = Sem(st_top, nc, "sT")
        sV = Sem(st_top, nc, "sV")
        sS = Sem(st_top, nc, "sS")
        sG = Sem(st_top, nc, "sG")
        sB = Sem(st_top, nc, "sB")
        dEye = Sem(st_top, nc, "dEye")
        dGW = [Sem(st_top, nc, f"dGW{i}") for i in range(6)]
        dGWT = Sem(st_top, nc, "dGWT")
        dQE = [Sem(st_top, nc, f"dQE{i}") for i in range(6)]
        dASM = Sem(st_top, nc, "dASM")
        dSC = Sem(st_top, nc, "dSC")
        dX = [Sem(st_top, nc, f"dX{i}") for i in range(2)]
        dXNT_S = Sem(st_top, nc, "dXNT_S")
        dQT_S = Sem(st_top, nc, "dQT_S")
        dKST = [Sem(st_top, nc, f"dKST{i}") for i in range(2)]
        dVST = [Sem(st_top, nc, f"dVST{i}") for i in range(2)]
        dKV = Sem(st_top, nc, "dKV")
        dQTL = [Sem(st_top, nc, f"dQTL{i}") for i in range(2)]
        dX2 = Sem(st_top, nc, "dX2")
        dAT_S = Sem(st_top, nc, "dAT_S")
        dATS2 = [Sem(st_top, nc, f"dATS2{i}") for i in range(2)]
        dXNT = [Sem(st_top, nc, f"dXNT{i}") for i in range(2)]
        dWGU = [Sem(st_top, nc, f"dWGU{i}") for i in range(3)]
        dWP = [Sem(st_top, nc, f"dWP{i}") for i in range(2)]
        dATL = [Sem(st_top, nc, f"dATL{i}") for i in range(2)]
        dOUT = Sem(st_top, nc, "dOUT")

        PS = st_top.enter_context(nc.psum_tensor("PS", [128, 4096], F32))
        eye = st_top.enter_context(nc.sbuf_tensor("eye_sb", [128, 128], BF16))
        ones_col = st_top.enter_context(nc.sbuf_tensor("ones_col", [128, 1], BF16))
        ones_row = st_top.enter_context(nc.sbuf_tensor("ones_row", [1, 128], BF16))
        ones128 = st_top.enter_context(nc.sbuf_tensor("ones128", [128, 128], BF16))

        cst = st_top.enter_context(nc.sbuf_tensor("cst", [128, 8], F32))
        P.add('y', lambda e: e.dma_start(eye[:, :], eye_d), inc=dEye, inc_n=16)
        eye_done = dEye.n
        P.add('v', lambda e: e.memset(ones_col[:, :], 1.0), inc=sV)
        P.add('v', lambda e: e.memset(ones_row[:, :], 1.0), inc=sV)
        P.add('v', lambda e: e.memset(cst[:, 0:1], LN_EPS), inc=sV)
        P.add('v', lambda e: e.memset(cst[:, 1:2], RMS_EPS), inc=sV)
        P.add('v', lambda e: e.memset(cst[:, 2:3], LOG_INV_SQRT_H), inc=sV)
        P.add('v', lambda e: e.memset(cst[:, 3:4], SOFTMAX_SHIFT), inc=sV)
        P.add('v', lambda e: e.memset(ones128[:, :], 1.0), inc=sV)
        P.add('v', lambda e: e.memset(cst[:, 4:5], FFN_EPS), inc=sV)
        ones_done = sV.n
        P.raw('s', lambda e, sh=sV.h, v=ones_done: e.wait_ge(sh, v))

        ALL_DMA_SEMS = ([dEye, dGWT, dASM, dSC, dXNT_S, dQT_S, dKV, dX2, dAT_S, dOUT]
                        + dATS2
                        + dGW + dQE + dX + dKST + dVST + dQTL + dXNT + dWGU + dWP + dATL)

        def barrier():
            for d in ALL_DMA_SEMS:
                if d.n > 0:
                    P.raw('y', lambda e, h=d.h, v=d.n: e.wait_ge(h, v))
            tgt = sB.n + 5
            for eng in "tvsgy":
                P.raw(eng, lambda e, s=sB.h: e.sem_inc(s, 1))
            sB.n = tgt
            for eng in "tvsgy":
                P.raw(eng, lambda e, s=sB.h, t=tgt: e.wait_ge(s, t))

        with ExitStack() as st:
          if '1' in PHASES:
            wq_sb = st.enter_context(nc.sbuf_tensor("wq_sb", [128, 16 * 2048], BF16))
            wk_sb = st.enter_context(nc.sbuf_tensor("wk_sb", [128, 16 * 512], BF16))
            wv_sb = st.enter_context(nc.sbuf_tensor("wv_sb", [128, 16 * 512], BF16))
            sc_sb = st.enter_context(nc.sbuf_tensor("sc_sb", [128, 2048], F32))
            x_b = [st.enter_context(nc.sbuf_tensor(f"x{p}", [128, 2048], F32)) for p in range(2)]
            xn_b = [st.enter_context(nc.sbuf_tensor(f"xn{p}", [128, 2048], BF16)) for p in range(2)]
            qf_b = [st.enter_context(nc.sbuf_tensor(f"qf{p}", [128, 2048], F32)) for p in range(2)]
            qr_b = [st.enter_context(nc.sbuf_tensor(f"p1qr{p}", [128, 2048], BF16)) for p in range(2)]
            kf_b = [st.enter_context(nc.sbuf_tensor(f"kf{p}", [128, 512], F32)) for p in range(2)]
            kr_b = [st.enter_context(nc.sbuf_tensor(f"kr{p}", [128, 512], BF16)) for p in range(2)]
            ra = st.enter_context(nc.sbuf_tensor("ra", [128, 1024], F32))
            rb = st.enter_context(nc.sbuf_tensor("rb", [128, 1024], F32))
            junk = st.enter_context(nc.sbuf_tensor("junk", [128, 2048], BF16))
            sm_b = [st.enter_context(nc.sbuf_tensor(f"sm{p}", [128, 8], F32)) for p in range(2)]
            xnT_qb = st.enter_context(nc.sbuf_tensor("xnT_qb", [128, 8192], BF16))
            qT_qb = st.enter_context(nc.sbuf_tensor("qT_qb", [128, 8192], BF16))
            kst_b = [st.enter_context(nc.sbuf_tensor(f"kst{p}", [128, 512], BF16)) for p in range(2)]
            vst_b = [st.enter_context(nc.sbuf_tensor(f"vst{p}", [128, 512], BF16)) for p in range(2)]

            P.add('g', lambda e: e.dma_start(
                wq_sb[:, :].rearrange("p (c n) -> p c n", n=2048),
                wq_d.rearrange("(c p) n -> p c n", p=128)), inc=dGWT, inc_n=16)
            P.add('g', lambda e: e.dma_start(
                wk_sb[:, :].rearrange("p (c n) -> p c n", n=512),
                wk_d.rearrange("(c p) n -> p c n", p=128)), inc=dGWT, inc_n=16)
            P.add('g', lambda e: e.dma_start(
                wv_sb[:, :].rearrange("p (c n) -> p c n", n=512),
                wv_d.rearrange("(c p) n -> p c n", p=128)), inc=dGWT, inc_n=16)
            w_ld = dGWT.n
            for t in range(NT):
                P.add('y', lambda e, t=t: e.dma_start(
                    sc_sb[:, t * 128:(t + 1) * 128], sc_d[t * 128:(t + 1) * 128, :]),
                    inc=dSC, inc_n=16)
            sc_ld = dSC.n

            haz_x = [0, 0]
            haz_xsq = [0, 0]
            haz_xn = [0, 0]
            haz_qf = [0, 0]
            haz_kf = [0, 0]
            haz_sm = [0, 0]
            haz_qr = [0, 0]
            haz_kr = [0, 0]
            haz_xnT_store = 0
            haz_qT_store = 0
            haz_kst = [0, 0]
            haz_vst = [0, 0]
            ps03_free = 0
            ps_q_free = [0, 0]
            ps_k_free = 0
            ps_v_free = 0
            CC = {}

            NTILES = BC * NT

            def stage_a_load(i):
                p = i % 2
                b, t = i // NT, i % NT
                row0 = b * T + t * 128
                P.add('y', lambda e, x_=x_b[p], row0=row0: e.dma_start(
                    x_[:, :], x_d[row0:row0 + 128, :]),
                    wait=[(sV, haz_x[p]), (sS, haz_xsq[p])], inc=dX[p], inc_n=16)
                CC[('xld', i)] = dX[p].n

            def stage_a_ln(i):
                p = i % 2
                x_, xn, sm = x_b[p], xn_b[p], sm_b[p]
                x_ld = CC[('xld', i)]
                P.add('v', lambda e, sm=sm, x_=x_: e.reduce_sum(
                    sm[:, 0:1], x_[:, :], axis=AX.X),
                    wait=[(dX[p], x_ld)], inc=sV)
                v_sum = sV.n
                P.add('v', lambda e, sm=sm: e.tensor_single_scalar(
                    sm[:, 1:2], sm[:, 0:1], -1.0 / 2048.0, op=ALU.mult),
                    wait=[(sV, v_sum)], inc=sV)
                v_mean = sV.n
                P.add('s', lambda e, x_=x_, sm=sm: e.activation(
                    junk[:, :], x_[:, :], AF.Square, bias=sm[:, 1:2],
                    accum_out=sm[:, 2:3]),
                    wait=[(sV, v_mean)], inc=sS)
                s_ssq = sS.n
                haz_xsq[p] = s_ssq
                P.add('s', lambda e, sm=sm: e.activation(
                    sm[:, 3:4], sm[:, 2:3], AF.Ln, scale=1.0 / 2048.0, bias=cst[:, 0:1]),
                    wait=[(sS, s_ssq)], inc=sS)
                s_lnv = sS.n
                P.add('s', lambda e, sm=sm: e.activation(
                    sm[:, 4:5], sm[:, 3:4], AF.Exp, scale=-0.5),
                    wait=[(sS, s_lnv)], inc=sS)
                CC[('rstd', i)] = sS.n

            def stage_a_xn(i):
                p = i % 2
                x_, xn, sm = x_b[p], xn_b[p], sm_b[p]
                P.add('v', lambda e, xn=xn, x_=x_, sm=sm: e.tensor_scalar(
                    xn[:, :], x_[:, :], sm[:, 1:2], sm[:, 4:5],
                    op0=ALU.add, op1=ALU.mult),
                    wait=[(sS, CC[('rstd', i)]), (sT, haz_xn[p])], inc=sV)
                v_xn = sV.n
                haz_x[p] = v_xn
                CC[('vxn', i)] = v_xn

            def stage_a_tr(i):
                nonlocal ps03_free, haz_xnT_store
                p = i % 2
                b, t = i // NT, i % NT
                qb, t4 = t // 4, t % 4
                xn = xn_b[p]
                for c in range(16):
                    P.add('t', lambda e, xn=xn, c=c: e.matmul(
                        PS[:, c * 128:(c + 1) * 128], xn[:, c * 128:(c + 1) * 128],
                        eye[:, :], start=True, stop=True),
                        wait=([(sV, CC[('vxn', i)]), (sS, ps03_free), (dEye, eye_done)]
                              if c == 0 else []),
                        inc=(sT if c == 15 else None))
                t_xntr = sT.n
                haz_xn[p] = t_xntr
                P.add('s', lambda e, t4=t4: e.copy(
                    xnT_qb[:, :].rearrange("p (c n) -> p c n", n=512)[:, :, t4 * 128:(t4 + 1) * 128],
                    PS[:, 0:2048].rearrange("p (c n) -> p c n", n=128)),
                    wait=[(sT, t_xntr), (dXNT_S, haz_xnT_store)], inc=sS)
                s_xnT = sS.n
                ps03_free = s_xnT
                CC[('sxnT', i)] = s_xnT
                if t4 == 3:
                    rows = (b * NQB + qb) * 128
                    P.add('y', lambda e, rows=rows: e.dma_start(
                        xnT_d[rows:rows + 128, :], xnT_qb[:, :]),
                        wait=[(sS, s_xnT)], inc=dXNT_S, inc_n=16)
                    haz_xnT_store = dXNT_S.n

            def stage_b_q(i):
                p = i % 2
                b, t = i // NT, i % NT
                t4 = t % 4
                qf = qf_b[p]
                s_qev = 0
                for n in range(4):
                    bank = 3072 + (n % 2) * 512
                    for c in range(16):
                        P.add('t', lambda e, n=n, c=c, t4=t4, bank=bank: e.matmul(
                            PS[:, bank:bank + 512],
                            xnT_qb[:, c * 512 + t4 * 128: c * 512 + (t4 + 1) * 128],
                            wq_sb[:, c * 2048 + n * 512: c * 2048 + (n + 1) * 512],
                            start=(c == 0), stop=(c == 15)),
                            wait=([(sS, CC[('sxnT', i)]), (sS, ps_q_free[n % 2]), (dGWT, w_ld)]
                                  if c == 0 else []),
                            inc=(sT if c == 15 else None))
                    t_q = sT.n
                    P.add('s', lambda e, qf=qf, n=n, bank=bank: e.copy(
                        qf[:, n * 512:(n + 1) * 512], PS[:, bank:bank + 512]),
                        wait=[(sT, t_q)] + ([(sV, haz_qf[p])] if n == 0 else []), inc=sS)
                    s_qev = sS.n
                    ps_q_free[n % 2] = s_qev
                CC[('sqev', i)] = s_qev

            def stage_b_kv(i):
                nonlocal ps_k_free, ps_v_free
                p = i % 2
                b, t = i // NT, i % NT
                t4 = t % 4
                qf, qr, kf, kr, sm = qf_b[p], qr_b[p], kf_b[p], kr_b[p], sm_b[p]
                for c in range(16):
                    P.add('t', lambda e, c=c, t4=t4: e.matmul(
                        PS[:, 2048:2560],
                        xnT_qb[:, c * 512 + t4 * 128: c * 512 + (t4 + 1) * 128],
                        wk_sb[:, c * 512: (c + 1) * 512],
                        start=(c == 0), stop=(c == 15)),
                        wait=([(sS, ps_k_free)] if c == 0 else []),
                        inc=(sT if c == 15 else None))
                t_k = sT.n
                for c in range(16):
                    P.add('t', lambda e, c=c, t4=t4: e.matmul(
                        PS[:, 2560:3072],
                        xnT_qb[:, c * 512 + t4 * 128: c * 512 + (t4 + 1) * 128],
                        wv_sb[:, c * 512: (c + 1) * 512],
                        start=(c == 0), stop=(c == 15)),
                        wait=([(sS, ps_v_free)] if c == 0 else []),
                        inc=(sT if c == 15 else None))
                t_v = sT.n
                P.add('s', lambda e, qf=qf, sm=sm: e.activation(
                    junk[:, :], qf[:, :], AF.Square, accum_out=sm[:, 5:6]),
                    wait=[(sS, CC[('sqev', i)]), (sV, haz_sm[p])], inc=sS)
                s1 = sS.n
                P.add('s', lambda e, sm=sm: e.activation(
                    sm[:, 6:7], sm[:, 5:6], AF.Ln, scale=1.0 / 2048.0, bias=cst[:, 1:2]),
                    wait=[(sS, s1)], inc=sS)
                s2 = sS.n
                P.add('s', lambda e, sm=sm: e.activation(
                    sm[:, 5:6], sm[:, 6:7], AF.Exp, scale=-0.5, bias=cst[:, 2:3]),
                    wait=[(sS, s2)], inc=sS)
                s_qrs = sS.n
                P.add('s', lambda e, kf=kf: e.copy(kf[:, :], PS[:, 2048:2560]),
                      wait=[(sT, t_k), (sV, haz_kf[p])], inc=sS)
                s_kev = sS.n
                ps_k_free = s_kev
                P.add('s', lambda e, kf=kf, sm=sm: e.activation(
                    junk[:, 0:512], kf[:, :], AF.Square, accum_out=sm[:, 6:7]),
                    wait=[(sS, s_kev)], inc=sS)
                s1 = sS.n
                P.add('s', lambda e, sm=sm: e.activation(
                    sm[:, 7:8], sm[:, 6:7], AF.Ln, scale=1.0 / 512.0, bias=cst[:, 1:2]),
                    wait=[(sS, s1)], inc=sS)
                s2 = sS.n
                P.add('s', lambda e, sm=sm: e.activation(
                    sm[:, 6:7], sm[:, 7:8], AF.Exp, scale=-0.5),
                    wait=[(sS, s2)], inc=sS)
                s_krs = sS.n
                P.add('s', lambda e, vst=vst_b[p]: e.copy(vst[:, :], PS[:, 2560:3072]),
                      wait=[(sT, t_v), (dVST[p], haz_vst[p])], inc=sS)
                s_vev = sS.n
                ps_v_free = s_vev
                P.add('y', lambda e, vst=vst_b[p], b=b, t=t: e.dma_start(
                    v_d[b * 128:(b + 1) * 128, t * 512:(t + 1) * 512], vst[:, :]),
                    wait=[(sS, s_vev)], inc=dVST[p], inc_n=16)
                haz_vst[p] = dVST[p].n
                sin16 = sc_sb[:, t * 128: t * 128 + 64].rearrange(
                    "p (o k) -> p o k", o=1).broadcast_to([128, 16, 64])
                cos16 = sc_sb[:, t * 128 + 64: t * 128 + 128].rearrange(
                    "p (o k) -> p o k", o=1).broadcast_to([128, 16, 64])
                qv = qf[:, :].rearrange("p (h x) -> p h x", x=128)
                qF, qS = qv[:, :, 0:64], qv[:, :, 64:128]
                rav = ra[:, :].rearrange("p (h k) -> p h k", k=64)
                rbv = rb[:, :].rearrange("p (h k) -> p h k", k=64)
                qrv = qr[:, :].rearrange("p (h x) -> p h x", x=128)
                P.add('v', lambda e, rav=rav, qF=qF, sm=sm, cos16=cos16: e.scalar_tensor_tensor(
                    rav, qF, sm[:, 5:6], cos16, op0=ALU.mult, op1=ALU.mult),
                    wait=[(sS, s_qrs), (dSC, sc_ld)], inc=sV)
                v1 = sV.n
                P.add('v', lambda e, rbv=rbv, qS=qS, sm=sm, sin16=sin16: e.scalar_tensor_tensor(
                    rbv, qS, sm[:, 5:6], sin16, op0=ALU.mult, op1=ALU.mult),
                    wait=[(sS, s_qrs)], inc=sV)
                v2 = sV.n
                P.add('v', lambda e, qrv=qrv, rav=rav, rbv=rbv: e.tensor_tensor(
                    qrv[:, :, 0:64], rav, rbv, op=ALU.subtract),
                    wait=[(sV, v1), (sV, v2), (sT, haz_qr[p])], inc=sV)
                v3 = sV.n
                P.add('v', lambda e, rav=rav, qS=qS, sm=sm, cos16=cos16: e.scalar_tensor_tensor(
                    rav, qS, sm[:, 5:6], cos16, op0=ALU.mult, op1=ALU.mult),
                    wait=[(sV, v3)], inc=sV)
                v4 = sV.n
                P.add('v', lambda e, rbv=rbv, qF=qF, sm=sm, sin16=sin16: e.scalar_tensor_tensor(
                    rbv, qF, sm[:, 5:6], sin16, op0=ALU.mult, op1=ALU.mult),
                    wait=[(sV, v3)], inc=sV)
                v5 = sV.n
                P.add('v', lambda e, qrv=qrv, rav=rav, rbv=rbv: e.tensor_tensor(
                    qrv[:, :, 64:128], rav, rbv, op=ALU.add),
                    wait=[(sV, v4), (sV, v5)], inc=sV)
                v_qrope = sV.n
                haz_qf[p] = v_qrope
                sin4 = sc_sb[:, t * 128: t * 128 + 64].rearrange(
                    "p (o k) -> p o k", o=1).broadcast_to([128, 4, 64])
                cos4 = sc_sb[:, t * 128 + 64: t * 128 + 128].rearrange(
                    "p (o k) -> p o k", o=1).broadcast_to([128, 4, 64])
                kvv = kf[:, :].rearrange("p (h x) -> p h x", x=128)
                kF, kS = kvv[:, :, 0:64], kvv[:, :, 64:128]
                rav4 = ra[:, 0:256].rearrange("p (h k) -> p h k", k=64)
                rbv4 = rb[:, 0:256].rearrange("p (h k) -> p h k", k=64)
                krv = kr[:, :].rearrange("p (h x) -> p h x", x=128)
                P.add('v', lambda e, rav4=rav4, kF=kF, sm=sm, cos4=cos4: e.scalar_tensor_tensor(
                    rav4, kF, sm[:, 6:7], cos4, op0=ALU.mult, op1=ALU.mult),
                    wait=[(sS, s_krs), (sV, v_qrope)], inc=sV)
                v1 = sV.n
                P.add('v', lambda e, rbv4=rbv4, kS=kS, sm=sm, sin4=sin4: e.scalar_tensor_tensor(
                    rbv4, kS, sm[:, 6:7], sin4, op0=ALU.mult, op1=ALU.mult),
                    wait=[(sS, s_krs)], inc=sV)
                v2 = sV.n
                P.add('v', lambda e, krv=krv, rav4=rav4, rbv4=rbv4: e.tensor_tensor(
                    krv[:, :, 0:64], rav4, rbv4, op=ALU.subtract),
                    wait=[(sV, v1), (sV, v2), (sT, haz_kr[p])], inc=sV)
                v3 = sV.n
                P.add('v', lambda e, rav4=rav4, kS=kS, sm=sm, cos4=cos4: e.scalar_tensor_tensor(
                    rav4, kS, sm[:, 6:7], cos4, op0=ALU.mult, op1=ALU.mult),
                    wait=[(sV, v3)], inc=sV)
                v4 = sV.n
                P.add('v', lambda e, rbv4=rbv4, kF=kF, sm=sm, sin4=sin4: e.scalar_tensor_tensor(
                    rbv4, kF, sm[:, 6:7], sin4, op0=ALU.mult, op1=ALU.mult),
                    wait=[(sV, v3)], inc=sV)
                v5 = sV.n
                P.add('v', lambda e, krv=krv, rav4=rav4, rbv4=rbv4: e.tensor_tensor(
                    krv[:, :, 64:128], rav4, rbv4, op=ALU.add),
                    wait=[(sV, v4), (sV, v5)], inc=sV)
                v_krope = sV.n
                haz_kf[p] = v_krope
                haz_sm[p] = v_krope
                CC[('vqrope', i)] = v_qrope
                CC[('vkrope', i)] = v_krope

            def stage_c(i):
                nonlocal ps03_free, ps_k_free, haz_qT_store
                p = i % 2
                b, t = i // NT, i % NT
                qb, t4 = t // 4, t % 4
                for n in range(16):
                    P.add('t', lambda e, qr=qr_b[p], n=n: e.matmul(
                        PS[:, n * 128:(n + 1) * 128], qr[:, n * 128:(n + 1) * 128],
                        eye[:, :], start=True, stop=True),
                        wait=([(sV, CC[('vqrope', i)]), (sS, ps03_free)] if n == 0 else []),
                        inc=(sT if n == 15 else None))
                t_qtr = sT.n
                haz_qr[p] = t_qtr
                P.add('s', lambda e, t4=t4: e.copy(
                    qT_qb[:, :].rearrange("p (h n) -> p h n", n=512)[:, :, t4 * 128:(t4 + 1) * 128],
                    PS[:, 0:2048].rearrange("p (h n) -> p h n", n=128)),
                    wait=[(sT, t_qtr), (dQT_S, haz_qT_store)], inc=sS)
                s_qT = sS.n
                ps03_free = s_qT
                for n in range(4):
                    P.add('t', lambda e, kr=kr_b[p], n=n: e.matmul(
                        PS[:, 2048 + n * 128: 2048 + (n + 1) * 128],
                        kr[:, n * 128:(n + 1) * 128], eye[:, :], start=True, stop=True),
                        wait=([(sV, CC[('vkrope', i)]), (sS, ps_k_free)] if n == 0 else []),
                        inc=(sT if n == 3 else None))
                t_ktr = sT.n
                haz_kr[p] = t_ktr
                P.add('s', lambda e, kst=kst_b[p]: e.copy(kst[:, :], PS[:, 2048:2560]),
                      wait=[(sT, t_ktr), (dKST[p], haz_kst[p])], inc=sS)
                s_kT = sS.n
                ps_k_free = s_kT
                P.add('y', lambda e, kst=kst_b[p], b=b, t=t: e.dma_start(
                    kT_d[b * 128:(b + 1) * 128, :].rearrange(
                        "p (h n) -> p h n", n=2048)[:, :, t * 128:(t + 1) * 128],
                    kst[:, :].rearrange("p (h n) -> p h n", n=128)),
                    wait=[(sS, s_kT)], inc=dKST[p], inc_n=16)
                haz_kst[p] = dKST[p].n
                if t4 == 3:
                    rows = (b * NQB + qb) * 128
                    P.add('y', lambda e, rows=rows: e.dma_start(
                        qT_d[rows:rows + 128, :], qT_qb[:, :]),
                        wait=[(sS, s_qT)], inc=dQT_S, inc_n=16)
                    haz_qT_store = dQT_S.n

            stage_a_load(0)
            for step in range(NTILES + 2):
                if step + 1 < NTILES:
                    stage_a_load(step + 1)
                if 2 <= step:
                    stage_c(step - 2)
                if step < NTILES:
                    stage_a_ln(step)
                    stage_a_xn(step)
                if 1 <= step <= NTILES:
                    stage_b_q(step - 1)
                    stage_b_kv(step - 1)
                if step < NTILES:
                    stage_a_tr(step)

            barrier()

        with ExitStack() as st:
            wo_sb = st.enter_context(nc.sbuf_tensor("wo_sb", [128, 16 * 2048], BF16))
            kT_sb = st.enter_context(nc.sbuf_tensor("kT_sb", [128, 8192], BF16))
            v_sb = st.enter_context(nc.sbuf_tensor("v_sb", [128, 8192], BF16))
            qT_l = [st.enter_context(nc.sbuf_tensor("qTl0", [128, 8192], BF16))]
            P_sb = st.enter_context(nc.sbuf_tensor("P_sb", [128, 4 * 512], BF16))
            encT = st.enter_context(nc.sbuf_tensor("encT", [128, 8192], BF16))
            rs_f = st.enter_context(nc.sbuf_tensor("rs_f", [1, 512], F32))
            rs_bf = st.enter_context(nc.sbuf_tensor("rs_bf", [1, 512], BF16))
            rsb_sb = st.enter_context(nc.sbuf_tensor("rsb_sb", [128, 512], F32))
            rstmp = st.enter_context(nc.sbuf_tensor("rstmp", [128, 512], F32))
            at_b = [st.enter_context(nc.sbuf_tensor(f"atp{p}", [128, 512], F32)) for p in range(2)]
            x2_b = [st.enter_context(nc.sbuf_tensor("x20", [128, 2048], F32))]
            NPQ = 6
            qw_b = [st.enter_context(nc.sbuf_tensor(f"qw{p}", [128, 512], BF16)) for p in range(NPQ)]
            qaw_b = [st.enter_context(nc.sbuf_tensor(f"qaw{p}", [128, 512], BF16)) for p in range(4)]
            qsg_b = [st.enter_context(nc.sbuf_tensor(f"qsg{p}", [128, 512], BF16)) for p in range(4)]
            qsc_b = [st.enter_context(nc.sbuf_tensor(f"qsc{p}", [2, 512], BF16)) for p in range(2)]
            qsr_b = [st.enter_context(nc.sbuf_tensor(f"qsr{p}", [128, 512], BF16)) for p in range(3)]
            qm_b = [st.enter_context(nc.sbuf_tensor(f"qm{p}", [128, 512], BF16)) for p in range(2)]
            qq_b = [st.enter_context(nc.sbuf_tensor(f"qq{p}", [128, 512], BF16)) for p in range(3)]
            blko = st.enter_context(nc.sbuf_tensor("blko", [128, 2], BF16))
            blkT = st.enter_context(nc.sbuf_tensor("blkT", [2, 128], BF16))

            P.add('g', lambda e: e.dma_start(
                wo_sb[:, :].rearrange("p (c n) -> p c n", n=2048),
                wo_d.rearrange("(c p) n -> p c n", p=128)), inc=dGWT, inc_n=16)
            wo_ld = dGWT.n

            NQH = 768
            QC = {}

            def q_src(i):
                if i < 512:
                    f, hf = i // 2, i % 2
                    j, h = f // 2, f % 2
                    c0 = h * 1024 + hf * 512
                    return wgu_d[j * 128:(j + 1) * 128, c0:c0 + 512]
                f, hf = (i - 512) // 2, i % 2
                c, hs = f // 8, f % 8
                c0 = hs * 1024 + hf * 512
                return wp_d[c * 128:(c + 1) * 128, c0:c0 + 512]

            P.add('v', lambda e: e.memset(blko[0:64, 1:2], 0.0), inc=sV)
            P.add('v', lambda e: e.memset(blko[64:128, 1:2], 1.0), inc=sV)
            P.add('v', lambda e: e.memset(blko[0:64, 0:1], 1.0), inc=sV)
            P.add('v', lambda e: e.memset(blko[64:128, 0:1], 0.0), inc=sV)
            blko_done = sV.n
            P.add('t', lambda e: e.matmul(
                PS[0:2, 3584:3712], blko[:, :], eye[:, :], start=True, stop=True),
                wait=[(sV, blko_done), (dEye, eye_done)], inc=sT)
            t_blkT = sT.n
            P.add('s', lambda e: e.copy(blkT[:, :], PS[0:2, 3584:3712]),
                  wait=[(sT, t_blkT)], inc=sS)
            s_blkT = sS.n

            def qst_load(i):
                p = i % NPQ
                P.add('g', lambda e, w=qw_b[p], src=q_src(i): e.dma_start(w[:, :], src),
                      wait=[(sT, QC.get(('tr', i - NPQ), 0))], inc=dGW[p], inc_n=16)
                QC[('ld', i)] = dGW[p].n

            def qst_tr(i):
                p = i % NPQ
                tc = 2048 + (i % 2) * 512
                for c in range(4):
                    P.add('t', lambda e, w=qw_b[p], c=c, tc=tc: e.matmul(
                        PS[:, tc + c * 128: tc + (c + 1) * 128],
                        w[:, c * 128:(c + 1) * 128], eye[:, :], start=True, stop=True),
                        wait=([(dGW[p], QC[('ld', i)]), (sS, QC.get(('sgn', i - 2), 0)),
                               (dEye, eye_done)] if c == 0 else []),
                        inc=(sT if c == 3 else None))
                QC[('tr', i)] = sT.n

            def qst_as(i):
                tc = 2048 + (i % 2) * 512
                pa = i % 4
                P.add('s', lambda e, aw=qaw_b[pa], tc=tc: e.activation(
                    aw[:, :], PS[:, tc:tc + 512], AF.Abs, scale=2.0),
                    wait=[(sT, QC[('tr', i)]), (sV, QC.get(('m', i - 4), 0))], inc=sS)
                QC[('aw', i)] = sS.n
                P.add('s', lambda e, sg=qsg_b[pa], tc=tc: e.activation(
                    sg[:, :], PS[:, tc:tc + 512], AF.Sign),
                    wait=[(sG, QC.get(('q', i - 4), 0))], inc=sS)
                QC[('sgn', i)] = sS.n

            def qst_gs(i):
                rc = 3072 + (i % 2) * 512
                w0 = [(sS, QC[('aw', i)]), (sV, QC.get(('evr', i - 2), 0))]
                if i == 1:
                    w0.append((sS, s_blkT))
                P.add('t', lambda e, aw=qaw_b[i % 4], rc=rc: e.matmul(
                    PS[0:2, rc:rc + 512], blko[:, :], aw[:, :], start=True, stop=True),
                    wait=w0, inc=sT)
                QC[('gs', i)] = sT.n

            def qst_screp(i):
                rc = 3072 + (i % 2) * 512
                psc, psr = i % 2, i % 3
                P.add('v', lambda e, sc=qsc_b[psc], rc=rc: e.tensor_scalar(
                    sc[:, :], PS[0:2, rc:rc + 512], 1.0 / 128.0, 1e-8,
                    op0=ALU.mult, op1=ALU.max),
                    wait=[(sT, QC[('gs', i)]), (sT, QC.get(('rep', i - 2), 0))], inc=sV)
                QC[('scv', i)] = sV.n
                w1 = [(sV, QC[('scv', i)])]
                if i == 0:
                    w1.append((sS, s_blkT))
                P.add('t', lambda e, sc=qsc_b[psc], rc=rc: e.matmul(
                    PS[:, rc:rc + 512], blkT[:, :], sc[:, :], start=True, stop=True),
                    wait=w1, inc=sT)
                QC[('rep', i)] = sT.n
                P.add('v', lambda e, sr=qsr_b[psr], rc=rc: e.tensor_copy(
                    sr[:, :], PS[:, rc:rc + 512]),
                    wait=[(sT, QC[('rep', i)]), (sV, QC.get(('wh', i - 3), 0))], inc=sV)
                QC[('evr', i)] = sV.n

            def qst_mqw(i):
                pm, pa, psr, pq = i % 2, i % 4, i % 3, i % 3
                P.add('v', lambda e, m=qm_b[pm], aw=qaw_b[pa], sr=qsr_b[psr]: e.tensor_tensor(
                    m[:, :], aw[:, :], sr[:, :], op=ALU.is_gt),
                    wait=[(sV, QC[('evr', i)]), (sV, QC.get(('wh', i - 2), 0))], inc=sV)
                QC[('m', i)] = sV.n
                P.add('g', lambda e, m=qm_b[pm], sg=qsg_b[pa]: e.tensor_tensor(
                    m[:, :], m[:, :], sg[:, :], op=ALU.mult),
                    wait=[(sV, QC[('m', i)]), (sS, QC[('sgn', i)])], inc=sG)
                QC[('q', i)] = sG.n
                P.add('v', lambda e, q=qq_b[pq], m=qm_b[pm], sr=qsr_b[psr]: e.tensor_tensor(
                    q[:, :], m[:, :], sr[:, :], op=ALU.mult),
                    wait=[(sG, QC[('q', i)]), (dQE[pq], QC.get(('st', i - 3), 0))], inc=sV)
                QC[('wh', i)] = sV.n

            def qst_fin(i):
                pq = i % 3
                if i < 512:
                    f, hf = i // 2, i % 2
                    j, h = f // 2, f % 2
                    jj, aa = (j, 0) if j < 64 else (j - 64, 1)
                    r = (jj * 2 + aa) * 128
                    c0 = h * 1024 + hf * 512
                    P.add('y', lambda e, q=qq_b[pq], r=r, c0=c0: e.dma_start(
                        wguT_d[r:r + 128, c0:c0 + 512], q[:, :]),
                        wait=[(sV, QC[('wh', i)])], inc=dQE[pq], inc_n=16)
                else:
                    f, hf = (i - 512) // 2, i % 2
                    c, hs = f // 8, f % 8
                    for hl in range(4):
                        hh = hf * 4 + hl
                        row = (hs * 4 + (c // 4)) * 128
                        col = hh * 512 + (c % 4) * 128
                        P.add('y', lambda e, q=qq_b[pq], hl=hl, row=row, col=col: e.dma_start(
                            wpT_d[row:row + 128, col:col + 128],
                            q[:, hl * 128:(hl + 1) * 128]),
                            wait=([(sV, QC[('wh', i)])] if hl == 0 else []),
                            inc=dQE[pq], inc_n=16)
                QC[('st', i)] = dQE[pq].n

            q_step = [0]

            def quant_step(n=1):
                for _ in range(n):
                    k = q_step[0]
                    if k >= NQH + 6:
                        return
                    q_step[0] += 1
                    if 4 <= k < NQH + 4:
                        qst_screp(k - 4)
                    if 2 <= k < NQH + 2:
                        qst_as(k - 2)
                    if 5 <= k < NQH + 5:
                        qst_mqw(k - 5)
                    if 6 <= k < NQH + 6:
                        qst_fin(k - 6)
                    if 3 <= k < NQH + 3:
                        qst_gs(k - 3)
                    if 1 <= k < NQH + 1:
                        qst_tr(k - 1)
                    if k < NQH:
                        qst_load(k)

            haz_kv_read = 0
            haz_qT_read = 0
            haz_P = {}
            haz_at = [0, 0]
            haz_x2 = [0]
            ps_L_free = [0, 0]
            ps_PV_free = 0
            ps_sums_free = 0
            ps_op_free = [0, 0]
            haz_encT = 0

            for b in range(BC):
                P.add('y', lambda e, b=b: e.dma_start(kT_sb[:, :], kT_d[b * 128:(b + 1) * 128, :]),
                      wait=[(sT, haz_kv_read)] + [(dd, dd.n) for dd in dKST + dVST],
                      inc=dKV, inc_n=16)
                P.add('y', lambda e, b=b: e.dma_start(v_sb[:, :], v_d[b * 128:(b + 1) * 128, :]),
                      inc=dKV, inc_n=16)
                kv_ld = dKV.n
                for qb in range(NQB):
                    rows = (b * NQB + qb) * 128
                    P.add('y', lambda e, rows=rows: e.dma_start(
                        qT_l[0][:, :], qT_d[rows:rows + 128, :]),
                        wait=[(sT, haz_qT_read), (dQT_S, dQT_S.n)], inc=dQTL[0], inc_n=16)
                    q_ld = dQTL[0].n
                    enc_last = 0
                    for u in range(16):
                        n_h = u
                        kvh = u // 4
                        exp_done = {}
                        for sc in range(18):
                            half = sc % 2
                            col = half * 512
                            if sc < 16:
                                if sc < 2:
                                    w0 = [(dKV, kv_ld), (dQTL[0], q_ld), (sS, ps_L_free[half]),
                                          (sT, haz_P.get(sc % 4, 0))]
                                else:
                                    w0 = [(sS, exp_done[sc - 2])]
                                P.add('t', lambda e, kvh=kvh, sc=sc, col=col, n_h=n_h: e.matmul(
                                    PS[:, col:col + 512],
                                    kT_sb[:, kvh * 2048 + sc * 128: kvh * 2048 + (sc + 1) * 128],
                                    qT_l[0][:, n_h * 512:(n_h + 1) * 512], start=True, stop=True),
                                    wait=w0, inc=sT)
                                t_L = sT.n
                                P.add('s', lambda e, sc=sc, col=col: e.activation(
                                    P_sb[:, (sc % 4) * 512:(sc % 4 + 1) * 512], PS[:, col:col + 512],
                                    AF.Exp, bias=cst[:, 3:4]),
                                    wait=[(sT, t_L), (sT, haz_P.get(sc % 4, 0))], inc=sS)
                                exp_done[sc] = sS.n
                                ps_L_free[half] = sS.n
                            if sc >= 2:
                                pv = sc - 2
                                P.add('t', lambda e, pv=pv, kvh=kvh: e.matmul(
                                    PS[:, 1024:1536],
                                    v_sb[:, pv * 512 + kvh * 128: pv * 512 + (kvh + 1) * 128],
                                    P_sb[:, (pv % 4) * 512:(pv % 4 + 1) * 512],
                                    start=(pv == 0), stop=(pv == 15)),
                                    wait=([(sS, exp_done[pv])] +
                                          ([(sV, ps_PV_free)] if pv == 0 else [])))
                                P.add('t', lambda e, pv=pv: e.matmul(
                                    PS[:, 1536:2048], ones128[:, :],
                                    P_sb[:, (pv % 4) * 512:(pv % 4 + 1) * 512],
                                    start=(pv == 0), stop=(pv == 15)),
                                    wait=([(sV, ones_done), (sS, ps_sums_free)] if pv == 0 else []),
                                    inc=sT)
                                haz_P[pv % 4] = sT.n
                            if sc % 3 == 2:
                                quant_step(1)
                        t_unit = sT.n
                        P.add('s', lambda e: e.activation(
                            rstmp[:, :], PS[:, 1536:2048], AF.Ln),
                            wait=[(sT, t_unit)], inc=sS)
                        s_ln = sS.n
                        ps_sums_free = s_ln
                        P.add('s', lambda e: e.activation(
                            rsb_sb[:, :], rstmp[:, :], AF.Exp, scale=-1.0),
                            wait=[(sS, s_ln), (sV, ps_PV_free)], inc=sS)
                        s_rs = sS.n
                        P.add('v', lambda e, n_h=n_h: e.tensor_tensor(
                            encT[:, n_h * 512:(n_h + 1) * 512], PS[:, 1024:1536],
                            rsb_sb[:, :], op=ALU.mult),
                            wait=[(sS, s_rs), (sT, haz_encT)], inc=sV)
                        ps_PV_free = sV.n
                        enc_last = sV.n
                    for st_i in range(4):
                        xrow = b * T + (qb * 4 + st_i) * 128
                        P.add('y', lambda e, xrow=xrow: e.dma_start(
                            x2_b[0][:, :], x_d[xrow:xrow + 128, :]),
                            wait=[(sV, haz_x2[0])], inc=dX2, inc_n=16)
                        x2_ld = dX2.n
                        for dc in range(4):
                            half = dc % 2
                            for n in range(16):
                                P.add('t', lambda e, n=n, st_i=st_i, dc=dc, half=half: e.matmul(
                                    PS[:, half * 512:(half + 1) * 512],
                                    encT[:, n * 512 + st_i * 128: n * 512 + (st_i + 1) * 128],
                                    wo_sb[:, n * 2048 + dc * 512: n * 2048 + (dc + 1) * 512],
                                    start=(n == 0), stop=(n == 15)),
                                    wait=([(sV, enc_last), (dGWT, wo_ld), (sV, ps_op_free[half]),
                                           (sS, ps_L_free[0]), (sS, ps_L_free[1])]
                                          if n == 0 else []),
                                    inc=(sT if n == 15 else None))
                            t_op = sT.n
                            pa = dc % 2
                            P.add('v', lambda e, pa=pa, dc=dc, half=half: e.tensor_tensor(
                                at_b[pa][:, :],
                                PS[:, half * 512:(half + 1) * 512],
                                x2_b[0][:, dc * 512:(dc + 1) * 512], op=ALU.add),
                                wait=[(sT, t_op), (dATS2[pa], haz_at[pa]), (dX2, x2_ld)], inc=sV)
                            ps_op_free[half] = sV.n
                            s_at = sV.n
                            arow = (b * NQB + qb) * 128
                            P.add('y', lambda e, arow=arow, st_i=st_i, dc=dc, pa=pa: e.dma_start(
                                attn_d[arow:arow + 128, st_i * 2048 + dc * 512: st_i * 2048 + (dc + 1) * 512],
                                at_b[pa][:, :]),
                                wait=[(sV, s_at)], inc=dATS2[pa], inc_n=16)
                            haz_at[pa] = dATS2[pa].n
                        haz_encT = t_op
                        haz_x2[0] = sV.n
                        quant_step(1)
                    haz_qT_read = sT.n
                haz_kv_read = sT.n
            quant_step(NQH + 6)

            barrier()

        with ExitStack() as st:
          if '3' in PHASES:
            xnT_l = [st.enter_context(nc.sbuf_tensor(f"xnTl{p}", [128, 8192], BF16)) for p in range(2)]
            wgu_l = [st.enter_context(nc.sbuf_tensor(f"wgul{p}", [128, 4096], BF16)) for p in range(3)]
            wp_l = [st.enter_context(nc.sbuf_tensor(f"wpl{p}", [128, 4096], BF16)) for p in range(2)]
            hT_sb = st.enter_context(nc.sbuf_tensor("hT_sb", [128, 32768], BF16))
            hsq_b = [st.enter_context(nc.sbuf_tensor(f"hsq{p}", [128, 512], BF16)) for p in range(2)]
            sg_b = [st.enter_context(nc.sbuf_tensor(f"sg{p}", [128, 512], F32)) for p in range(2)]
            at_l = [st.enter_context(nc.sbuf_tensor(f"atl{p}", [128, 512], F32)) for p in range(2)]
            out_sb = st.enter_context(nc.sbuf_tensor("out_sb", [128, 8192], F32))
            rsc_sb = st.enter_context(nc.sbuf_tensor("rsc_sb", [128, 8], F32))

            haz_xnT_rd = [0, 0]
            haz_wgu = [0, 0, 0]
            haz_wp = [0, 0]
            haz_hT = 0
            haz_gate_ps = [0, 0]
            haz_up_ps = [0, 0]
            haz_hsq = [0, 0]
            haz_sg = [0, 0]
            haz_proj_ps = [0, 0]
            haz_out_store = 0
            haz_at_rd = [0, 0]
            haz_rms_ps = 0

            for ci in range(BC * NQB):
                b, qb = ci // NQB, ci % NQB
                pc = ci % 2
                rows = (b * NQB + qb) * 128
                P.add('y', lambda e, rows=rows, pc=pc: e.dma_start(
                    xnT_l[pc][:, :], xnT_d[rows:rows + 128, :]),
                    wait=[(sT, haz_xnT_rd[pc]), (dXNT_S, dXNT_S.n)], inc=dXNT[pc], inc_n=16)
                xnT_ld = dXNT[pc].n
                hT_writes = 0
                v_hsq_by_j = {}
                for j in range(64):
                    pj = j % 2
                    pw = j % 3
                    P.add('y', lambda e, j=j, pw=pw: e.dma_start(
                        wgu_l[pw][:, :].rearrange("p (a n) -> p a n", n=2048),
                        wguT_d[j * 256:(j + 1) * 256, :].rearrange("(a p) n -> p a n", p=128)),
                        wait=[(sT, haz_wgu[pw])] + [(dd, dd.n) for dd in dQE],
                        inc=dWGU[pw], inc_n=16)
                    w_ld2 = dWGU[pw].n
                    for c in range(16):
                        P.add('t', lambda e, c=c, pj=pj, pw=pw, pc=pc: e.matmul(
                            PS[:, pj * 512:(pj + 1) * 512],
                            wgu_l[pw][:, c * 128:(c + 1) * 128],
                            xnT_l[pc][:, c * 512:(c + 1) * 512],
                            start=(c == 0), stop=(c == 15)),
                            wait=(([(dWGU[pw], w_ld2), (dXNT[pc], xnT_ld), (sS, haz_gate_ps[pj])] +
                                   ([(sV, haz_proj_ps[0])] if j == 0 else []))
                                  if c == 0 else []))
                    for c in range(16):
                        P.add('t', lambda e, c=c, pj=pj, pw=pw, pc=pc: e.matmul(
                            PS[:, 1024 + pj * 512: 1024 + (pj + 1) * 512],
                            wgu_l[pw][:, 2048 + c * 128: 2048 + (c + 1) * 128],
                            xnT_l[pc][:, c * 512:(c + 1) * 512],
                            start=(c == 0), stop=(c == 15)),
                            wait=([(sV, haz_up_ps[pj])] if c == 0 else []),
                            inc=(sT if c == 15 else None))
                    t_gu = sT.n
                    haz_wgu[pw] = t_gu
                    haz_xnT_rd[pc] = t_gu
                    P.add('s', lambda e, pj=pj: e.activation(
                        sg_b[pj][:, :], PS[:, pj * 512:(pj + 1) * 512], AF.Silu),
                        wait=[(sT, t_gu), (sV, haz_sg[pj])], inc=sS)
                    s_silu = sS.n
                    haz_gate_ps[pj] = s_silu
                    P.add('v', lambda e, j=j, pj=pj: e.tensor_tensor(
                        hT_sb[:, j * 512:(j + 1) * 512], sg_b[pj][:, :],
                        PS[:, 1024 + pj * 512: 1024 + (pj + 1) * 512], op=ALU.mult),
                        wait=[(sS, s_silu)] + ([(sT, haz_hT)] if j == 0 else []), inc=sV)
                    v_hT = sV.n
                    haz_up_ps[pj] = v_hT
                    haz_sg[pj] = v_hT
                    hT_writes = v_hT
                    P.add('v', lambda e, j=j, pj=pj: e.tensor_tensor(
                        hsq_b[pj][:, :], hT_sb[:, j * 512:(j + 1) * 512],
                        hT_sb[:, j * 512:(j + 1) * 512], op=ALU.mult),
                        wait=[(sV, v_hT), (sT, haz_hsq[pj])], inc=sV)
                    v_hsq_by_j[j] = sV.n
                    for jr in ([j - 1] if j >= 1 else []) + ([j] if j == 63 else []):
                        pjr = jr % 2
                        for st_i in range(4):
                            w0 = []
                            if st_i == 0:
                                w0 = [(sV, v_hsq_by_j[jr])]
                                if jr == 0:
                                    w0 += [(sS, haz_rms_ps), (sV, haz_proj_ps[1])]
                            P.add('t', lambda e, st_i=st_i, pjr=pjr, jr=jr: e.matmul(
                                PS[:, (4 + st_i) * 512: (4 + st_i) * 512 + 1],
                                hsq_b[pjr][:, st_i * 128:(st_i + 1) * 128], ones_col[:, :],
                                start=(jr == 0), stop=(jr == 63)),
                                wait=w0, inc=(sT if st_i == 3 else None))
                        haz_hsq[pjr] = sT.n
                t_rms = sT.n
                P.add('s', lambda e: e.activation(
                    rsc_sb[:, 4:8].rearrange("p (a o) -> p a o", o=1),
                    PS[:, 2048:4096].rearrange("p (a n) -> p a n", n=512)[:, :, 0:1],
                    AF.Ln, scale=1.0 / 8192.0, bias=cst[:, 4:5]),
                    wait=[(sT, t_rms)], inc=sS)
                s1 = sS.n
                P.add('s', lambda e: e.activation(
                    rsc_sb[:, 0:4], rsc_sb[:, 4:8], AF.Exp, scale=-0.5),
                    wait=[(sS, s1)], inc=sS)
                s_rsc = sS.n
                haz_rms_ps = s_rsc
                adds_by_st = [0, 0, 0, 0]
                for q in range(4):
                    ph = q % 2
                    base = ph * 2048
                    for jg in range(8):
                        pw = jg % 2
                        prow = (jg * 4 + q) * 128
                        P.add('y', lambda e, prow=prow, pw=pw: e.dma_start(
                            wp_l[pw][:, :], wpT_d[prow:prow + 128, :]),
                            wait=[(sT, haz_wp[pw]), (dASM, dASM.n)], inc=dWP[pw], inc_n=16)
                        wp_ld = dWP[pw].n
                        for jin in range(8):
                            j = jg * 8 + jin
                            for st_i in range(4):
                                w0 = []
                                if jin == 0 and st_i == 0 and jg == 0:
                                    w0 = [(dWP[pw], wp_ld), (sV, hT_writes), (sS, s_rsc),
                                          (sV, haz_proj_ps[ph])]
                                elif jin == 0 and st_i == 0:
                                    w0 = [(dWP[pw], wp_ld)]
                                P.add('t', lambda e, j=j, jin=jin, st_i=st_i, pw=pw, base=base: e.matmul(
                                    PS[:, base + st_i * 512: base + (st_i + 1) * 512],
                                    hT_sb[:, j * 512 + st_i * 128: j * 512 + (st_i + 1) * 128],
                                    wp_l[pw][:, jin * 512:(jin + 1) * 512],
                                    start=(j == 0), stop=(j == 63)),
                                    wait=w0,
                                    inc=(sT if (jin == 7 and st_i == 3) else None))
                        haz_wp[pw] = sT.n
                    t_q = sT.n
                    if q >= 2:
                        haz_hT = t_q
                    for st_i in range(4):
                        pa = (q * 4 + st_i) % 2
                        arow = (b * NQB + qb) * 128
                        P.add('y', lambda e, arow=arow, st_i=st_i, q=q, pa=pa: e.dma_start(
                            at_l[pa][:, :],
                            attn_d[arow:arow + 128, st_i * 2048 + q * 512: st_i * 2048 + (q + 1) * 512]),
                            wait=[(sV, haz_at_rd[pa]), (dAT_S, dAT_S.n)], inc=dATL[pa], inc_n=16)
                        at_ld = dATL[pa].n
                        P.add('v', lambda e, st_i=st_i, q=q, base=base, pa=pa: e.scalar_tensor_tensor(
                            out_sb[:, st_i * 2048 + q * 512: st_i * 2048 + (q + 1) * 512],
                            PS[:, base + st_i * 512: base + (st_i + 1) * 512],
                            rsc_sb[:, st_i:st_i + 1], at_l[pa][:, :],
                            op0=ALU.mult, op1=ALU.add),
                            wait=[(sT, t_q), (dATL[pa], at_ld), (sS, s_rsc), (dOUT, haz_out_store)],
                            inc=sV)
                        adds_by_st[st_i] = sV.n
                        haz_at_rd[pa] = sV.n
                        haz_proj_ps[ph] = sV.n
                for st_i in range(4):
                    yrow = b * T + (qb * 4 + st_i) * 128
                    P.add('y', lambda e, yrow=yrow, st_i=st_i: e.dma_start(
                        y_d[yrow:yrow + 128, :], out_sb[:, st_i * 2048:(st_i + 1) * 2048]),
                        wait=[(sV, adds_by_st[st_i])], inc=dOUT, inc_n=16)
                haz_out_store = dOUT.n

        P.raw('y', lambda e, v=dOUT.n, h=dOUT.h: e.wait_ge(h, v))

        with nc.Block() as block:
            @block.tensor
            def _(e):
                for f in P.ops['t']:
                    f(e)

            @block.vector
            def _(e):
                for f in P.ops['v']:
                    f(e)

            @block.scalar
            def _(e):
                for f in P.ops['s']:
                    f(e)

            @block.gpsimd
            def _(e):
                for f in P.ops['g']:
                    f(e)

            @block.sync
            def _(e):
                for f in P.ops['y']:
                    f(e)

    return nc


def _sincos_table():
    fraction = np.arange(0, 128, 2, dtype=np.float64) / 128.0
    inv_freq = 1.0 / (10000.0 ** fraction)
    t = np.arange(T, dtype=np.float64)[:, None]
    ang = t * inv_freq[None, :]
    return np.concatenate([np.sin(ang), np.cos(ang)], axis=1).astype(np.float32)


def kernel(L0_attn_x, L0_attn_q_kernel, L0_attn_k_kernel, L0_attn_v_kernel,
           L0_attn_out_kernel, L0_ffn_w_gate_up, L0_ffn_w_proj):
    x = np.ascontiguousarray(np.asarray(L0_attn_x, dtype=np.float32))
    wq = np.ascontiguousarray(np.asarray(L0_attn_q_kernel, dtype=np.float32))
    wk = np.ascontiguousarray(np.asarray(L0_attn_k_kernel, dtype=np.float32))
    wv = np.ascontiguousarray(np.asarray(L0_attn_v_kernel, dtype=np.float32))
    wo = np.ascontiguousarray(np.asarray(L0_attn_out_kernel, dtype=np.float32))
    wgu = np.ascontiguousarray(np.asarray(L0_ffn_w_gate_up, dtype=np.float32))
    wp = np.ascontiguousarray(np.asarray(L0_ffn_w_proj, dtype=np.float32))
    sincos = _sincos_table()
    eye = np.eye(128, dtype=np.float32).astype(bfloat16)

    nc = build_nc()
    in_maps = []
    for c in range(N_CORES):
        in_maps.append(dict(
            x=np.ascontiguousarray(x[c * BC:(c + 1) * BC].reshape(BC * T, D)),
            wq=wq, wk=wk, wv=wv, wo=wo, wgu=wgu, wp=wp,
            sincos=sincos, eye=eye))
    res = run_bass_kernel_spmd(nc, in_maps, core_ids=list(range(N_CORES)))
    out = np.concatenate([np.asarray(r["y"]).reshape(BC, T, D) for r in res.results], axis=0)
    return out


if __name__ == "__main__":
    import time
    t0 = time.time()
    nc = build_nc()
    n = sum(1 for _ in nc.all_instructions())
    print(f"build ok in {time.time()-t0:.1f}s, {n} instructions")
